# version 06
# baseline (speedup 1.0000x reference)
import math
import numpy as np
import ml_dtypes
from contextlib import ExitStack

import concourse.bass as bass
import concourse.mybir as mybir
from concourse.bass_utils import run_bass_kernel_spmd
from concourse.bass_core_rust import IndirectOffsetOnAxis as IOA

F32 = mybir.dt.float32
BF16 = mybir.dt.bfloat16
U32 = mybir.dt.uint32
U8 = mybir.dt.uint8
AF = mybir.ActivationFunctionType
ALU = mybir.AluOpType
AX = mybir.AxisListType

NCORES = 8
D = 1024
S = 2048
NHEAD = 16
DH = 64
E = 16
CAP = 2560
BLK = 640
F = 1408
INNER = 2048
L1H = 8
ST = 64
HD = 256
TC = 128


def _bf(x):
    return np.asarray(x).astype(ml_dtypes.bfloat16)


def alibi_slopes(h=NHEAD):
    x = (2.0 ** 8) ** (1.0 / h)
    return np.asarray([1.0 / x ** (i + 1) for i in range(h)], np.float64)



ENG = ("t", "v", "s", "g", "q")


class P:

    def __init__(self, nc):
        self.nc = nc
        self.ops = {e: [] for e in ENG}
        self.cnt = {e: 0 for e in ENG}
        self.lastw = {}
        self.lastr = {}
        self.waited = {e: {e2: 0 for e2 in ENG} for e in ENG}
        self.nop = 0

    def emit(self, eng, fn, r=(), w=(), dma=False, tag=""):
        amt = 16 if dma else 1
        waits = {}

        def dep(d):
            if d is None:
                return
            e2, c2 = d
            if self.waited[eng][e2] >= c2:
                return
            if e2 == eng and not dma:
                pass
            waits[e2] = max(waits.get(e2, 0), c2)

        for k in r:
            dep(self.lastw.get(k))
        for k in w:
            dep(self.lastw.get(k))
            for e2, c2 in self.lastr.get(k, {}).items():
                dep((e2, c2))
        for e2, c2 in waits.items():
            self.waited[eng][e2] = max(self.waited[eng][e2], c2)
        self.cnt[eng] += amt
        me = (eng, self.cnt[eng])
        for k in w:
            self.lastw[k] = me
            self.lastr.pop(k, None)
        for k in r:
            self.lastr.setdefault(k, {})[eng] = self.cnt[eng]
        self.ops[eng].append((fn, dict(waits), amt, tag))
        self.nop += 1

    def barrier(self, engs=ENG, on=ENG):
        snap = {e: self.cnt[e] for e in on}
        for eng in engs:
            waits = {}
            for e2, c2 in snap.items():
                if c2 > self.waited[eng][e2]:
                    waits[e2] = c2
                    self.waited[eng][e2] = c2
            if waits:
                self.ops[eng].append((None, waits, 0, "barrier"))

    def seal(self, es):
        nc = self.nc
        waited = {e: set() for e in ENG}
        for ename in ENG:
            for fn, waits, amt, tag in self.ops[ename]:
                for e2, c2 in waits.items():
                    waited[e2].add(c2)
        remap = {}
        marks = {}
        for e in ENG:
            m = {0: 0}
            mk = []
            c = new = 0
            for fn, waits, amt, tag in self.ops[e]:
                if fn is None:
                    mk.append(False)
                    continue
                c += amt
                inc = (amt != 1) or (c in waited[e])
                if inc:
                    new += amt
                mk.append(inc)
                m[c] = new
            remap[e] = m
            marks[e] = mk
        sems = {e: es.enter_context(nc.semaphore(f"sem_{e}")) for e in ENG}
        block = es.enter_context(nc.Block())
        secs = dict(t=block.tensor, v=block.vector, s=block.scalar,
                    g=block.gpsimd, q=block.sync)

        def runner(ename):
            def run(engine):
                for (fn, waits, amt, tag), inc in zip(self.ops[ename],
                                                      marks[ename]):
                    for e2, c2 in waits.items():
                        engine.wait_ge(sems[e2], remap[e2][c2])
                    if fn is None:
                        continue
                    inst = fn(engine)
                    if inc:
                        inst.then_inc(sems[ename], amt)
            return run

        for ename, sec in secs.items():
            sec(runner(ename))
        return sems



def host_constants():
    c = {}
    c["ones_col_f32"] = np.ones((128, 1), np.float32)
    c["ones_row_f32"] = np.ones((65, 128), np.float32)
    c["ones_row_bf"] = _bf(np.ones((1, 128)))
    c["ident_bf"] = _bf(np.eye(128))
    c["ident_f32"] = np.eye(128, dtype=np.float32)
    iot = np.zeros((65, 2048), np.float32)
    iot[0] = np.arange(2048); iot[64] = np.arange(2048)
    c["iota_row_bf"] = _bf(iot)
    c["iota_col_f32"] = np.arange(128, dtype=np.float32).reshape(128, 1)
    c["iota16_f32"] = np.tile(np.arange(16, dtype=np.float32), (128, 1))
    sidx = np.arange(128)
    c["lt_incl_f32"] = (sidx[:, None] <= sidx[None, :]).astype(np.float32)
    for off in (0, 128, 256, 384):
        i = np.arange(512)[None, :]
        p = np.arange(128)[:, None]
        c[f"cmask{off}"] = (i >= p + off).astype(np.uint8)
    slopes = alibi_slopes()
    ab = np.zeros((128, NHEAD * 16), np.float32)
    for h in range(NHEAD):
        for jt in range(16):
            ab[:, h * 16 + jt] = slopes[h] * (128 * jt + np.arange(128))
    c["alibi_cols"] = ab
    c["zero_bf_512"] = _bf(np.zeros((128, 512)))
    c["trimask_u8"] = (sidx[:, None] <= sidx[None, :]).astype(np.uint8)
    c["zero_f32_128"] = np.zeros((128, 128), np.float32)
    c["lo_mask4_u8"] = (sidx[:, None] < 64).astype(np.uint8) * np.ones(
        (1, 4), np.uint8)
    c["i64pat_f32"] = (np.arange(64)[None, :] == (sidx % 64)[:, None]
                       ).astype(np.float32)
    return c


class TView:

    def __init__(self, ap):
        self._ap = ap

    def ap(self):
        return self._ap

    def __getitem__(self, idx):
        return self._ap.__getitem__(idx)


class Ctx:

    def __init__(self, name):
        self.nc = bass.Bass("TRN2", target_bir_lowering=False, debug=False,
                            dynamic_dma_scratch_size=16384)
        self.p = P(self.nc)
        self.es = ExitStack()
        self.sb = {}
        self.ps = {}
        self.const_ap = {}

    def sbuf(self, name, shape, dtype):
        if name in self.sb:
            return self.sb[name]
        es = getattr(self, "cur_es", None) or self.es
        uname = name if not getattr(self, "scope_tag", "") else \
            f"{self.scope_tag}_{name}"
        t = es.enter_context(self.nc.sbuf_tensor(uname, list(shape), dtype))
        self.sb[name] = t
        return t

    def psum(self, name, shape, dtype=F32):
        if name in self.ps:
            return self.ps[name]
        t = self.es.enter_context(self.nc.psum_tensor(name, list(shape), dtype))
        self.ps[name] = t
        return t

    def view(self, name, ap):
        t = TView(ap)
        self.sb[name] = t
        return t

    def push_scope(self, tag):
        self.cur_es = ExitStack()
        self.scope_tag = tag

    def pop_scope(self):
        self.cur_es.close()
        self.cur_es = None
        self.scope_tag = ""

    def load_consts(self, consts, names):
        for nm in names:
            if nm in self.const_ap:
                continue
            arr = consts[nm]
            dt = {np.dtype(np.float32): F32, np.dtype(ml_dtypes.bfloat16): BF16,
                  np.dtype(np.uint8): U8, np.dtype(np.uint16): mybir.dt.uint16,
                  np.dtype(np.uint32): U32}[arr.dtype]
            dram = self.nc.inline_tensor(np.ascontiguousarray(arr), f"cd_{nm}").ap()
            tile = self.sbuf(f"c_{nm}", arr.shape, dt)
            self.p.emit("q", (lambda d=dram, t=tile: lambda e:
                              e.dma_start(out=t[:], in_=d))(),
                        w=[f"c_{nm}"], dma=True)
            self.const_ap[nm] = tile


def build_layernorm(cx, res_names, out_bf, out_f32=None, scratch="lnscr",
                    tag="ln"):
    p, sb = cx.p, cx.sb
    ones = cx.const_ap["ones_col_f32"]
    onesr = cx.const_ap["ones_row_f32"]
    s45, s67 = cx.ps["PS45"], cx.ps["PS67"]
    scr = sb[scratch]
    stM, stV = sb["stM"], sb["stV"]
    for half in range(2):
        hs = slice(half * 1024, half * 1024 + 1024)
        for i in range(8):
            for q in range(2):
                qs = slice(half * 1024 + q * 512, half * 1024 + q * 512 + 512)
                p.emit("t", (lambda i=i, q=q, qs=qs: lambda e:
                             e.matmul(s45[0:1, q * 512:q * 512 + 512],
                                      ones[:], sb[res_names[i]][:, qs],
                                      start=(i == 0), stop=(i == 7)))(),
                       r=[res_names[i], "c_ones_col_f32"], w=["PS45"], tag=tag + "su")
        for i in range(8):
            p.emit("s", (lambda i=i, hs=hs: lambda e:
                         e.activation(scr[:, 0:1024], sb[res_names[i]][:, hs],
                                      AF.Square))(),
                   r=[res_names[i]], w=[scratch], tag=tag + "sq")
            for q in range(2):
                p.emit("t", (lambda i=i, q=q: lambda e:
                             e.matmul(s67[0:1, q * 512:q * 512 + 512],
                                      ones[:], scr[:, q * 512:q * 512 + 512],
                                      start=(i == 0), stop=(i == 7)))(),
                       r=[scratch, "c_ones_col_f32"], w=["PS67"], tag=tag + "s2")
        p.emit("v", (lambda hs=hs: lambda e:
                     e.tensor_scalar_mul(stM[0:1, hs], s45[0:1, 0:1024],
                                         -1.0 / 1024))(),
               r=["PS45"], w=["lnstats"], tag=tag + "mean")
        p.emit("v", (lambda hs=hs: lambda e:
                     e.tensor_scalar_mul(stV[0:1, hs], s67[0:1, 0:1024],
                                         1.0 / 1024))(),
               r=["PS67"], w=["lnstats"], tag=tag + "msq")
    p.emit("v", lambda e: e.tensor_mul(scr[0:1, 0:2048], stM[:], stM[:]),
           r=["lnstats"], w=["lnscr"], tag=tag + "m2")
    p.emit("v", lambda e: e.tensor_sub(stV[:], stV[:], scr[0:1, 0:2048]),
           r=["lnstats", "lnscr"], w=["lnstats"], tag=tag + "var")
    p.emit("v", lambda e: e.tensor_scalar_add(stV[:], stV[:], 1e-5),
           r=["lnstats"], w=["lnstats"], tag=tag + "eps")
    p.emit("s", lambda e: e.activation(stV[:], stV[:], AF.Ln),
           r=["lnstats"], w=["lnstats"], tag=tag + "ln")
    p.emit("s", lambda e: e.activation(stV[:], stV[:], AF.Exp, scale=-0.5),
           r=["lnstats"], w=["lnstats"], tag=tag + "exp")
    p.emit("v", lambda e: e.tensor_mul(stM[:], stM[:], stV[:]),
           r=["lnstats"], w=["lnstats"], tag=tag + "nmr")
    rb, nb = sb["ln_rb"], sb["ln_nb"]
    for half in range(2):
        for q in range(2):
            qs = slice(half * 1024 + q * 512, half * 1024 + q * 512 + 512)
            p.emit("t", (lambda qs=qs, q=q: lambda e:
                         e.matmul(s45[:, q * 512:q * 512 + 512], onesr[0:1, :],
                                  stV[0:1, qs], start=True, stop=True))(),
                   r=["lnstats", "c_ones_row_f32"], w=["PS45"], tag=tag + "bc")
            p.emit("t", (lambda qs=qs, q=q: lambda e:
                         e.matmul(s67[:, q * 512:q * 512 + 512], onesr[0:1, :],
                                  stM[0:1, qs], start=True, stop=True))(),
                   r=["lnstats", "c_ones_row_f32"], w=["PS67"], tag=tag + "bc")
        p.emit("v", (lambda half=half: lambda e:
                     e.tensor_copy(rb[:, half * 1024:half * 1024 + 1024],
                                   s45[:, 0:1024]))(),
               r=["PS45"], w=["ln_rb"], tag=tag + "cb")
        p.emit("v", (lambda half=half: lambda e:
                     e.tensor_copy(nb[:, half * 1024:half * 1024 + 1024],
                                   s67[:, 0:1024]))(),
               r=["PS67"], w=["ln_nb"], tag=tag + "cb")
    for i in range(8):
        p.emit("v", (lambda i=i: lambda e:
                     e.tensor_mul(scr[:], sb[res_names[i]][:], rb[:]))(),
               r=[res_names[i], "ln_rb"], w=[scratch], tag=tag + "nm")
        p.emit("v", (lambda i=i: lambda e:
                     e.tensor_add(sb[out_bf[i]][:], scr[:], nb[:]))(),
               r=[scratch, "ln_nb"], w=[out_bf[i]], tag=tag + "na")
        if out_f32 is not None:
            p.emit("v", (lambda i=i: lambda e:
                         e.tensor_add(sb[out_f32[i]][:], scr[:], nb[:]))(),
                   r=[scratch, "ln_nb"], w=[out_f32[i]], tag=tag + "nf")




def build_launch_a(consts, stage=99):
    cx = Ctx("A")
    nc, p = cx.nc, cx.p
    x_in = nc.dram_tensor("x", [S, D], F32, kind="ExternalInput").ap()
    watt = nc.dram_tensor("watt", [8, 128, 3072], F32, kind="ExternalInput").ap()
    rw_in = nc.dram_tensor("rw", [D, E], F32, kind="ExternalInput").ap()
    rb_in = nc.dram_tensor("rb", [1, E], F32, kind="ExternalInput").ap()
    xout = nc.dram_tensor("xout", [D, S], F32, kind="ExternalOutput").ap()
    lgout = nc.dram_tensor("lgout", [S, E], F32, kind="ExternalOutput").ap()
    NBLKROW = E * BLK + 2 * S
    blkout = nc.dram_tensor("blkout", [NBLKROW, D], BF16, kind="ExternalOutput").ap()

    sb = cx.sb
    for i in range(8):
        cx.sbuf(f"res{i}", (128, S), F32)
    for i in range(8):
        cx.sbuf(f"lnb{i}", (128, S), BF16)
    for j in range(2):
        cx.sbuf(f"qop{j}", (128, S), BF16)
        cx.sbuf(f"kop{j}", (128, S), BF16)
        cx.sbuf(f"usb{j}", (128, 1024), BF16)
        cx.sbuf(f"usbm{j}", (128, 512), BF16)
        cx.sbuf(f"vbuf{j}", (128, 1040), BF16)
    cx.sbuf("wbuf0", (128, 3072), BF16)
    cx.sbuf("wbuf1", (128, 3072), BF16)
    cx.sbuf("lnscr", (128, S), F32)
    cx.sbuf("ln_rb", (128, S), F32)
    cx.sbuf("ln_nb", (128, S), F32)
    cx.sbuf("stM", (1, S), F32)
    cx.sbuf("stV", (1, S), F32)
    cx.view("lnftmp", cx.sb["lnscr"].ap())
    cx.sbuf("lntm", (128, 1024), BF16)
    cx.sbuf("rwsb", (128, 8 * E), F32)
    cx.sbuf("rbsb", (1, E), F32)
    cx.sbuf("crA", (1, 1024), F32)
    cx.sbuf("crB", (1, 1024), F32)
    cx.sbuf("rcpb", (128, 1024), F32)
    cx.sbuf("atmp", (128, 1024), F32)
    cx.sbuf("lgsb", (128, 16 * E), F32)
    cx.sbuf("RW", (128, 10 * E), F32)
    cx.sbuf("RC", (128, 16), F32)
    cx.sbuf("RU", (128, 4), U8)
    cx.sbuf("IDX", (128, 4), U32)
    cx.sbuf("CR", (1, E), F32)
    cx.sbuf("augscal", (65, 1), F32)

    ps01 = cx.psum("PS01", (128, 1024))
    ps23 = cx.psum("PS23", (128, 1024))
    ps45 = cx.psum("PS45", (128, 1024))
    ps67 = cx.psum("PS67", (128, 1024))

    cx.load_consts(consts, ["ones_col_f32", "ones_row_f32", "ident_f32",
                            "ident_bf", "iota_row_bf", "iota_col_f32",
                            "iota16_f32", "lt_incl_f32", "cmask0", "cmask128",
                            "cmask256", "cmask384", "alibi_cols", "zero_bf_512"])

    p.emit("q", lambda e: e.dma_start(
        out=sb["rwsb"].ap().rearrange("p (k e) -> p k e", e=E),
        in_=rw_in.rearrange("(k p) e -> p k e", p=128)),
        w=["rwsb"], dma=True)
    p.emit("q", lambda e: e.dma_start(out=sb["rbsb"][:], in_=rb_in),
           w=["rbsb"], dma=True)

    stg = sb["lnftmp"]
    for tt in range(16):
        p.emit("q", (lambda tt=tt: lambda e: e.dma_start(
            out=stg[:, 0:1024], in_=x_in[tt * 128:(tt + 1) * 128, :]))(),
            r=["lnftmp"], w=["lnftmp"], dma=True, tag="ldx")
        psx = ps01 if tt % 2 == 0 else ps23
        pkey = "PS01" if tt % 2 == 0 else "PS23"
        for i in range(8):
            p.emit("t", (lambda i=i, psx=psx: lambda e: e.transpose(
                psx[:, i * 128:(i + 1) * 128],
                stg[:, i * 128:(i + 1) * 128],
                cx.const_ap["ident_f32"][:]))(),
                r=["lnftmp", "c_ident_f32"], w=[pkey], tag="trx")
        for i in range(8):
            p.emit("v", (lambda i=i, tt=tt, psx=psx: lambda e: e.tensor_copy(
                sb[f"res{i}"][:, tt * 128:(tt + 1) * 128],
                psx[:, i * 128:(i + 1) * 128]))(),
                r=[pkey], w=[f"res{i}", "lnftmp"], tag="evx")

    res_names = [f"res{i}" for i in range(8)]
    lnb_names = [f"lnb{i}" for i in range(8)]

    if stage >= 2:
        build_layernorm(cx, res_names, lnb_names, tag="ln0")
    if stage == 2:
        for i in range(8):
            p.emit("v", (lambda i=i: lambda e: e.tensor_copy(
                sb["lnscr"][:], sb[f"lnb{i}"][:]))(),
                r=[f"lnb{i}"], w=["lnscr"], tag="dbg")
            p.emit("q", (lambda i=i: lambda e: e.dma_start(
                out=xout[i * 128:(i + 1) * 128, :], in_=sb["lnscr"][:]))(),
                r=["lnscr"], w=["xout_d"], dma=True, tag="dbg")
    if stage >= 3:

        slopes = alibi_slopes()
        for pair in range(8):
            wb = sb[f"wbuf{pair % 2}"]
            wkey = f"wbuf{pair % 2}"
            p.emit("g", (lambda pair=pair, wb=wb: lambda e: e.dma_start(
                out=wb[:], in_=watt[pair]))(),
                r=[wkey], w=[wkey], dma=True, tag="ldw")
            for qk in range(2):
                dst = "qop" if qk == 0 else "kop"
                for hh in range(2):
                    for kt in range(8):
                        for ch in range(4):
                            psx = ps01 if ch < 2 else ps23
                            c0 = qk * 1024 + kt * 128 + hh * 64
                            p.emit("t", (lambda kt=kt, ch=ch, psx=psx, wb=wb, c0=c0:
                                         lambda e: e.matmul(
                                             psx[0:64, (ch % 2) * 512:(ch % 2) * 512 + 512],
                                             wb[:, c0:c0 + 64],
                                             sb[f"lnb{kt}"][:, ch * 512:ch * 512 + 512],
                                             start=(kt == 0), stop=(kt == 7)))(),
                                   r=[wkey, f"lnb{kt}"],
                                   w=["PS01" if ch < 2 else "PS23"], tag="qkproj")
                    for ch2 in range(2):
                        psx = ps01 if ch2 == 0 else ps23
                        p.emit("s", (lambda hh=hh, ch2=ch2, psx=psx, dst=dst:
                                     lambda e: e.activation(
                                         sb[f"{dst}{hh}"][0:64, ch2 * 1024:ch2 * 1024 + 1024],
                                         psx[0:64, 0:1024], AF.Copy))(),
                               r=["PS01" if ch2 == 0 else "PS23"],
                               w=[f"{dst}{hh}"], tag="qkevac")
            for hh in range(2):
                p.emit("v", (lambda hh=hh, pair=pair: lambda e: e.memset(
                    sb["augscal"][64:65, 0:1],
                    -32.0 * float(slopes[2 * pair + hh])))(),
                    w=["augscal"], tag="qaugc")
                p.emit("v", (lambda hh=hh: lambda e: e.tensor_scalar_mul(
                    sb[f"qop{hh}"][64:65, :], cx.const_ap["iota_row_bf"][64:65, :],
                    sb["augscal"][64:65, 0:1]))(),
                    r=["c_iota_row_bf", "augscal"], w=[f"qop{hh}"], tag="qaug")
                p.emit("v", (lambda hh=hh: lambda e: e.memset(
                    sb[f"kop{hh}"][64:65, :], 1.0))(),
                    w=[f"kop{hh}"], tag="kaug")
            for hh in range(2):
                p.emit("v", (lambda hh=hh: lambda e: e.memset(sb[f"vbuf{hh}"][:], 1.0))(),
                       w=[f"vbuf{hh}"], tag="vones")
            vfm = sb["lnftmp"].ap().bitcast(BF16)
            for ch in range(4):
                reg = (ch % 2) * 512
                for kt in range(8):
                    p.emit("t", (lambda kt=kt, ch=ch, reg=reg, wb=wb: lambda e:
                                 e.matmul(
                                     ps45[:, reg:reg + 512],
                                     wb[:, 2048 + kt * 128:2048 + kt * 128 + 128],
                                     sb[f"lnb{kt}"][:, ch * 512:ch * 512 + 512],
                                     start=(kt == 0), stop=(kt == 7)))(),
                           r=[wkey, f"lnb{kt}"], w=["PS45"], tag="vproj")
                p.emit("s", (lambda ch=ch, reg=reg, vfm=vfm: lambda e:
                             e.activation(
                                 vfm[:, ch * 512:ch * 512 + 512],
                                 ps45[:, reg:reg + 512], AF.Copy))(),
                       r=["PS45"], w=["lnftmp"], tag="vfev")
            psb23v = ps23.ap().bitcast(BF16)
            for tt in range(16):
                p.emit("t", (lambda tt=tt, vfm=vfm, psb23v=psb23v: lambda e:
                             e.transpose(
                                 psb23v[:, (tt % 4) * 128:(tt % 4) * 128 + 128],
                                 vfm[:, tt * 128:tt * 128 + 128],
                                 cx.const_ap["ident_bf"][:]))(),
                       r=["lnftmp", "c_ident_bf"], w=["PS23"], tag="vtr")
                for hh in range(2):
                    p.emit("v", (lambda tt=tt, hh=hh, psb23v=psb23v: lambda e:
                                 e.tensor_copy(
                        sb[f"vbuf{hh}"][:, tt * 65:tt * 65 + 64],
                        psb23v[:, (tt % 4) * 128 + hh * 64:
                               (tt % 4) * 128 + hh * 64 + 64]))(),
                        r=["PS23"], w=[f"vbuf{hh}"], tag="vevac")
            if stage == 31:
                for nm, r0 in (("qop0", 0), ("kop0", 128), ("vbuf0", 256)):
                    wdt = 1040 if nm == "vbuf0" else 2048
                    p.emit("v", (lambda nm=nm, wdt=wdt: lambda e: e.tensor_copy(
                        sb["lnscr"][:, 0:wdt], sb[nm][:, 0:wdt]))(),
                        r=[nm], w=["lnscr"], tag="dbg31")
                    p.emit("q", (lambda nm=nm, r0=r0, wdt=wdt: lambda e: e.dma_start(
                        out=xout[r0:r0 + 128, 0:wdt],
                        in_=sb["lnscr"][:, 0:wdt]))(),
                        r=["lnscr"], w=["xout_d"], dma=True, tag="dbg31")
                continue
            for hh in range(2):
                h = 2 * pair + hh
                qo, ko, vb = sb[f"qop{hh}"], sb[f"kop{hh}"], sb[f"vbuf{hh}"]
                qkk, kkk, vkk = f"qop{hh}", f"kop{hh}", f"vbuf{hh}"
                for qh in range(2):
                    for jt in range(8 * qh + 8):
                        jtc0 = jt * 128
                        ub = jt % 2
                        psx = ps01 if ub == 0 else ps23
                        pk = "PS01" if ub == 0 else "PS23"
                        chunks = [c for c in range(2)
                                  if 1024 * qh + 512 * c + 512 > jtc0]
                        for c in chunks:
                            p.emit("t", (lambda c=c, jtc0=jtc0, qh=qh, psx=psx,
                                         qo=qo, ko=ko: lambda e: e.matmul(
                                             psx[:, c * 512:c * 512 + 512],
                                             ko[0:65, jtc0:jtc0 + 128],
                                             qo[0:65, 1024 * qh + 512 * c:
                                                1024 * qh + 512 * c + 512],
                                             start=True, stop=True))(),
                                   r=[qkk, kkk], w=[pk], tag="smm")
                        lo = min(chunks) * 512
                        hi = max(chunks) * 512 + 512
                        p.emit("s", (lambda ub=ub, psx=psx, h=h, jt=jt, lo=lo,
                                     hi=hi: lambda e: e.activation(
                                         sb[f"usb{ub}"][:, lo:hi],
                                         psx[:, lo:hi],
                                         AF.Exp,
                                         bias=cx.const_ap["alibi_cols"]
                                         [:, h * 16 + jt:h * 16 + jt + 1],
                                         scale=1.0 / 32.0))(),
                               r=[pk, "c_alibi_cols"], w=[f"usb{ub}"], tag="exp")
                        doff = jtc0 - 1024 * qh
                        dc = doff // 512 if 0 <= doff < 1024 else -1
                        if dc >= 0:
                            off = doff % 512
                            p.emit("v", (lambda ub=ub, dc=dc, off=off: lambda e:
                                         e.select(
                                             sb[f"usbm{ub}"][:],
                                             cx.const_ap[f"cmask{off}"][:],
                                             sb[f"usb{ub}"][:, dc * 512:dc * 512 + 512],
                                             cx.const_ap["zero_bf_512"][:]))(),
                                   r=[f"usb{ub}", f"c_cmask{off}", "c_zero_bf_512"],
                                   w=[f"usbm{ub}"], tag="csel")
                        for c in chunks:
                            jt_max = min(8 * qh + 7, (1024 * qh + 512 * c + 511) // 128)
                            src_m = (c == dc)
                            p.emit("t", (lambda c=c, jt=jt, ub=ub, vb=vb,
                                         jt_max=jt_max, src_m=src_m: lambda e: e.matmul(
                                             ps45[0:65, c * 512:c * 512 + 512],
                                             vb[:, 65 * jt:65 * jt + 65],
                                             sb[f"usbm{ub}"][:] if src_m else
                                             sb[f"usb{ub}"][:, c * 512:c * 512 + 512],
                                             start=(jt == 0), stop=(jt == jt_max)))(),
                                   r=[(f"usbm{ub}" if src_m else f"usb{ub}"), vkk],
                                   w=["PS45"], tag="avmm")
                    p.emit("v", lambda e: e.tensor_copy(sb["crA"][:],
                                                        ps45[64:65, 0:1024]),
                           r=["PS45"], w=["crow"], tag="csum")
                    p.emit("s", lambda e: e.activation(sb["crB"][:],
                                                       sb["crA"][:], AF.Ln),
                           r=["crow"], w=["crow"], tag="rln")
                    p.emit("s", lambda e: e.activation(sb["crB"][:],
                                                       sb["crB"][:], AF.Exp,
                                                       scale=-1.0),
                           r=["crow"], w=["crow"], tag="rexp")
                    for c in range(2):
                        p.emit("t", (lambda c=c: lambda e: e.matmul(
                            ps67[0:64, c * 512:c * 512 + 512],
                            cx.const_ap["ones_row_f32"][0:1, 0:64],
                            sb["crB"][0:1, c * 512:c * 512 + 512],
                            start=True, stop=True))(),
                            r=["crow", "c_ones_row_f32"], w=["PS67"], tag="rbc")
                    slab = h // 2
                    r0 = (h % 2) * 64
                    p.emit("v", lambda e: e.tensor_copy(
                        sb["rcpb"][0:64, :], ps67[0:64, :]),
                        r=["PS67"], w=["rcpb"], tag="rcp")
                    p.emit("v", lambda e: e.tensor_mul(
                        sb["atmp"][0:64, :], ps45[0:64, :], sb["rcpb"][0:64, :]),
                        r=["PS45", "rcpb"], w=["atmp"], tag="oscale")
                    p.emit("g", (lambda slab=slab, r0=r0, qh=qh: lambda e:
                                 e.dma_start(
                                     out=sb[f"res{slab}"][r0:r0 + 64,
                                                          qh * 1024:qh * 1024 + 1024],
                                     in_=sb["atmp"][0:64, :],
                                     accum_op=ALU.add))(),
                           r=["atmp", f"res{slab}"], w=[f"res{slab}"], dma=True,
                           tag="resadd")


    if stage >= 4 and not (30 <= stage < 99):
        def router_hook(i):
            for tt in range(16):
                p.emit("t", (lambda i=i, tt=tt: lambda e: e.matmul(
                    ps01[:, tt * 16:tt * 16 + 16],
                    sb["lnftmp"][:, tt * 128:tt * 128 + 128],
                    sb["rwsb"][:, i * 16:i * 16 + 16],
                    start=True, stop=True))(),
                    r=["lnftmp", "rwsb"], w=["PS01"], tag="router")
            if i == 0:
                p.emit("v", lambda e: e.tensor_copy(sb["lgsb"][:], ps01[:, 0:256]),
                       r=["PS01"], w=["lgsb"], tag="lgcp")
            else:
                p.emit("v", (lambda: lambda e: e.tensor_add(
                    sb["lgsb"][:], sb["lgsb"][:], ps01[:, 0:256]))(),
                    r=["PS01", "lgsb"], w=["lgsb"], tag="lgadd")

        build_layernorm_hooked(cx, res_names, lnb_names, router_hook, tag="ln1")
        p.emit("t", lambda e: e.matmul(
            ps01[:, 0:16], cx.const_ap["ones_row_f32"][0:1, :],
            sb["rbsb"][:], start=True, stop=True),
            r=["rbsb", "c_ones_row_f32"], w=["PS01"], tag="rbias")
        for tt in range(16):
            p.emit("v", (lambda tt=tt: lambda e: e.tensor_add(
                sb["lgsb"][:, tt * 16:tt * 16 + 16],
                sb["lgsb"][:, tt * 16:tt * 16 + 16], ps01[:, 0:16]))(),
                r=["PS01", "lgsb"], w=["lgsb"], tag="rbadd")
        p.emit("q", lambda e: e.dma_start(
            out=lgout.rearrange("(tt p) e -> p tt e", p=128),
            in_=sb["lgsb"].ap().rearrange("p (tt e) -> p tt e", e=E)),
            r=["lgsb"], w=["lgout_d"], dma=True, tag="lgst")

        p.emit("v", lambda e: e.memset(sb["CR"][:], 0.0), w=["CR"], tag="cr0")
        RW, RC, RU, IDX = sb["RW"], sb["RC"], sb["RU"], sb["IDX"]
        zt = RW.ap()[:, 0:16]
        eq = RW.ap()[:, 16:32]
        ca = RW.ap()[:, 32:48]
        cb = RW.ap()[:, 48:64]
        is1 = RW.ap()[:, 64:80]
        zm = RW.ap()[:, 80:96]
        is2 = RW.ap()[:, 96:112]
        sel = RW.ap()[:, 112:128]
        pos = RW.ap()[:, 128:144]
        tmp16 = RW.ap()[:, 144:160]

        def first_max_mask(src_ap, out_ap, mcol, tagp):
            p.emit("v", (lambda: lambda e: e.reduce_max(
                RC[:, mcol:mcol + 1], src_ap, axis=AX.X))(),
                r=["RW"], w=["RC"], tag=tagp + "max")
            p.emit("v", (lambda: lambda e: e.tensor_scalar(
                eq, src_ap, RC[:, mcol:mcol + 1], None, ALU.is_equal))(),
                r=["RW", "RC"], w=["RW"], tag=tagp + "eq")
            p.emit("v", lambda e: e.tensor_copy(ca, eq), r=["RW"], w=["RW"])
            p.emit("v", lambda e: e.memset(cb, 0.0), w=["RW"])
            sh = 1
            cur, nxt = ca, cb
            for stp in range(4):
                p.emit("v", (lambda cur=cur, nxt=nxt, sh=sh: lambda e:
                             e.tensor_copy(nxt[:, 0:sh], cur[:, 0:sh]))(),
                       r=["RW"], w=["RW"])
                p.emit("v", (lambda cur=cur, nxt=nxt, sh=sh: lambda e:
                             e.tensor_add(nxt[:, sh:16], cur[:, sh:16],
                                          cur[:, 0:16 - sh]))(),
                       r=["RW"], w=["RW"])
                cur, nxt = nxt, cur
                sh *= 2
            p.emit("v", (lambda cur=cur: lambda e: e.tensor_single_scalar(
                tmp16, cur, 1.0, ALU.is_equal))(), r=["RW"], w=["RW"])
            p.emit("v", (lambda: lambda e: e.tensor_mul(out_ap, eq, tmp16))(),
                   r=["RW"], w=["RW"], tag=tagp + "is")

        for tt in range(16):
            lg_t = sb["lgsb"].ap()[:, tt * 16:tt * 16 + 16]
            p.emit("v", (lambda lg_t=lg_t: lambda e: e.reduce_max(
                RC[:, 0:1], lg_t, axis=AX.X))(), r=["lgsb"], w=["RC"], tag="rmax")
            p.emit("v", (lambda: lambda e: e.tensor_scalar_mul(
                RC[:, 1:2], RC[:, 0:1], -1.0))(), r=["RC"], w=["RC"])
            p.emit("s", (lambda lg_t=lg_t: lambda e: e.activation(
                zt, lg_t, AF.Exp, bias=RC[:, 1:2]))(),
                r=["lgsb", "RC"], w=["RW"], tag="zexp")
            p.emit("v", (lambda: lambda e: e.reduce_sum(
                RC[:, 2:3], zt, axis=AX.X))(), r=["RW"], w=["RC"], tag="zsum")
            first_max_mask(zt, is1, 3, "t1")
            p.emit("v", (lambda: lambda e: e.scalar_tensor_tensor(
                zm, is1, -1e30, zt, ALU.mult, ALU.add))(), r=["RW"], w=["RW"])
            first_max_mask(zm, is2, 4, "t2")
            p.emit("v", lambda e: e.tensor_add(sel, is1, is2), r=["RW"], w=["RW"])
            p.emit("v", (lambda: lambda e: e.tensor_add(
                RC[:, 5:6], RC[:, 3:4], RC[:, 4:5]))(), r=["RC"], w=["RC"])
            p.emit("v", (lambda: lambda e: e.scalar_tensor_tensor(
                RC[:, 5:6], RC[:, 2:3], 1e-9, RC[:, 5:6], ALU.mult, ALU.add))(),
                r=["RC"], w=["RC"])
            p.emit("t", (lambda: lambda e: e.matmul(
                ps45[:, 0:16], cx.const_ap["lt_incl_f32"][:], sel,
                start=True, stop=False))(),
                r=["RW", "c_lt_incl_f32"], w=["PS45"], tag="posmm")
            p.emit("t", (lambda: lambda e: e.matmul(
                ps45[:, 0:16], cx.const_ap["ones_row_f32"][0:1, :], sb["CR"][:],
                start=False, stop=True))(),
                r=["CR", "c_ones_row_f32"], w=["PS45"], tag="poscr")
            p.emit("v", (lambda: lambda e: e.tensor_copy(pos, ps45[:, 0:16]))(),
                   r=["PS45"], w=["RW"], tag="posev")
            p.emit("t", (lambda: lambda e: e.matmul(
                ps67[0:1, 0:16], cx.const_ap["ones_col_f32"][:], sel,
                start=True, stop=True))(),
                r=["RW", "c_ones_col_f32"], w=["PS67"], tag="cntmm")
            p.emit("v", (lambda: lambda e: e.tensor_add(
                sb["CR"][:], sb["CR"][:], ps67[0:1, 0:16]))(),
                r=["PS67", "CR"], w=["CR"], tag="cradd")
            psb = ps23.ap().bitcast(BF16)
            for i in range(8):
                p.emit("t", (lambda i=i, tt=tt, psb=psb: lambda e: e.transpose(
                    psb[:, i * 128:i * 128 + 128],
                    sb[f"lnb{i}"][:, tt * 128:tt * 128 + 128],
                    cx.const_ap["ident_bf"][:]))(),
                    r=[f"lnb{i}"], w=["PS23"], tag="trln")
            p.emit("v", (lambda psb=psb: lambda e: e.tensor_copy(
                sb["lntm"][:], psb[:, 0:1024]))(),
                r=["PS23"], w=["lntm"], tag="trev")
            for k in range(2):
                ismask = is1 if k == 0 else is2
                p.emit("v", (lambda ismask=ismask: lambda e: e.tensor_mul(
                    tmp16, ismask, cx.const_ap["iota16_f32"][:]))(),
                    r=["RW", "c_iota16_f32"], w=["RW"])
                p.emit("v", (lambda k=k: lambda e: e.reduce_sum(
                    RC[:, 6 + k:7 + k], tmp16, axis=AX.X))(), r=["RW"], w=["RC"])
                p.emit("v", (lambda ismask=ismask: lambda e: e.tensor_mul(
                    tmp16, ismask, pos))(), r=["RW"], w=["RW"])
                p.emit("v", (lambda k=k: lambda e: e.reduce_sum(
                    RC[:, 8 + k:9 + k], tmp16, axis=AX.X))(), r=["RW"], w=["RC"])
                p.emit("v", (lambda k=k: lambda e: e.tensor_scalar_add(
                    RC[:, 8 + k:9 + k], RC[:, 8 + k:9 + k], -1.0))(),
                    r=["RC"], w=["RC"])
                p.emit("v", (lambda k=k: lambda e: e.scalar_tensor_tensor(
                    RC[:, 10 + k:11 + k], RC[:, 6 + k:7 + k], float(BLK),
                    RC[:, 8 + k:9 + k], ALU.mult, ALU.add))(), r=["RC"], w=["RC"])
                p.emit("v", (lambda k=k: lambda e: e.tensor_single_scalar(
                    RC[:, 12 + k:13 + k], RC[:, 8 + k:9 + k], float(BLK),
                    ALU.is_lt))(), r=["RC"], w=["RC"])
                p.emit("v", (lambda k=k: lambda e: e.tensor_copy(
                    RU[:, k:k + 1], RC[:, 12 + k:13 + k]))(), r=["RC"], w=["RU"])
                p.emit("v", (lambda k=k, tt=tt: lambda e: e.tensor_scalar(
                    RC[:, 14:15], cx.const_ap["iota_col_f32"][:], 2.0,
                    float(E * BLK + 256 * tt + k), ALU.mult, ALU.add))(),
                    r=["c_iota_col_f32"], w=["RC"])
                p.emit("v", (lambda k=k: lambda e: e.select(
                    RC[:, 15:16], RU[:, k:k + 1], RC[:, 10 + k:11 + k],
                    RC[:, 14:15]))(), r=["RC", "RU"], w=["RC"])
                p.emit("v", (lambda k=k: lambda e: e.tensor_copy(
                    IDX[:, k:k + 1], RC[:, 15:16]))(), r=["RC"], w=["IDX"])
                p.emit("g", (lambda k=k: lambda e: e.indirect_dma_start(
                    out=blkout, out_offset=IOA(IDX[:, k:k + 1], 0),
                    in_=sb["lntm"][:], in_offset=None))(),
                    r=["IDX", "lntm"], w=["blk_d"], dma=True, tag="scat")
            p.barrier(engs=("v",), on=("g",))


    for i in range(8 if stage not in (2, 30, 31, 32) else 0):
        p.emit("q", (lambda i=i: lambda e: e.dma_start(
            out=xout[i * 128:(i + 1) * 128, :], in_=sb[f"res{i}"][:]))(),
            r=[f"res{i}"], w=["xout_d"], dma=True, tag="xst")
    p.barrier(engs=("q",), on=("q", "g"))
    p.barrier(engs=("g",), on=("g",))
    return cx


def build_layernorm_hooked(cx, res_names, out_bf, hook, tag="ln"):
    p, sb = cx.p, cx.sb
    ones = cx.const_ap["ones_col_f32"]
    onesr = cx.const_ap["ones_row_f32"]
    s45, s67 = cx.ps["PS45"], cx.ps["PS67"]
    scr = sb["lnscr"]
    stM, stV = sb["stM"], sb["stV"]
    for half in range(2):
        hs = slice(half * 1024, half * 1024 + 1024)
        for i in range(8):
            for q in range(2):
                qs = slice(half * 1024 + q * 512, half * 1024 + q * 512 + 512)
                p.emit("t", (lambda i=i, q=q, qs=qs: lambda e:
                             e.matmul(s45[0:1, q * 512:q * 512 + 512],
                                      ones[:], sb[res_names[i]][:, qs],
                                      start=(i == 0), stop=(i == 7)))(),
                       r=[res_names[i], "c_ones_col_f32"], w=["PS45"], tag=tag + "su")
        for i in range(8):
            p.emit("s", (lambda i=i, hs=hs: lambda e:
                         e.activation(scr[:, 0:1024], sb[res_names[i]][:, hs],
                                      AF.Square))(),
                   r=[res_names[i]], w=["lnscr"], tag=tag + "sq")
            for q in range(2):
                p.emit("t", (lambda i=i, q=q: lambda e:
                             e.matmul(s67[0:1, q * 512:q * 512 + 512],
                                      ones[:], scr[:, q * 512:q * 512 + 512],
                                      start=(i == 0), stop=(i == 7)))(),
                       r=["lnscr", "c_ones_col_f32"], w=["PS67"], tag=tag + "s2")
        p.emit("v", (lambda hs=hs: lambda e:
                     e.tensor_scalar_mul(stM[0:1, hs], s45[0:1, 0:1024],
                                         -1.0 / 1024))(),
               r=["PS45"], w=["lnstats"], tag=tag + "mean")
        p.emit("v", (lambda hs=hs: lambda e:
                     e.tensor_scalar_mul(stV[0:1, hs], s67[0:1, 0:1024],
                                         1.0 / 1024))(),
               r=["PS67"], w=["lnstats"], tag=tag + "msq")
    p.emit("v", lambda e: e.tensor_mul(scr[0:1, 0:2048], stM[:], stM[:]),
           r=["lnstats"], w=["lnscr"], tag=tag + "m2")
    p.emit("v", lambda e: e.tensor_sub(stV[:], stV[:], scr[0:1, 0:2048]),
           r=["lnstats", "lnscr"], w=["lnstats"], tag=tag + "var")
    p.emit("v", lambda e: e.tensor_scalar_add(stV[:], stV[:], 1e-5),
           r=["lnstats"], w=["lnstats"], tag=tag + "eps")
    p.emit("s", lambda e: e.activation(stV[:], stV[:], AF.Ln),
           r=["lnstats"], w=["lnstats"], tag=tag + "ln")
    p.emit("s", lambda e: e.activation(stV[:], stV[:], AF.Exp, scale=-0.5),
           r=["lnstats"], w=["lnstats"], tag=tag + "exp")
    p.emit("v", lambda e: e.tensor_mul(stM[:], stM[:], stV[:]),
           r=["lnstats"], w=["lnstats"], tag=tag + "nmr")
    rb, nb = sb["ln_rb"], sb["ln_nb"]
    for half in range(2):
        for q in range(2):
            qs = slice(half * 1024 + q * 512, half * 1024 + q * 512 + 512)
            p.emit("t", (lambda qs=qs, q=q: lambda e:
                         e.matmul(s45[:, q * 512:q * 512 + 512], onesr[0:1, :],
                                  stV[0:1, qs], start=True, stop=True))(),
                   r=["lnstats", "c_ones_row_f32"], w=["PS45"], tag=tag + "bc")
            p.emit("t", (lambda qs=qs, q=q: lambda e:
                         e.matmul(s67[:, q * 512:q * 512 + 512], onesr[0:1, :],
                                  stM[0:1, qs], start=True, stop=True))(),
                   r=["lnstats", "c_ones_row_f32"], w=["PS67"], tag=tag + "bc")
        p.emit("v", (lambda half=half: lambda e:
                     e.tensor_copy(rb[:, half * 1024:half * 1024 + 1024],
                                   s45[:, 0:1024]))(),
               r=["PS45"], w=["ln_rb"], tag=tag + "cb")
        p.emit("v", (lambda half=half: lambda e:
                     e.tensor_copy(nb[:, half * 1024:half * 1024 + 1024],
                                   s67[:, 0:1024]))(),
               r=["PS67"], w=["ln_nb"], tag=tag + "cb")
    for i in range(8):
        p.emit("v", (lambda i=i: lambda e:
                     e.tensor_mul(scr[:], sb[res_names[i]][:], rb[:]))(),
               r=[res_names[i], "ln_rb"], w=["lnscr"], tag=tag + "nm")
        p.emit("v", (lambda i=i: lambda e:
                     e.tensor_add(sb[out_bf[i]][:], scr[:], nb[:]))(),
               r=["lnscr", "ln_nb"], w=[out_bf[i]], tag=tag + "na")
        p.emit("v", (lambda i=i: lambda e:
                     e.tensor_add(sb["lnftmp"][:], scr[:], nb[:]))(),
               r=["lnscr", "ln_nb"], w=["lnftmp"], tag=tag + "nf")
        hook(i)



def _prep_watt(w_kqv):
    w = np.asarray(w_kqv, np.float32)
    out = np.zeros((8, 128, 3072), np.float32)
    for pair in range(8):
        cp = 128 * pair
        for kt in range(8):
            rows = slice(128 * kt, 128 * kt + 128)
            cc = kt * 128
            out[pair, :, cc:cc + 128] = w[rows, 1024 + cp:1024 + cp + 128]
            out[pair, :, 1024 + cc:1024 + cc + 128] = w[rows, cp:cp + 128]
            out[pair, :, 2048 + cc:2048 + cc + 128] = w[rows, 2048 + cp:2048 + cp + 128]
    return out


def run_launch_a(inputs, stage=99):
    consts = host_constants()
    cx = build_launch_a(consts, stage=stage)
    cx.p.seal(cx.es)
    cx.es.close()
    x = np.asarray(inputs["L0_attn_x"], np.float32)
    watt = _prep_watt(inputs["L0_attn_w_kqv"])
    rw = np.ascontiguousarray(np.asarray(inputs["L0_ffn_router_w"], np.float32))
    rb = np.asarray(inputs["L0_ffn_router_b"], np.float32).reshape(1, E)
    in_maps = [dict(x=np.ascontiguousarray(x[c]), watt=watt, rw=rw, rb=rb)
               for c in range(NCORES)]
    return run_bass_kernel_spmd(cx.nc, in_maps, core_ids=list(range(NCORES)))



def build_launch_b(consts):
    cx = Ctx("B")
    nc, p = cx.nc, cx.p
    lgf = nc.dram_tensor("lgfull", [NCORES * S, E], F32, kind="ExternalInput").ap()
    xblk = nc.dram_tensor("xblk", [2 * NCORES * BLK, D], BF16,
                          kind="ExternalInput").ap()
    wg_in = nc.dram_tensor("wg", [2, 11, 128, 1024], F32,
                           kind="ExternalInput").ap()
    wu_in = nc.dram_tensor("wu", [2, 11, 128, 1024], F32,
                           kind="ExternalInput").ap()
    wd_in = nc.dram_tensor("wd", [2, F, D], F32, kind="ExternalInput").ap()
    yblk = nc.dram_tensor("yblk", [2 * NCORES * BLK + 128, D], BF16,
                          kind="ExternalOutput").ap()
    esel_in = [nc.dram_tensor(f"eselin{i}", [1, 128], F32,
                              kind="ExternalInput").ap() for i in range(2)]

    sb = cx.sb
    cx.sbuf("lgt", (128, 128 * E), F32)
    cx.sbuf("RW", (128, 10 * E), F32)
    cx.sbuf("RC", (128, 16), F32)
    cx.sbuf("CNT", (1, 128), F32)
    cx.sbuf("RWB", (1, 128), F32)
    cx.sbuf("esel0", (1, 128), F32)
    cx.sbuf("esel1", (1, 128), F32)
    cx.sbuf("OFF", (128, 8), F32)
    cx.sbuf("CO", (1, 64), F32)
    cx.sbuf("COEF", (128, 8), F32)
    cx.sbuf("TCOL", (128, 1), F32)
    cx.sbuf("GID", (128, 1), F32)
    cx.sbuf("M8", (128, 8), F32)
    cx.sbuf("IDXF", (128, 2), F32)
    cx.sbuf("IDX", (128, 20 * 2), U32)
    cx.sbuf("RU", (128, 1), U8)
    cx.sbuf("xg", (128, 4 * 1024), BF16)
    cx.sbuf("xfm", (128, 8 * 512), BF16)
    cx.sbuf("hfm", (128, 11 * 512), BF16)
    cx.sbuf("ht", (128, 512), BF16)
    cx.sbuf("yout", (128, 4 * 1024), BF16)
    for i in range(2):
        cx.sbuf(f"wgsb{i}", (128, 8 * 128), BF16)
        cx.sbuf(f"wusb{i}", (128, 8 * 128), BF16)
        cx.sbuf(f"wdsb{i}", (128, 1024), BF16)
    ps01 = cx.psum("PS01", (128, 1024))
    ps23 = cx.psum("PS23", (128, 1024))
    ps45 = cx.psum("PS45", (128, 1024))
    ps67 = cx.psum("PS67", (128, 1024))
    cx.load_consts(consts, ["ones_col_f32", "ones_row_f32", "ident_bf",
                            "iota_col_f32", "iota16_f32"])

    p.emit("q", lambda e: e.dma_start(
        out=sb["lgt"].ap().rearrange("p (tt e) -> p tt e", e=E),
        in_=lgf.rearrange("(tt p) e -> p tt e", p=128)),
        w=["lgt"], dma=True)
    for i in range(2):
        p.emit("q", (lambda i=i: lambda e: e.dma_start(
            out=sb[f"esel{i}"][:], in_=esel_in[i]))(), w=["esel"], dma=True)

    RW, RC = sb["RW"], sb["RC"]
    zt = RW.ap()[:, 0:16]
    eq = RW.ap()[:, 16:32]
    ca = RW.ap()[:, 32:48]
    cb = RW.ap()[:, 48:64]
    is1 = RW.ap()[:, 64:80]
    zm = RW.ap()[:, 80:96]
    is2 = RW.ap()[:, 96:112]
    sel = RW.ap()[:, 112:128]
    tmp16 = RW.ap()[:, 144:160]

    def first_max(src_ap, out_ap, mcol):
        p.emit("v", (lambda: lambda e: e.reduce_max(
            RC[:, mcol:mcol + 1], src_ap, axis=AX.X))(), r=["RW"], w=["RC"])
        p.emit("v", (lambda: lambda e: e.tensor_scalar(
            eq, src_ap, RC[:, mcol:mcol + 1], None, ALU.is_equal))(),
            r=["RW", "RC"], w=["RW"])
        p.emit("v", lambda e: e.tensor_copy(ca, eq), r=["RW"], w=["RW"])
        sh = 1
        cur, nxt = ca, cb
        for stp in range(4):
            p.emit("v", (lambda cur=cur, nxt=nxt, sh=sh: lambda e:
                         e.tensor_copy(nxt[:, 0:sh], cur[:, 0:sh]))(),
                   r=["RW"], w=["RW"])
            p.emit("v", (lambda cur=cur, nxt=nxt, sh=sh: lambda e:
                         e.tensor_add(nxt[:, sh:16], cur[:, sh:16],
                                      cur[:, 0:16 - sh]))(),
                   r=["RW"], w=["RW"])
            cur, nxt = nxt, cur
            sh *= 2
        p.emit("v", (lambda cur=cur: lambda e: e.tensor_single_scalar(
            tmp16, cur, 1.0, ALU.is_equal))(), r=["RW"], w=["RW"])
        p.emit("v", (lambda: lambda e: e.tensor_mul(out_ap, eq, tmp16))(),
               r=["RW"], w=["RW"])

    for tt in range(128):
        c = tt // 16
        lg_t = sb["lgt"].ap()[:, tt * 16:tt * 16 + 16]
        p.emit("v", (lambda lg_t=lg_t: lambda e: e.reduce_max(
            RC[:, 0:1], lg_t, axis=AX.X))(), r=["lgt"], w=["RC"])
        p.emit("v", (lambda: lambda e: e.tensor_scalar_mul(
            RC[:, 1:2], RC[:, 0:1], -1.0))(), r=["RC"], w=["RC"])
        p.emit("s", (lambda lg_t=lg_t: lambda e: e.activation(
            zt, lg_t, AF.Exp, bias=RC[:, 1:2]))(), r=["lgt", "RC"], w=["RW"])
        first_max(zt, is1, 3)
        p.emit("v", (lambda: lambda e: e.scalar_tensor_tensor(
            zm, is1, -1e30, zt, ALU.mult, ALU.add))(), r=["RW"], w=["RW"])
        first_max(zm, is2, 4)
        p.emit("v", lambda e: e.tensor_add(sel, is1, is2), r=["RW"], w=["RW"])
        p.emit("t", (lambda c=c, tt=tt: lambda e: e.matmul(
            ps67[0:1, c * 16:c * 16 + 16], cx.const_ap["ones_col_f32"][:], sel,
            start=(tt % 16 == 0), stop=(tt % 16 == 15)))(),
            r=["RW", "c_ones_col_f32"], w=["PS67"], tag="cnt")
    p.emit("v", lambda e: e.tensor_copy(sb["CNT"][:], ps67[0:1, 0:128]),
           r=["PS67"], w=["CNT"], tag="cntcp")

    CO = sb["CO"]
    for ei in range(2):
        p.emit("v", (lambda ei=ei: lambda e: e.tensor_mul(
            sb["RWB"][:], sb["CNT"][:], sb[f"esel{ei}"][:]))(),
            r=["CNT", "esel"], w=["RWB"])
        p.emit("v", (lambda: lambda e: e.reduce_sum(
            CO[0:1, 0:8],
            sb["RWB"].ap()[0:1, 0:128].rearrange("a (c e) -> a c e", e=16),
            axis=AX.X))(), r=["RWB"], w=["CO"])
        p.emit("v", lambda e: e.memset(CO[0:1, 8:16], 0.0), w=["CO"])
        for c in range(1, 8):
            p.emit("v", (lambda c=c: lambda e: e.tensor_add(
                CO[0:1, 8 + c:9 + c], CO[0:1, 7 + c:8 + c],
                CO[0:1, c - 1:c]))(), r=["CO"], w=["CO"])
        p.emit("v", lambda e: e.tensor_add(CO[0:1, 16:17], CO[0:1, 15:16],
                                           CO[0:1, 7:8]), r=["CO"], w=["CO"])
        p.emit("v", lambda e: e.memset(CO[0:1, 24:25], 640.0), w=["CO"])
        for c in range(1, 8):
            p.emit("v", (lambda c=c: lambda e: e.tensor_scalar(
                CO[0:1, 24 + c:25 + c], CO[0:1, c - 1:c], -1.0, 640.0,
                ALU.mult, ALU.add))(), r=["CO"], w=["CO"])
        for c in range(8):
            p.emit("t", (lambda c=c: lambda e: e.matmul(
                ps45[:, c:c + 1], cx.const_ap["ones_row_f32"][0:1, :],
                CO[0:1, 8 + c:9 + c], start=True, stop=True))(),
                r=["CO", "c_ones_row_f32"], w=["PS45"])
            p.emit("t", (lambda c=c: lambda e: e.matmul(
                ps45[:, 8 + c:9 + c], cx.const_ap["ones_row_f32"][0:1, :],
                CO[0:1, 24 + c:25 + c], start=True, stop=True))(),
                r=["CO", "c_ones_row_f32"], w=["PS45"])
        p.emit("t", lambda e: e.matmul(
            ps45[:, 16:17], cx.const_ap["ones_row_f32"][0:1, :],
            CO[0:1, 16:17], start=True, stop=True),
            r=["CO", "c_ones_row_f32"], w=["PS45"])
        p.emit("v", lambda e: e.tensor_copy(sb["OFF"][:], ps45[:, 0:8]),
               r=["PS45"], w=["OFF"])
        p.emit("v", lambda e: e.tensor_copy(sb["COEF"][:], ps45[:, 8:16]),
               r=["PS45"], w=["COEF"])
        p.emit("v", lambda e: e.tensor_copy(sb["TCOL"][:], ps45[:, 16:17]),
               r=["PS45"], w=["TCOL"])
        for j in range(20):
            p.emit("v", (lambda j=j: lambda e: e.tensor_scalar_add(
                sb["GID"][:], cx.const_ap["iota_col_f32"][:], float(128 * j)))(),
                r=["c_iota_col_f32"], w=["GID"])
            p.emit("v", (lambda: lambda e: e.tensor_scalar(
                sb["M8"][:], sb["OFF"][:], sb["GID"][:], None, ALU.is_le))(),
                r=["OFF", "GID"], w=["M8"])
            p.emit("v", (lambda: lambda e: e.tensor_mul(
                sb["M8"][:], sb["M8"][:], sb["COEF"][:]))(),
                r=["M8", "COEF"], w=["M8"])
            p.emit("v", (lambda: lambda e: e.reduce_sum(
                sb["IDXF"][:, 0:1], sb["M8"][:], axis=AX.X))(),
                r=["M8"], w=["IDXF"])
            base = ei * NCORES * BLK - BLK
            p.emit("v", (lambda base=base: lambda e: e.scalar_tensor_tensor(
                sb["IDXF"][:, 0:1], sb["GID"][:], 1.0, sb["IDXF"][:, 0:1],
                ALU.mult, ALU.add))(), r=["GID", "IDXF"], w=["IDXF"])
            p.emit("v", (lambda base=base: lambda e: e.tensor_scalar_add(
                sb["IDXF"][:, 0:1], sb["IDXF"][:, 0:1], float(base)))(),
                r=["IDXF"], w=["IDXF"])
            p.emit("v", (lambda: lambda e: e.tensor_scalar(
                sb["IDXF"][:, 1:2], sb["GID"][:], sb["TCOL"][:], None,
                ALU.is_lt))(), r=["GID", "TCOL"], w=["IDXF"])
            p.emit("v", lambda e: e.tensor_copy(sb["RU"][:], sb["IDXF"][:, 1:2]),
                   r=["IDXF"], w=["RU"])
            p.emit("v", (lambda: lambda e: e.tensor_scalar_add(
                sb["GID"][:], cx.const_ap["iota_col_f32"][:],
                float(2 * NCORES * BLK)))(), r=["c_iota_col_f32"], w=["GID"])
            p.emit("v", (lambda: lambda e: e.select(
                sb["IDXF"][:, 0:1], sb["RU"][:], sb["IDXF"][:, 0:1],
                sb["GID"][:]))(), r=["IDXF", "RU", "GID"], w=["IDXF"])
            p.emit("v", (lambda j=j, ei=ei: lambda e: e.tensor_copy(
                sb["IDX"][:, ei * 20 + j:ei * 20 + j + 1],
                sb["IDXF"][:, 0:1]))(), r=["IDXF"], w=["IDX"])
        for grp in range(5):
            for tj in range(4):
                j = grp * 4 + tj
                p.emit("g", (lambda j=j, tj=tj, ei=ei: lambda e:
                             e.indirect_dma_start(
                                 out=sb["xg"][:, tj * 1024:tj * 1024 + 1024],
                                 out_offset=None, in_=xblk,
                                 in_offset=IOA(sb["IDX"][:, ei * 20 + j:
                                                         ei * 20 + j + 1], 0)))(),
                       r=["IDX"], w=["xg"], dma=True, tag="gat")
            for tj in range(4):
                psb = (ps01 if tj % 2 == 0 else ps23).ap().bitcast(BF16)
                pk = "PS01" if tj % 2 == 0 else "PS23"
                for kt in range(8):
                    p.emit("t", (lambda tj=tj, kt=kt, psb=psb: lambda e:
                                 e.transpose(
                                     psb[:, kt * 128:kt * 128 + 128],
                                     sb["xg"][:, tj * 1024 + kt * 128:
                                              tj * 1024 + kt * 128 + 128],
                                     cx.const_ap["ident_bf"][:]))(),
                           r=["xg", "c_ident_bf"], w=[pk], tag="xtr")
                p.emit("v", (lambda tj=tj, psb=psb: lambda e: e.tensor_copy(
                    sb["xfm"].ap().rearrange("p (k t) -> p k t", t=512)
                    [:, 0:8, tj * 128:tj * 128 + 128],
                    psb.rearrange("p (k t) -> p k t", t=128)[:, 0:8, :]))(),
                    r=[pk], w=["xfm"], tag="xtre")
            for f in range(11):
                gn, un = f"wgsb{f % 2}", f"wusb{f % 2}"
                for wnm, win in ((gn, wg_in), (un, wu_in)):
                    p.emit("g", (lambda wnm=wnm, win=win, f=f, ei=ei: lambda e:
                                 e.dma_start(
                                     out=sb[wnm][:], in_=win[ei][f]))(),
                           r=[wnm], w=[wnm], dma=True, tag="ldw")
                for kt in range(8):
                    p.emit("t", (lambda kt=kt, gn=gn: lambda e: e.matmul(
                        ps45[:, 0:512], sb[gn][:, kt * 128:kt * 128 + 128],
                        sb["xfm"][:, kt * 512:kt * 512 + 512],
                        start=(kt == 0), stop=(kt == 7)))(),
                        r=[gn, "xfm"], w=["PS45"], tag="gmm")
                    p.emit("t", (lambda kt=kt, un=un: lambda e: e.matmul(
                        ps67[:, 0:512], sb[un][:, kt * 128:kt * 128 + 128],
                        sb["xfm"][:, kt * 512:kt * 512 + 512],
                        start=(kt == 0), stop=(kt == 7)))(),
                        r=[un, "xfm"], w=["PS67"], tag="umm")
                p.emit("s", (lambda f=f: lambda e: e.activation(
                    sb["ht"][:], ps45[:, 0:512], AF.Silu))(),
                    r=["PS45"], w=["ht"], tag="silu")
                p.emit("v", (lambda f=f: lambda e: e.tensor_mul(
                    sb["hfm"][:, f * 512:f * 512 + 512], sb["ht"][:],
                    ps67[:, 0:512]))(), r=["ht", "PS67"], w=["hfm"], tag="hmul")
            PSL = [ps01, ps23, ps45, ps67]
            PKL = ["PS01", "PS23", "PS45", "PS67"]
            for f in range(11):
                dn = f"wdsb{f % 2}"
                p.emit("g", (lambda f=f, ei=ei, dn=dn: lambda e: e.dma_start(
                    out=sb[dn][:], in_=wd_in[ei][f * 128:f * 128 + 128, :]))(),
                    r=[dn], w=[dn], dma=True, tag="lwd")
                for tj in range(4):
                    psY = PSL[tj]
                    pk = PKL[tj]
                    for ch in range(2):
                        p.emit("t", (lambda f=f, tj=tj, ch=ch, psY=psY, dn=dn:
                                     lambda e: e.matmul(
                                         psY[:, ch * 512:ch * 512 + 512],
                                         sb["hfm"][:, f * 512 + tj * 128:
                                                   f * 512 + tj * 128 + 128],
                                         sb[dn][:, ch * 512:ch * 512 + 512],
                                         start=(f == 0), stop=(f == 10)))(),
                               r=["hfm", dn], w=[pk], tag="dmm")
            for tj in range(4):
                psY = PSL[tj]
                pk = PKL[tj]
                p.emit("s", (lambda tj=tj, psY=psY: lambda e: e.activation(
                    sb["yout"][:, tj * 1024:tj * 1024 + 1024], psY[:],
                    AF.Copy))(), r=[pk], w=["yout"], tag="yev")
                j = grp * 4 + tj
                p.emit("g", (lambda j=j, tj=tj, ei=ei: lambda e:
                             e.indirect_dma_start(
                                 out=yblk, out_offset=IOA(
                                     sb["IDX"][:, ei * 20 + j:ei * 20 + j + 1], 0),
                                 in_=sb["yout"][:, tj * 1024:tj * 1024 + 1024],
                                 in_offset=None))(),
                       r=["IDX", "yout"], w=["yblk_d"], dma=True, tag="scat")
    p.barrier(engs=("g",), on=("g",))
    p.barrier(engs=("q",), on=("q", "g"))
    return cx


def run_launch_b(inputs, lgfull, ablk):
    consts = host_constants()
    cx = build_launch_b(consts)
    cx.p.seal(cx.es)
    cx.es.close()
    wg = np.asarray(inputs["L0_ffn_w_gate"], np.float32)
    wu = np.asarray(inputs["L0_ffn_w_up"], np.float32)
    wd = np.asarray(inputs["L0_ffn_w_down"], np.float32)
    in_maps = []
    for pcore in range(NCORES):
        ea, eb = 2 * pcore, 2 * pcore + 1
        xb = np.concatenate(
            [np.concatenate([ablk[c][e * BLK:(e + 1) * BLK] for c in range(8)], 0)
             for e in (ea, eb)], 0)
        es0 = np.zeros((1, 16), np.float32); es0[0, ea] = 1.0
        es1 = np.zeros((1, 16), np.float32); es1[0, eb] = 1.0
        es0 = np.tile(es0, (1, 8)); es1 = np.tile(es1, (1, 8))
        pkb = lambda w: np.ascontiguousarray(
            w.reshape(2, 8, 128, 11, 128).transpose(0, 3, 2, 1, 4)
            .reshape(2, 11, 128, 1024))
        in_maps.append(dict(
            eselin0=es0, eselin1=es1,
            lgfull=lgfull, xblk=np.ascontiguousarray(xb),
            wg=pkb(wg[ea:eb + 1]),
            wu=pkb(wu[ea:eb + 1]),
            wd=np.ascontiguousarray(wd[ea:eb + 1])))
    return run_bass_kernel_spmd(cx.nc, in_maps, core_ids=list(range(NCORES)))



def build_launch_c(consts, stage=99, cx=None, ext=None):
    merged = ext is not None
    if cx is None:
        cx = Ctx("C")
    nc, p = cx.nc, cx.p
    if merged:
        xin, lgown, yin = ext["xin"], ext["lgown"], ext["yin"]
        swg_in, swu_in, swd_in = ext["swg"], ext["swu"], ext["swd"]
        xmid = None
    else:
        xin = nc.dram_tensor("xin", [D, S], F32, kind="ExternalInput").ap()
        lgown = nc.dram_tensor("lgown", [S, E], F32, kind="ExternalInput").ap()
        yin = nc.dram_tensor("yin", [E * BLK, D], BF16,
                             kind="ExternalInput").ap()
        swg_in = nc.dram_tensor("swg", [D, F], F32, kind="ExternalInput").ap()
        swu_in = nc.dram_tensor("swu", [D, F], F32, kind="ExternalInput").ap()
        swd_in = nc.dram_tensor("swd", [F, D], F32, kind="ExternalInput").ap()
        xmid = nc.dram_tensor("xmid", [D, S], F32, kind="ExternalOutput").ap()

    sb = cx.sb
    for i in range(8):
        cx.sbuf(f"res{i}", (128, S), F32)
    for i in range(8):
        cx.sbuf(f"lnb{i}", (128, S), BF16)
    cx.sbuf("lnscr", (128, S), F32)
    cx.sbuf("ln_rb", (128, S), F32)
    cx.sbuf("ln_nb", (128, S), F32)
    cx.sbuf("stM", (1, S), F32)
    cx.sbuf("stV", (1, S), F32)
    cx.sbuf("lnftmp", (128, S), F32)
    cx.sbuf("lgsb", (128, 16 * E), F32)
    cx.sbuf("RW", (128, 10 * E), F32)
    cx.sbuf("RC", (128, 16), F32)
    cx.sbuf("RU", (128, 4), U8)
    cx.sbuf("IDX", (128, 4), U32)
    cx.sbuf("CR", (1, E), F32)
    cx.sbuf("yg", (128, 2 * 1024), BF16)
    cx.sbuf("mo", (128, 1024), F32)
    ps01 = cx.psum("PS01", (128, 1024))
    ps23 = cx.psum("PS23", (128, 1024))
    ps45 = cx.psum("PS45", (128, 1024))
    ps67 = cx.psum("PS67", (128, 1024))
    cx.load_consts(consts, ["ones_col_f32", "ones_row_f32", "ident_f32",
                            "ident_bf", "iota_col_f32", "iota16_f32",
                            "lt_incl_f32"])

    res_names = [f"res{i}" for i in range(8)]
    lnb_names = [f"lnb{i}" for i in range(8)]

    for i in range(8):
        p.emit("q", (lambda i=i: lambda e: e.dma_start(
            out=sb[f"res{i}"][:], in_=xin[i * 128:(i + 1) * 128, :]))(),
            w=[f"res{i}"], dma=True, tag="ldx")
    p.emit("q", lambda e: e.dma_start(
        out=sb["lgsb"].ap().rearrange("p (tt e) -> p tt e", e=E),
        in_=lgown.rearrange("(tt p) e -> p tt e", p=128)),
        w=["lgsb"], dma=True)

    build_layernorm(cx, res_names, lnb_names, tag="c1")
    for i in range(8):
        p.emit("v", (lambda i=i: lambda e: e.tensor_mul(
            sb["lnscr"][:], sb[f"res{i}"][:], sb["ln_rb"][:]))(),
            r=[f"res{i}", "ln_rb"], w=["lnscr"], tag="lt1")
        p.emit("v", (lambda i=i: lambda e: e.tensor_add(
            sb["lnscr"][:], sb["lnscr"][:], sb["ln_nb"][:]))(),
            r=["lnscr", "ln_nb"], w=["lnscr"], tag="lt2")
        p.emit("v", (lambda i=i: lambda e: e.tensor_add(
            sb[f"res{i}"][:], sb[f"res{i}"][:], sb["lnscr"][:]))(),
            r=["lnscr", f"res{i}"], w=[f"res{i}"], tag="lt3")

    RW, RC, RU, IDX = sb["RW"], sb["RC"], sb["RU"], sb["IDX"]
    zt = RW.ap()[:, 0:16]
    eq = RW.ap()[:, 16:32]
    ca = RW.ap()[:, 32:48]
    cb = RW.ap()[:, 48:64]
    is1 = RW.ap()[:, 64:80]
    zm = RW.ap()[:, 80:96]
    is2 = RW.ap()[:, 96:112]
    sel = RW.ap()[:, 112:128]
    pos = RW.ap()[:, 128:144]
    tmp16 = RW.ap()[:, 144:160]

    def first_max(src_ap, out_ap, mcol):
        p.emit("v", (lambda: lambda e: e.reduce_max(
            RC[:, mcol:mcol + 1], src_ap, axis=AX.X))(), r=["RW"], w=["RC"])
        p.emit("v", (lambda: lambda e: e.tensor_scalar(
            eq, src_ap, RC[:, mcol:mcol + 1], None, ALU.is_equal))(),
            r=["RW", "RC"], w=["RW"])
        p.emit("v", lambda e: e.tensor_copy(ca, eq), r=["RW"], w=["RW"])
        sh = 1
        cur, nxt = ca, cb
        for stp in range(4):
            p.emit("v", (lambda cur=cur, nxt=nxt, sh=sh: lambda e:
                         e.tensor_copy(nxt[:, 0:sh], cur[:, 0:sh]))(),
                   r=["RW"], w=["RW"])
            p.emit("v", (lambda cur=cur, nxt=nxt, sh=sh: lambda e:
                         e.tensor_add(nxt[:, sh:16], cur[:, sh:16],
                                      cur[:, 0:16 - sh]))(), r=["RW"], w=["RW"])
            cur, nxt = nxt, cur
            sh *= 2
        p.emit("v", (lambda cur=cur: lambda e: e.tensor_single_scalar(
            tmp16, cur, 1.0, ALU.is_equal))(), r=["RW"], w=["RW"])
        p.emit("v", (lambda: lambda e: e.tensor_mul(out_ap, eq, tmp16))(),
               r=["RW"], w=["RW"])

    p.emit("v", lambda e: e.memset(sb["CR"][:], 0.0), w=["CR"])
    for tt in range(16):
        lg_t = sb["lgsb"].ap()[:, tt * 16:tt * 16 + 16]
        p.emit("v", (lambda lg_t=lg_t: lambda e: e.reduce_max(
            RC[:, 0:1], lg_t, axis=AX.X))(), r=["lgsb"], w=["RC"])
        p.emit("v", (lambda: lambda e: e.tensor_scalar_mul(
            RC[:, 1:2], RC[:, 0:1], -1.0))(), r=["RC"], w=["RC"])
        p.emit("s", (lambda lg_t=lg_t: lambda e: e.activation(
            zt, lg_t, AF.Exp, bias=RC[:, 1:2]))(), r=["lgsb", "RC"], w=["RW"])
        p.emit("v", (lambda: lambda e: e.reduce_sum(
            RC[:, 2:3], zt, axis=AX.X))(), r=["RW"], w=["RC"])
        first_max(zt, is1, 3)
        p.emit("v", (lambda: lambda e: e.scalar_tensor_tensor(
            zm, is1, -1e30, zt, ALU.mult, ALU.add))(), r=["RW"], w=["RW"])
        first_max(zm, is2, 4)
        p.emit("v", lambda e: e.tensor_add(sel, is1, is2), r=["RW"], w=["RW"])
        p.emit("v", (lambda: lambda e: e.tensor_add(
            RC[:, 5:6], RC[:, 3:4], RC[:, 4:5]))(), r=["RC"], w=["RC"])
        p.emit("v", (lambda: lambda e: e.scalar_tensor_tensor(
            RC[:, 5:6], RC[:, 2:3], 1e-9, RC[:, 5:6], ALU.mult, ALU.add))(),
            r=["RC"], w=["RC"])
        p.emit("s", lambda e: e.activation(RC[:, 6:7], RC[:, 5:6], AF.Ln),
               r=["RC"], w=["RC"])
        p.emit("s", lambda e: e.activation(RC[:, 6:7], RC[:, 6:7], AF.Exp,
                                           scale=-1.0), r=["RC"], w=["RC"])
        p.emit("v", (lambda: lambda e: e.tensor_mul(
            RC[:, 7:8], RC[:, 3:4], RC[:, 6:7]))(), r=["RC"], w=["RC"])
        p.emit("v", (lambda: lambda e: e.tensor_mul(
            RC[:, 8:9], RC[:, 4:5], RC[:, 6:7]))(), r=["RC"], w=["RC"])
        p.emit("t", (lambda: lambda e: e.matmul(
            ps45[:, 0:16], cx.const_ap["lt_incl_f32"][:], sel,
            start=True, stop=False))(),
            r=["RW", "c_lt_incl_f32"], w=["PS45"])
        p.emit("t", (lambda: lambda e: e.matmul(
            ps45[:, 0:16], cx.const_ap["ones_row_f32"][0:1, :], sb["CR"][:],
            start=False, stop=True))(), r=["CR", "c_ones_row_f32"], w=["PS45"])
        p.emit("v", (lambda: lambda e: e.tensor_copy(pos, ps45[:, 0:16]))(),
               r=["PS45"], w=["RW"])
        p.emit("t", (lambda: lambda e: e.matmul(
            ps67[0:1, 0:16], cx.const_ap["ones_col_f32"][:], sel,
            start=True, stop=True))(), r=["RW", "c_ones_col_f32"], w=["PS67"])
        p.emit("v", (lambda: lambda e: e.tensor_add(
            sb["CR"][:], sb["CR"][:], ps67[0:1, 0:16]))(),
            r=["PS67", "CR"], w=["CR"])
        for k in range(2):
            ismask = is1 if k == 0 else is2
            p.emit("v", (lambda ismask=ismask: lambda e: e.tensor_mul(
                tmp16, ismask, cx.const_ap["iota16_f32"][:]))(),
                r=["RW", "c_iota16_f32"], w=["RW"])
            p.emit("v", (lambda k=k: lambda e: e.reduce_sum(
                RC[:, 9 + k:10 + k], tmp16, axis=AX.X))(), r=["RW"], w=["RC"])
            p.emit("v", (lambda ismask=ismask: lambda e: e.tensor_mul(
                tmp16, ismask, pos))(), r=["RW"], w=["RW"])
            p.emit("v", (lambda k=k: lambda e: e.reduce_sum(
                RC[:, 11 + k:12 + k], tmp16, axis=AX.X))(), r=["RW"], w=["RC"])
            p.emit("v", (lambda k=k: lambda e: e.tensor_scalar_add(
                RC[:, 11 + k:12 + k], RC[:, 11 + k:12 + k], -1.0))(),
                r=["RC"], w=["RC"])
            p.emit("v", (lambda k=k: lambda e: e.scalar_tensor_tensor(
                RC[:, 13:14], RC[:, 9 + k:10 + k], float(BLK),
                RC[:, 11 + k:12 + k], ALU.mult, ALU.add))(), r=["RC"], w=["RC"])
            p.emit("v", (lambda k=k: lambda e: e.tensor_single_scalar(
                RC[:, 14:15], RC[:, 11 + k:12 + k], float(BLK), ALU.is_lt))(),
                r=["RC"], w=["RC"])
            p.emit("v", (lambda k=k: lambda e: e.tensor_copy(
                RU[:, k:k + 1], RC[:, 14:15]))(), r=["RC"], w=["RU"])
            p.emit("v", (lambda k=k: lambda e: e.tensor_mul(
                RC[:, 7 + k:8 + k], RC[:, 7 + k:8 + k], RC[:, 14:15]))(),
                r=["RC"], w=["RC"])
            p.emit("v", (lambda k=k: lambda e: e.tensor_mul(
                RC[:, 13:14], RC[:, 13:14], RC[:, 14:15]))(), r=["RC"], w=["RC"])
            p.emit("v", (lambda k=k: lambda e: e.tensor_copy(
                IDX[:, k:k + 1], RC[:, 13:14]))(), r=["RC"], w=["IDX"])
            p.emit("g", (lambda k=k: lambda e: e.indirect_dma_start(
                out=sb["yg"][:, k * 1024:k * 1024 + 1024], out_offset=None,
                in_=yin, in_offset=IOA(IDX[:, k:k + 1], 0)))(),
                r=["IDX"], w=["yg"], dma=True, tag="ygat")
        p.emit("v", (lambda: lambda e: e.tensor_scalar(
            sb["mo"][:], sb["yg"][:, 0:1024], RC[:, 7:8], None, ALU.mult))(),
            r=["yg", "RC"], w=["mo"])
        p.emit("v", (lambda: lambda e: e.scalar_tensor_tensor(
            sb["mo"][:], sb["yg"][:, 1024:2048], RC[:, 8:9], sb["mo"][:],
            ALU.mult, ALU.add))(), r=["yg", "RC", "mo"], w=["mo"])
        for i in range(8):
            psx = ps01 if i % 2 == 0 else ps23
            pk = "PS01" if i % 2 == 0 else "PS23"
            p.emit("t", (lambda i=i, psx=psx: lambda e: e.transpose(
                psx[:, 0:128], sb["mo"][:, i * 128:i * 128 + 128],
                cx.const_ap["ident_f32"][:]))(),
                r=["mo", "c_ident_f32"], w=[pk], tag="motr")
            p.emit("v", (lambda i=i, tt=tt, psx=psx: lambda e: e.tensor_add(
                sb[f"res{i}"][:, tt * 128:tt * 128 + 128],
                sb[f"res{i}"][:, tt * 128:tt * 128 + 128],
                psx[:, 0:128]))(), r=[pk, f"res{i}"], w=[f"res{i}"], tag="moadd")

    cx.sbuf("wgsb", (128, 8 * 128), BF16)
    cx.sbuf("wusb", (128, 8 * 128), BF16)
    cx.sbuf("wdall", (128, 11 * 1024), BF16)
    cx.sbuf("hfm", (128, 11 * 1024), BF16)
    cx.sbuf("ht", (128, 1024), BF16)
    for half in range(2):
        t0 = half * 1024
        for f in range(11):
            for wnm, win in (("wgsb", swg_in), ("wusb", swu_in)):
                p.emit("g", (lambda wnm=wnm, win=win, f=f: lambda e:
                             e.dma_start(
                                 out=sb[wnm].ap().rearrange(
                                     "p (k f) -> p k f", f=128),
                                 in_=win.rearrange("(k p) f -> p k f", p=128)
                                 [:, :, f * 128:f * 128 + 128]))(),
                       r=[wnm], w=[wnm], dma=True, tag="lsw")
            for kt in range(8):
                for ch in range(2):
                    p.emit("t", (lambda kt=kt, ch=ch, t0=t0: lambda e: e.matmul(
                        ps45[:, ch * 512:ch * 512 + 512],
                        sb["wgsb"][:, kt * 128:kt * 128 + 128],
                        sb[f"lnb{kt}"][:, t0 + ch * 512:t0 + ch * 512 + 512],
                        start=(kt == 0), stop=(kt == 7)))(),
                        r=["wgsb", f"lnb{kt}"], w=["PS45"], tag="sg")
                    p.emit("t", (lambda kt=kt, ch=ch, t0=t0: lambda e: e.matmul(
                        ps67[:, ch * 512:ch * 512 + 512],
                        sb["wusb"][:, kt * 128:kt * 128 + 128],
                        sb[f"lnb{kt}"][:, t0 + ch * 512:t0 + ch * 512 + 512],
                        start=(kt == 0), stop=(kt == 7)))(),
                        r=["wusb", f"lnb{kt}"], w=["PS67"], tag="su")
            p.emit("s", (lambda f=f: lambda e: e.activation(
                sb["ht"][:], ps45[:, 0:1024], AF.Silu))(),
                r=["PS45"], w=["ht"], tag="ssilu")
            p.emit("v", (lambda f=f: lambda e: e.tensor_mul(
                sb["hfm"][:, f * 1024:f * 1024 + 1024], sb["ht"][:],
                ps67[:, 0:1024]))(), r=["ht", "PS67"], w=["hfm"], tag="shmul")
        if half == 0:
            for f in range(11):
                p.emit("g", (lambda f=f: lambda e: e.dma_start(
                    out=sb["wdall"][:, f * 1024:f * 1024 + 1024],
                    in_=swd_in[f * 128:f * 128 + 128, :]))(),
                    r=["wdall"], w=["wdall"], dma=True, tag="lswd")
        for m in range(8):
            psD = ps01 if m % 2 == 0 else ps23
            pk = "PS01" if m % 2 == 0 else "PS23"
            for f in range(11):
                for ch in range(2):
                    p.emit("t", (lambda f=f, m=m, ch=ch, psD=psD, t0=t0:
                                 lambda e: e.matmul(
                                     psD[:, ch * 512:ch * 512 + 512],
                                     sb["wdall"][:, f * 1024 + m * 128:
                                                 f * 1024 + m * 128 + 128],
                                     sb["hfm"][:, f * 1024 + ch * 512:
                                               f * 1024 + ch * 512 + 512],
                                     start=(f == 0), stop=(f == 10)))(),
                           r=["wdall", "hfm"], w=[pk], tag="sd")
            p.emit("v", (lambda m=m, psD=psD, t0=t0: lambda e:
                         e.tensor_add(
                             sb[f"res{m}"][:, t0:t0 + 1024],
                             sb[f"res{m}"][:, t0:t0 + 1024],
                             psD[:, 0:1024]))(),
                   r=[pk, f"res{m}"], w=[f"res{m}"], tag="sdadd")
    if not merged:
        for i in range(8):
            p.emit("q", (lambda i=i: lambda e: e.dma_start(
                out=xmid[i * 128:(i + 1) * 128, :], in_=sb[f"res{i}"][:]))(),
                r=[f"res{i}"], w=["xmid_d"], dma=True, tag="xst")
        p.barrier(engs=("q",), on=("q", "g"))
        p.barrier(engs=("g",), on=("g",))
    return cx


def run_launch_c1(inputs, xouts, lgouts, yblks):
    consts = host_constants()
    cx = build_launch_c(consts)
    cx.p.seal(cx.es)
    cx.es.close()
    swg = np.ascontiguousarray(np.asarray(inputs["L0_ffn_sw_gate"], np.float32))
    swu = np.ascontiguousarray(np.asarray(inputs["L0_ffn_sw_up"], np.float32))
    swd = np.ascontiguousarray(np.asarray(inputs["L0_ffn_sw_down"], np.float32))
    in_maps = []
    for c in range(NCORES):
        yin = np.concatenate(
            [yblks[e // 2][(e % 2) * NCORES * BLK + c * BLK:
                           (e % 2) * NCORES * BLK + (c + 1) * BLK] for e in range(E)], 0)
        in_maps.append(dict(xin=xouts[c], lgown=lgouts[c],
                            yin=np.ascontiguousarray(yin),
                            swg=swg, swu=swu, swd=swd))
    return run_bass_kernel_spmd(cx.nc, in_maps, core_ids=list(range(NCORES)))



def build_layernorm2(cx, res_names, out_bf, hook=None, tag="ln"):
    p, sb = cx.p, cx.sb
    ones = cx.const_ap["ones_col_f32"]
    onesr = cx.const_ap["ones_row_f32"]
    s45, s67 = cx.ps["PS45"], cx.ps["PS67"]
    scr = sb["lnscr"]
    stM, stV, stR = sb["stM"], sb["stV"], sb["stR"]
    rb, nb = sb["ln_rb"], sb["ln_nb"]
    for half in range(2):
        hs = slice(half * 1024, half * 1024 + 1024)
        for i in range(8):
            for q in range(2):
                qs = slice(half * 1024 + q * 512, half * 1024 + q * 512 + 512)
                p.emit("t", (lambda i=i, q=q, qs=qs: lambda e:
                             e.matmul(s45[0:1, q * 512:q * 512 + 512],
                                      ones[:], sb[res_names[i]][:, qs],
                                      start=(i == 0), stop=(i == 7)))(),
                       r=[res_names[i], "c_ones_col_f32"], w=["PS45"], tag=tag + "su")
        for i in range(8):
            p.emit("s", (lambda i=i, hs=hs: lambda e:
                         e.activation(scr[:], sb[res_names[i]][:, hs],
                                      AF.Square))(),
                   r=[res_names[i]], w=["lnscr"], tag=tag + "sq")
            for q in range(2):
                p.emit("t", (lambda i=i, q=q: lambda e:
                             e.matmul(s67[0:1, q * 512:q * 512 + 512],
                                      ones[:], scr[:, q * 512:q * 512 + 512],
                                      start=(i == 0), stop=(i == 7)))(),
                       r=["lnscr", "c_ones_col_f32"], w=["PS67"], tag=tag + "s2")
        p.emit("v", (lambda hs=hs: lambda e:
                     e.tensor_scalar_mul(stM[0:1, hs], s45[0:1, 0:1024],
                                         -1.0 / 1024))(),
               r=["PS45"], w=["lnstats"], tag=tag + "mean")
        p.emit("v", (lambda hs=hs: lambda e:
                     e.tensor_scalar_mul(stV[0:1, hs], s67[0:1, 0:1024],
                                         1.0 / 1024))(),
               r=["PS67"], w=["lnstats"], tag=tag + "msq")
    p.emit("v", lambda e: e.tensor_mul(stR[:], stM[:], stM[:]),
           r=["lnstats"], w=["lnstats"], tag=tag + "m2")
    p.emit("v", lambda e: e.tensor_sub(stV[:], stV[:], stR[:]),
           r=["lnstats"], w=["lnstats"], tag=tag + "var")
    p.emit("v", lambda e: e.tensor_scalar_add(stV[:], stV[:], 1e-5),
           r=["lnstats"], w=["lnstats"], tag=tag + "eps")
    p.emit("s", lambda e: e.activation(stR[:], stV[:], AF.Ln),
           r=["lnstats"], w=["lnstats"], tag=tag + "ln")
    p.emit("s", lambda e: e.activation(stR[:], stR[:], AF.Exp, scale=-0.5),
           r=["lnstats"], w=["lnstats"], tag=tag + "exp")
    p.emit("v", lambda e: e.tensor_mul(stM[:], stM[:], stR[:]),
           r=["lnstats"], w=["lnstats"], tag=tag + "nmr")
    for half in range(2):
        hs = slice(half * 1024, half * 1024 + 1024)
        for q in range(2):
            qs = slice(half * 1024 + q * 512, half * 1024 + q * 512 + 512)
            p.emit("t", (lambda qs=qs, q=q: lambda e:
                         e.matmul(s45[:, q * 512:q * 512 + 512], onesr[0:1, :],
                                  sb["stR"][0:1, qs], start=True, stop=True))(),
                   r=["lnstats", "c_ones_row_f32"], w=["PS45"], tag=tag + "bc")
            p.emit("t", (lambda qs=qs, q=q: lambda e:
                         e.matmul(s67[:, q * 512:q * 512 + 512], onesr[0:1, :],
                                  sb["stM"][0:1, qs], start=True, stop=True))(),
                   r=["lnstats", "c_ones_row_f32"], w=["PS67"], tag=tag + "bc")
        p.emit("v", lambda e: e.tensor_copy(sb["ln_rb"][:], s45[:, 0:1024]),
               r=["PS45"], w=["ln_rb"], tag=tag + "cb")
        p.emit("v", lambda e: e.tensor_copy(sb["ln_nb"][:], s67[:, 0:1024]),
               r=["PS67"], w=["ln_nb"], tag=tag + "cb")
        for i in range(8):
            p.emit("v", (lambda i=i, hs=hs: lambda e:
                         e.tensor_mul(scr[:], sb[res_names[i]][:, hs],
                                      sb["ln_rb"][:]))(),
                   r=[res_names[i], "ln_rb"], w=["lnscr"], tag=tag + "nm")
            p.emit("v", (lambda i=i, hs=hs: lambda e:
                         e.tensor_add(sb[out_bf[i]][:, hs], scr[:],
                                      sb["ln_nb"][:]))(),
                   r=["lnscr", "ln_nb"], w=[out_bf[i]], tag=tag + "na")
            if hook is not None:
                hook(i, half)


def build_launch_c2(consts, stage=99, cx=None, ext=None):
    merged = ext is not None
    if cx is None:
        cx = Ctx("C2")
    nc, p = cx.nc, cx.p
    if merged:
        xin = None
    else:
        xin = nc.dram_tensor("xin", [D, S], F32, kind="ExternalInput").ap()
    wx_in = nc.dram_tensor("wxt", [16, 128, 1024], F32, kind="ExternalInput").ap()
    wz_in = nc.dram_tensor("wzt", [16, 128, 1024], F32, kind="ExternalInput").ap()
    cw_in = nc.dram_tensor("cw", [16, 128, 4], F32, kind="ExternalInput").ap()
    cb_in = nc.dram_tensor("cb", [128, 16], F32, kind="ExternalInput").ap()
    wb_in = nc.dram_tensor("wbt", [4, 128, 2048], F32, kind="ExternalInput").ap()
    wc_in = nc.dram_tensor("wct", [4, 128, 2048], F32, kind="ExternalInput").ap()
    wdt_in = nc.dram_tensor("wdtt", [128, 128], F32, kind="ExternalInput").ap()
    bdt_in = nc.dram_tensor("bdt", [1, L1H], F32, kind="ExternalInput").ap()
    la_in = nc.dram_tensor("loga", [1, L1H], F32, kind="ExternalInput").ap()
    dp_in = nc.dram_tensor("dparam", [1, L1H], F32, kind="ExternalInput").ap()
    gw_in = nc.dram_tensor("gnw", [128, 16], F32, kind="ExternalInput").ap()
    gb_in = nc.dram_tensor("gnb", [128, 16], F32, kind="ExternalInput").ap()
    wo_in = nc.dram_tensor("woutt", [8, 128, 2048], F32, kind="ExternalInput").ap()
    xout = None if merged else nc.dram_tensor("xout", [D, S], F32,
                                              kind="ExternalOutput").ap()
    ykind = "ExternalOutput" if stage == 2 else "Internal"
    ydram = nc.dram_tensor("ydram", [16, 128, S], BF16, kind=ykind).ap()
    lnbdram = nc.dram_tensor("lnbdram", [D, S], BF16, kind="Internal").ap()
    if stage == 2:
        dmp_ln = nc.dram_tensor("dmp_ln", [D, S], BF16,
                                kind="ExternalOutput").ap()
        dmp_x = nc.dram_tensor("dmp_x", [INNER, S], BF16,
                               kind="ExternalOutput").ap()
        dmp_bc = nc.dram_tensor("dmp_bc", [8 * 128, S], BF16,
                                kind="ExternalOutput").ap()
        dmp_dt = nc.dram_tensor("dmp_dt", [128, 128], F32,
                                kind="ExternalOutput").ap()
        dmp_yst = nc.dram_tensor("dmp_yst", [128, 32], F32,
                                 kind="ExternalOutput").ap()
        dmp_mrs = nc.dram_tensor("dmp_mrs", [128, 64], F32,
                                 kind="ExternalOutput").ap()
        dmp_grow = nc.dram_tensor("dmp_grow", [1, 64], F32,
                                  kind="ExternalOutput").ap()
        dmp_urg = nc.dram_tensor("dmp_urg", [128, 8192], BF16,
                                 kind="ExternalOutput").ap()

    sb = cx.sb
    for i in range(8):
        cx.sbuf(f"res{i}", (128, S), F32)
    for i in range(8):
        cx.sbuf(f"lnb{i}", (128, S), BF16)
    pool = cx.sbuf("pool", (128, 16 * S), BF16)
    pbf = pool.ap()
    pf32 = pbf.bitcast(F32)
    for i in range(16):
        cx.view(f"xcv{i}", pbf[:, i * S:(i + 1) * S])
    cx.view("lnscr", pf32[:, 0:1024])
    cx.view("ln_rb", pf32[:, 1024:2048])
    cx.view("ln_nb", pf32[:, 2048:3072])
    cx.view("stM", pf32[0:1, 3072:5120])
    cx.view("stV", pf32[0:1, 5120:7168])
    cx.view("stR", pf32[0:1, 7168:9216])
    cx.view("yrl", pbf[:, 0:8192])
    cx.view("urg", pbf[:, 8192:16384])
    cx.view("zt", pbf[:, 16384:16896])
    cx.sbuf("wt0", (128, INNER), BF16)
    cx.sbuf("wt1", (128, INNER), BF16)
    cx.sbuf("cwd", (128, 4 * 128), BF16)
    cx.sbuf("cbcol", (128, 16), F32)
    cx.sbuf("btm", (128, 512), BF16)
    cx.sbuf("xtm", (128, S), BF16)
    cx.sbuf("ycs", (128, S), BF16)
    cx.sbuf("dtp", (128, 8), F32)
    cx.sbuf("csT", (128, 8), F32)
    cx.sbuf("nacs", (128, 8), F32)
    cx.sbuf("acsrow", (1, 128), F32)
    cx.sbuf("erow", (1, 128), F32)
    cx.sbuf("arow", (1, 8), F32)
    cx.sbuf("arowp", (1, 8), F32)
    cx.sbuf("AVB2", (128, 8), F32)
    cx.sbuf("DB", (128, 8), F32)
    cx.sbuf("wcol", (128, 2), F32)
    cx.sbuf("atc", (128, 1), F32)
    cx.sbuf("SST", (128, 4 * HD), F32)
    cx.sbuf("sstt", (128, HD), F32)
    cx.sbuf("et", (128, 128), F32)
    cx.sbuf("etm", (128, 128), F32)
    cx.sbuf("mtb", (128, 128), BF16)
    cx.sbuf("cts", (128, 128), F32)
    cx.sbuf("cbxb", (128, 128), F32)
    cx.sbuf("bsbE", (128, 128), BF16)
    cx.sbuf("bsbO", (128, 128), BF16)
    ps01 = cx.psum("PS01", (128, 1024))
    ps23 = cx.psum("PS23", (128, 1024))
    ps45 = cx.psum("PS45", (128, 1024))
    ps67 = cx.psum("PS67", (128, 1024))
    cx.load_consts(consts, ["ones_col_f32", "ones_row_f32", "ident_bf",
                            "ident_f32", "lt_incl_f32", "trimask_u8",
                            "zero_f32_128"])

    res_names = [f"res{i}" for i in range(8)]
    lnb_names = [f"lnb{i}" for i in range(8)]
    if not merged:
        for i in range(8):
            p.emit("q", (lambda i=i: lambda e: e.dma_start(
                out=sb[f"res{i}"][:], in_=xin[i * 128:(i + 1) * 128, :]))(),
                w=[f"res{i}"], dma=True, tag="ldx")
    p.emit("q", lambda e: e.dma_start(out=sb["arow"][:], in_=la_in),
           w=["arow"], dma=True)
    p.emit("q", lambda e: e.dma_start(out=sb["cbcol"][:], in_=cb_in),
           w=["cbcol"], dma=True)
    p.emit("s", lambda e: e.activation(sb["arow"][:], sb["arow"][:], AF.Exp,
                                       scale=1.0), r=["arow"], w=["arow"])
    p.emit("v", lambda e: e.tensor_scalar_mul(sb["arowp"][:], sb["arow"][:], 1.0),
           r=["arow"], w=["arowp"])
    p.emit("v", lambda e: e.tensor_scalar_mul(sb["arow"][:], sb["arow"][:], -1.0),
           r=["arow"], w=["arow"])
    p.emit("t", lambda e: e.matmul(ps45[:, 0:8],
                                   cx.const_ap["ones_row_f32"][0:1, :],
                                   sb["arowp"][:], start=True, stop=True),
           r=["arowp", "c_ones_row_f32"], w=["PS45"])
    p.emit("v", lambda e: e.tensor_copy(sb["AVB2"][:], ps45[:, 0:8]),
           r=["PS45"], w=["AVB2"])
    p.emit("g", lambda e: e.dma_start(out=sb["grow"][0:1, 0:8], in_=dp_in),
           w=["grow"], dma=True)
    p.emit("t", lambda e: e.matmul(ps45[:, 8:16],
                                   cx.const_ap["ones_row_f32"][0:1, :],
                                   sb["grow"][0:1, 0:8], start=True, stop=True),
           r=["grow", "c_ones_row_f32"], w=["PS45"])
    p.emit("v", lambda e: e.tensor_copy(sb["DB"][:], ps45[:, 8:16]),
           r=["PS45"], w=["DB"])
    p.emit("g", lambda e: e.dma_start(out=sb["grow"][0:1, 8:16], in_=bdt_in),
           w=["grow"], dma=True)

    build_layernorm2(cx, res_names, lnb_names, tag="ln2")
    for i in range(8):
        p.emit("q", (lambda i=i: lambda e: e.dma_start(
            out=lnbdram[i * 128:(i + 1) * 128, :], in_=sb[f"lnb{i}"][:]))(),
            r=[f"lnb{i}"], w=["lnbd"], dma=True, tag="lnsp")
    if stage == 2:
        for i in range(8):
            p.emit("q", (lambda i=i: lambda e: e.dma_start(
                out=dmp_ln[i * 128:(i + 1) * 128, :], in_=sb[f"lnb{i}"][:]))(),
                r=[f"lnb{i}"], w=["dmp"], dma=True, tag="dln")

    cx.sbuf("dtfull", (128, 16 * 8), F32)
    cx.sbuf("BDTB", (128, 8), F32)
    cx.sbuf("yst", (128, 32), F32)
    cx.sbuf("MRS", (128, 64), F32)
    cx.sbuf("grow", (1, 64), F32)
    cx.sbuf("wdts", (128, 16 * 8), BF16)
    for m in range(16):
        wtn = f"wt{m % 2}"
        p.emit("g", (lambda m=m, wtn=wtn: lambda e: e.dma_start(
            out=sb[wtn][:, 0:1024], in_=wx_in[m]))(), r=[wtn], w=[wtn],
            dma=True, tag="lwx")
        for ch in range(4):
            psx = ps45 if ch % 2 == 0 else ps67
            pk = "PS45" if ch % 2 == 0 else "PS67"
            for kt in range(8):
                p.emit("t", (lambda ch=ch, kt=kt, psx=psx, wtn=wtn: lambda e:
                             e.matmul(psx[:, (ch // 2) * 512:(ch // 2) * 512 + 512],
                                      sb[wtn][:, kt * 128:kt * 128 + 128],
                                      sb[f"lnb{kt}"][:, ch * 512:ch * 512 + 512],
                                      start=(kt == 0), stop=(kt == 7)))(),
                       r=[wtn, f"lnb{kt}"], w=[pk], tag="xproj")
            p.emit("s", (lambda m=m, ch=ch, psx=psx: lambda e: e.activation(
                sb[f"xcv{m}"][:, ch * 512:ch * 512 + 512],
                psx[:, (ch // 2) * 512:(ch // 2) * 512 + 512], AF.Copy))(),
                r=[pk], w=[f"xcv{m}"], tag="xevac")
        p.emit("q", (lambda m=m: lambda e: e.dma_start(
            out=sb["dtp"][:, 0:4], in_=cw_in[m]))(),
            r=["dtp"], w=["dtp"], dma=True, tag="lcw")
        for k in range(4):
            p.emit("v", (lambda k=k: lambda e: e.tensor_scalar(
                sb["et"][:], cx.const_ap["ident_f32"][:], sb["dtp"][:, k:k + 1],
                None, ALU.mult))(), r=["dtp", "c_ident_f32"], w=["et"])
            p.emit("v", (lambda k=k: lambda e: e.tensor_copy(
                sb["cwd"][:, k * 128:k * 128 + 128], sb["et"][:]))(),
                r=["et"], w=["cwd"])
        for ch in range(4):
            psx = ps01 if ch < 2 else ps23
            pk = "PS01" if ch < 2 else "PS23"
            reg = (ch % 2) * 512
            for k in (3, 2, 1, 0):
                sh = 3 - k
                src0 = ch * 512 - sh
                lo = max(0, src0)
                width = 512 - (lo - src0)
                o0 = reg + (lo - src0)
                p.emit("t", (lambda k=k, psx=psx, lo=lo, width=width, o0=o0,
                             m=m: lambda e: e.matmul(
                                 psx[:, o0:o0 + width],
                                 sb["cwd"][:, k * 128:k * 128 + 128],
                                 sb[f"xcv{m}"][:, lo:lo + width],
                                 start=(k == 3), stop=(k == 0)))(),
                       r=["cwd", f"xcv{m}"], w=[pk], tag="conv")
        for ch in range(4):
            psx = ps01 if ch < 2 else ps23
            pk = "PS01" if ch < 2 else "PS23"
            reg = (ch % 2) * 512
            p.emit("s", (lambda m=m, ch=ch, psx=psx, reg=reg: lambda e:
                         e.activation(
                             sb[f"xcv{m}"][:, ch * 512:ch * 512 + 512],
                             psx[:, reg:reg + 512], AF.Silu,
                             bias=sb["cbcol"][:, m:m + 1]))(),
                   r=[pk, "cbcol"], w=[f"xcv{m}"], tag="csilu")

    for which, w_in, dst0 in (("b", wb_in, 0), ("c", wc_in, 4)):
        for sl in range(4):
            wtn = f"wt{sl % 2}"
            p.emit("g", (lambda w_in=w_in, sl=sl, wtn=wtn: lambda e: e.dma_start(
                out=sb[wtn][:], in_=w_in[sl]))(), r=[wtn], w=[wtn],
                dma=True, tag="lwb")
            for ch in range(4):
                psx = ps45 if ch % 2 == 0 else ps67
                pk = "PS45" if ch % 2 == 0 else "PS67"
                for kt in range(16):
                    p.emit("t", (lambda ch=ch, kt=kt, psx=psx, wtn=wtn: lambda e:
                                 e.matmul(
                                     psx[:, (ch // 2) * 512:(ch // 2) * 512 + 512],
                                     sb[wtn][:, kt * 128:kt * 128 + 128],
                                     sb[f"xcv{kt}"][:, ch * 512:ch * 512 + 512],
                                     start=(kt == 0), stop=(kt == 15)))(),
                           r=[wtn, f"xcv{kt}"], w=[pk], tag="bcproj")
                p.emit("s", (lambda sl=sl, ch=ch, psx=psx, dst0=dst0: lambda e:
                             e.activation(
                                 sb[f"lnb{dst0 + sl}"][:, ch * 512:ch * 512 + 512],
                                 psx[:, (ch // 2) * 512:(ch // 2) * 512 + 512],
                                 AF.Copy))(),
                       r=[pk], w=[f"lnb{dst0 + sl}"], tag="bcevac")

    p.emit("g", lambda e: e.dma_start(out=sb["wdts"][:], in_=wdt_in),
           w=["wdts"], dma=True, tag="lwdt")
    p.emit("t", lambda e: e.matmul(ps45[:, 16:24],
                                   cx.const_ap["ones_row_f32"][0:1, :],
                                   sb["grow"][0:1, 8:16], start=True, stop=True),
           r=["grow", "c_ones_row_f32"], w=["PS45"])
    p.emit("v", lambda e: e.tensor_copy(sb["BDTB"][:], ps45[:, 16:24]),
           r=["PS45"], w=["BDTB"])
    for tt in range(16):
        for kt in range(16):
            p.emit("t", (lambda tt=tt, kt=kt: lambda e: e.matmul(
                ps67[:, 0:8], sb[f"xcv{kt}"][:, tt * 128:tt * 128 + 128],
                sb["wdts"][:, kt * 8:kt * 8 + 8],
                start=(kt == 0), stop=(kt == 15)))(),
                r=[f"xcv{kt}", "wdts"], w=["PS67"], tag="dtmm")
        p.emit("v", (lambda tt=tt: lambda e: e.tensor_add(
            sb["dtp"][:], ps67[:, 0:8], sb["BDTB"][:]))(),
            r=["PS67", "BDTB"], w=["dtp"], tag="dtadd")
        p.emit("s", (lambda: lambda e: e.activation(
            sb["csT"][:], sb["dtp"][:], AF.Abs))(),
            r=["dtp"], w=["csT"])
        p.emit("s", (lambda: lambda e: e.activation(
            sb["csT"][:], sb["csT"][:], AF.Exp, scale=-1.0))(),
            r=["csT"], w=["csT"])
        p.emit("v", (lambda: lambda e: e.tensor_scalar_add(
            sb["csT"][:], sb["csT"][:], 1.0))(), r=["csT"], w=["csT"])
        p.emit("s", (lambda: lambda e: e.activation(
            sb["csT"][:], sb["csT"][:], AF.Ln))(), r=["csT"], w=["csT"])
        p.emit("v", (lambda: lambda e: e.tensor_scalar_max(
            sb["dtp"][:], sb["dtp"][:], 0.0))(), r=["dtp"], w=["dtp"])
        p.emit("v", (lambda tt=tt: lambda e: e.tensor_add(
            sb["dtfull"][:, tt * 8:tt * 8 + 8], sb["dtp"][:], sb["csT"][:]))(),
            r=["dtp", "csT"], w=["dtfull"], tag="dtsp")

    if stage == 2:
        for i in range(16):
            p.emit("q", (lambda i=i: lambda e: e.dma_start(
                out=dmp_x[i * 128:(i + 1) * 128, :], in_=sb[f"xcv{i}"][:]))(),
                r=[f"xcv{i}"], w=["dmp"], dma=True, tag="dxc")
        for i in range(8):
            p.emit("q", (lambda i=i: lambda e: e.dma_start(
                out=dmp_bc[i * 128:(i + 1) * 128, :], in_=sb[f"lnb{i}"][:]))(),
                r=[f"lnb{i}"], w=["dmp"], dma=True, tag="dbc")
        p.emit("q", lambda e: e.dma_start(out=dmp_dt, in_=sb["dtfull"][:]),
               r=["dtfull"], w=["dmp"], dma=True, tag="ddt")
    p.emit("v", lambda e: e.memset(sb["SST"][:], 0.0), w=["SST"])
    p.emit("v", lambda e: e.memset(sb["yst"][:], 0.0), w=["yst"])
    p.emit("v", lambda e: e.memset(sb["bsbE"][:], 0.0), w=["bsbE"])
    p.emit("v", lambda e: e.memset(sb["bsbO"][:], 0.0), w=["bsbO"])
    for c in range(16):
        dtc = sb["dtfull"].ap()[:, c * 8:c * 8 + 8]
        p.emit("t", (lambda dtc=dtc: lambda e: e.matmul(
            ps45[:, 24:32], cx.const_ap["lt_incl_f32"][:], dtc,
            start=True, stop=True))(), r=["dtfull", "c_lt_incl_f32"],
            w=["PS45"], tag="csmm")
        p.emit("v", lambda e: e.tensor_copy(sb["csT"][:], ps45[:, 24:32]),
               r=["PS45"], w=["csT"])
        p.emit("v", lambda e: e.tensor_mul(sb["nacs"][:], sb["csT"][:],
                                           sb["AVB2"][:]),
               r=["csT", "AVB2"], w=["nacs"])
        psb23 = ps23.ap().bitcast(BF16)
        for it in range(16):
            p.emit("t", (lambda it=it, c=c, psb23=psb23: lambda e: e.transpose(
                psb23[:, it * 128:it * 128 + 128],
                sb[f"xcv{it}"][:, c * 128:c * 128 + 128],
                cx.const_ap["ident_bf"][:]))(),
                r=[f"xcv{it}", "c_ident_bf"], w=["PS23"], tag="xtmtr")
        p.emit("v", (lambda psb23=psb23: lambda e: e.tensor_copy(
            sb["xtm"][:], psb23[:, 0:2048]))(), r=["PS23"], w=["xtm"])
        psb01 = ps01.ap().bitcast(BF16)
        for sl in range(4):
            p.emit("t", (lambda sl=sl, c=c, psb01=psb01: lambda e: e.transpose(
                psb01[:, sl * 128:sl * 128 + 128],
                sb[f"lnb{sl}"][:, c * 128:c * 128 + 128],
                cx.const_ap["ident_bf"][:]))(),
                r=[f"lnb{sl}", "c_ident_bf"], w=["PS01"], tag="btmtr")
        p.emit("v", (lambda psb01=psb01: lambda e: e.tensor_copy(
            sb["btm"][:], psb01[:, 0:512]))(), r=["PS01"], w=["btm"])
        for h in range(8):
            sl = h // 2
            b0 = (h % 2) * 64
            p.emit("t", (lambda h=h, dtc=dtc: lambda e: e.matmul(
                ps67[0:1, 512:640], dtc[:, h:h + 1],
                cx.const_ap["lt_incl_f32"][:], start=True, stop=True))(),
                r=["dtfull", "c_lt_incl_f32"], w=["PS67"], tag="csr")
            p.emit("v", (lambda h=h: lambda e: e.tensor_scalar(
                sb["acsrow"][:], ps67[0:1, 512:640],
                sb["arow"][0:1, h:h + 1], None, ALU.mult))(),
                r=["PS67", "arow"], w=["acsrow"])
            p.emit("s", (lambda: lambda e: e.activation(
                sb["erow"][:], sb["acsrow"][:], AF.Exp))(),
                r=["acsrow"], w=["erow"])
            p.emit("t", (lambda sl=sl, b0=b0, c=c: lambda e: e.matmul(
                ps45[:, 0:128], sb[f"lnb{sl}"][b0:b0 + 64, c * 128:c * 128 + 128],
                sb[f"lnb{4 + sl}"][b0:b0 + 64, c * 128:c * 128 + 128],
                start=True, stop=True))(),
                r=[f"lnb{sl}", f"lnb{4 + sl}"], w=["PS45"], tag="gmm")
            p.emit("t", (lambda h=h: lambda e: e.matmul(
                ps45[:, 512:640], cx.const_ap["ones_row_f32"][0:1, :],
                sb["acsrow"][0:1, 0:128],
                start=True, stop=True))(),
                r=["acsrow", "c_ones_row_f32"], w=["PS45"], tag="emm")
            p.emit("s", (lambda h=h: lambda e: e.activation(
                sb["et"][:], ps45[:, 512:640], AF.Exp,
                bias=sb["nacs"][:, h:h + 1]))(),
                r=["PS45", "nacs"], w=["et"], tag="eexp")
            p.emit("v", (lambda: lambda e: e.select(
                sb["etm"][:], cx.const_ap["trimask_u8"][:], sb["et"][:],
                cx.const_ap["zero_f32_128"][:]))(),
                r=["et", "c_trimask_u8", "c_zero_f32_128"], w=["etm"])
            p.emit("v", (lambda: lambda e: e.tensor_mul(
                sb["et"][:], ps45[:, 0:128], sb["etm"][:]))(),
                r=["PS45", "etm"], w=["et"])
            p.emit("v", (lambda h=h, c=c: lambda e: e.tensor_scalar(
                sb["mtb"][:], sb["et"][:],
                sb["dtfull"][:, c * 8 + h:c * 8 + h + 1],
                None, ALU.mult))(), r=["et", "dtfull"], w=["mtb"], tag="mts")
            p.emit("t", (lambda h=h: lambda e: e.matmul(
                ps67[:, 512:640], cx.const_ap["ones_row_f32"][0:1, 0:128],
                sb["erow"][0:1, 0:128],
                start=True, stop=True))(),
                r=["erow", "c_ones_row_f32"], w=["PS67"], tag="cbc")
            p.emit("v", (lambda: lambda e: e.tensor_copy(
                sb["cbxb"][:], ps67[:, 512:640]))(), r=["PS67"], w=["cbxb"])
            p.emit("v", (lambda sl=sl, b0=b0, c=c: lambda e: e.tensor_mul(
                sb["cts"][b0:b0 + 64, :],
                sb[f"lnb{4 + sl}"][b0:b0 + 64, c * 128:c * 128 + 128],
                sb["cbxb"][b0:b0 + 64, :]))(),
                r=[f"lnb{4 + sl}", "cbxb"], w=["cts"], tag="ctsm")
            for half in range(2):
                reg = 0 if half == 0 else 512
                b0h = (h % 2) * 64
                c0h = (h // 2) * 256
                p.emit("t", (lambda b0h=b0h, c0h=c0h, reg=reg, half=half:
                             lambda e: e.matmul(
                    ps67[:, reg:reg + 128],
                    sb["SST"][b0h:b0h + 64, c0h + half * 128:
                              c0h + half * 128 + 128],
                    sb["cts"][b0h:b0h + 64, :], start=True, stop=False))(),
                    r=["SST", "cts"], w=["PS67"], tag="yinter")
                p.emit("t", (lambda h=h, reg=reg, half=half: lambda e: e.matmul(
                    ps67[:, reg:reg + 128],
                    sb["xtm"][:, h * 256 + half * 128:h * 256 + half * 128 + 128],
                    sb["mtb"][:], start=False, stop=True))(),
                    r=["xtm", "mtb"], w=["PS67"], tag="yintra")
                it = 2 * h + half
                p.emit("v", (lambda it=it, c=c, h=h, reg=reg: lambda e:
                             e.scalar_tensor_tensor(
                                 sb["ycs"][:, it * 128:it * 128 + 128],
                                 sb[f"xcv{it}"][:, c * 128:c * 128 + 128],
                                 sb["DB"][:, h:h + 1], ps67[:, reg:reg + 128],
                                 ALU.mult, ALU.add))(),
                       r=[f"xcv{it}", "DB", "PS67"], w=["ycs"], tag="yev")
            p.emit("t", (lambda h=h: lambda e: e.matmul(
                ps45[:, 128:129], cx.const_ap["ones_row_f32"][0:1, :],
                sb["acsrow"][0:1, 127:128],
                start=True, stop=True))(),
                r=["acsrow", "c_ones_row_f32"], w=["PS45"], tag="atlb")
            p.emit("v", (lambda h=h: lambda e: e.tensor_add(
                sb["wcol"][:, 0:1], ps45[:, 128:129], sb["nacs"][:, h:h + 1]))(),
                r=["PS45", "nacs"], w=["wcol"])
            p.emit("s", (lambda: lambda e: e.activation(
                sb["wcol"][:, 0:1], sb["wcol"][:, 0:1], AF.Exp))(),
                r=["wcol"], w=["wcol"])
            p.emit("v", (lambda h=h, c=c: lambda e: e.tensor_mul(
                sb["wcol"][:, 1:2], sb["wcol"][:, 0:1],
                sb["dtfull"][:, c * 8 + h:c * 8 + h + 1]))(),
                r=["wcol", "dtfull"], w=["wcol"])
            p.emit("t", (lambda h=h: lambda e: e.matmul(
                ps45[:, 128:129], cx.const_ap["ones_row_f32"][0:1, :],
                sb["erow"][0:1, 127:128],
                start=True, stop=True))(),
                r=["erow", "c_ones_row_f32"], w=["PS45"], tag="atb")
            p.emit("v", (lambda: lambda e: e.tensor_copy(
                sb["atc"][:], ps45[:, 128:129]))(), r=["PS45"], w=["atc"])
            b0 = (h % 2) * 64
            c0 = (h // 2) * 256
            bk = "bsbE" if h % 2 == 0 else "bsbO"
            p.emit("v", (lambda h=h, b0=b0, sl=sl, bk=bk: lambda e:
                         e.tensor_scalar(
                sb[bk][:, b0:b0 + 64],
                sb["btm"][:, sl * 128 + b0:sl * 128 + b0 + 64],
                sb["wcol"][:, 1:2], None, ALU.mult))(),
                r=["btm", "wcol"], w=[bk], tag="bscale")
            p.emit("t", (lambda h=h, bk=bk: lambda e: e.matmul(
                ps45[:, 256:512], sb[bk][:],
                sb["xtm"][:, h * 256:h * 256 + 256], start=True, stop=True))(),
                r=[bk, "xtm"], w=["PS45"], tag="dsmm")
            p.emit("s", (lambda b0=b0, c0=c0: lambda e: e.activation(
                sb["sstt"][b0:b0 + 64, :], sb["SST"][b0:b0 + 64, c0:c0 + 256],
                AF.Identity, scale=sb["atc"][b0:b0 + 64, 0:1]))(),
                r=["SST", "atc"], w=["sstt"], tag="sdec")
            p.emit("v", (lambda b0=b0, c0=c0: lambda e: e.tensor_add(
                sb["SST"][b0:b0 + 64, c0:c0 + 256], sb["sstt"][b0:b0 + 64, :],
                ps45[b0:b0 + 64, 256:512]))(),
                r=["sstt", "PS45"], w=["SST"], tag="supd")
        p.emit("q", (lambda c=c: lambda e: e.dma_start(
            out=ydram[c], in_=sb["ycs"][:]))(),
            r=["ycs"], w=["ydram_d"], dma=True, tag="ysp")
        p.emit("s", (lambda: lambda e: e.activation(
            sb["xtm"][:], sb["ycs"][:], AF.Square))(), r=["ycs", "xtm"],
            w=["xtm"])
        p.emit("v", (lambda: lambda e: e.reduce_sum(
            sb["MRS"][:, 0:16],
            sb["ycs"].ap().rearrange("p (it t) -> p it t", t=128), axis=AX.X))(),
            r=["ycs"], w=["MRS"])
        p.emit("v", (lambda: lambda e: e.tensor_add(
            sb["yst"][:, 0:16], sb["yst"][:, 0:16], sb["MRS"][:, 0:16]))(),
            r=["MRS", "yst"], w=["yst"])
        p.emit("v", (lambda: lambda e: e.reduce_sum(
            sb["MRS"][:, 0:16],
            sb["xtm"].ap().rearrange("p (it t) -> p it t", t=128), axis=AX.X))(),
            r=["xtm"], w=["MRS"])
        p.emit("v", (lambda: lambda e: e.tensor_add(
            sb["yst"][:, 16:32], sb["yst"][:, 16:32], sb["MRS"][:, 0:16]))(),
            r=["MRS", "yst"], w=["yst"])

    p.emit("t", lambda e: e.matmul(ps45[0:1, 0:32],
                                   cx.const_ap["ones_col_f32"][:],
                                   sb["yst"][:], start=True, stop=True),
           r=["yst", "c_ones_col_f32"], w=["PS45"], tag="gsum")
    p.emit("v", lambda e: e.tensor_copy(sb["grow"][0:1, 0:32], ps45[0:1, 0:32]),
           r=["PS45"], w=["grow"])
    p.emit("v", lambda e: e.tensor_add(
        sb["grow"][0:1, 32:40],
        sb["grow"].ap()[0:1, 0:16][0:1, 0::2],
        sb["grow"].ap()[0:1, 0:16][0:1, 1::2]), r=["grow"], w=["grow"])
    p.emit("v", lambda e: e.tensor_add(
        sb["grow"][0:1, 40:48],
        sb["grow"].ap()[0:1, 16:32][0:1, 0::2],
        sb["grow"].ap()[0:1, 16:32][0:1, 1::2]), r=["grow"], w=["grow"])
    NEL = float(S * HD)
    p.emit("v", lambda e: e.tensor_scalar_mul(sb["grow"][0:1, 32:40],
                                              sb["grow"][0:1, 32:40], 1.0 / NEL),
           r=["grow"], w=["grow"])
    p.emit("v", lambda e: e.tensor_scalar_mul(sb["grow"][0:1, 40:48],
                                              sb["grow"][0:1, 40:48], 1.0 / NEL),
           r=["grow"], w=["grow"])
    p.emit("v", lambda e: e.tensor_mul(sb["grow"][0:1, 48:56],
                                       sb["grow"][0:1, 32:40],
                                       sb["grow"][0:1, 32:40]),
           r=["grow"], w=["grow"])
    p.emit("v", lambda e: e.tensor_sub(sb["grow"][0:1, 40:48],
                                       sb["grow"][0:1, 40:48],
                                       sb["grow"][0:1, 48:56]),
           r=["grow"], w=["grow"])
    p.emit("v", lambda e: e.tensor_scalar_add(sb["grow"][0:1, 40:48],
                                              sb["grow"][0:1, 40:48], 1e-5),
           r=["grow"], w=["grow"])
    p.emit("s", lambda e: e.activation(sb["grow"][0:1, 40:48],
                                       sb["grow"][0:1, 40:48], AF.Ln),
           r=["grow"], w=["grow"])
    p.emit("s", lambda e: e.activation(sb["grow"][0:1, 40:48],
                                       sb["grow"][0:1, 40:48], AF.Exp,
                                       scale=-0.5), r=["grow"], w=["grow"])
    for src, dst in ((32, 0), (40, 16)):
        p.emit("v", (lambda src=src: lambda e: e.tensor_copy(
            sb["erow"].ap()[0:1, 0:16][0:1, 0::2],
            sb["grow"][0:1, src:src + 8]))(), r=["grow"], w=["erow"])
        p.emit("v", (lambda src=src: lambda e: e.tensor_copy(
            sb["erow"].ap()[0:1, 0:16][0:1, 1::2],
            sb["grow"][0:1, src:src + 8]))(), r=["grow"], w=["erow"])
        p.emit("t", (lambda dst=dst: lambda e: e.matmul(
            ps45[:, 32:48], cx.const_ap["ones_row_f32"][0:1, :],
            sb["erow"][0:1, 0:16], start=True, stop=True))(),
            r=["erow", "c_ones_row_f32"], w=["PS45"])
        p.emit("v", (lambda dst=dst: lambda e: e.tensor_copy(
            sb["MRS"][:, dst:dst + 16], ps45[:, 32:48]))(),
            r=["PS45"], w=["MRS"])
    p.emit("q", lambda e: e.dma_start(out=sb["MRS"][:, 32:48], in_=gw_in),
           r=["MRS"], w=["MRS"], dma=True)
    p.emit("q", lambda e: e.dma_start(out=sb["MRS"][:, 48:64], in_=gb_in),
           r=["MRS"], w=["MRS"], dma=True)
    p.emit("v", lambda e: e.tensor_mul(sb["MRS"][:, 16:32], sb["MRS"][:, 16:32],
                                       sb["MRS"][:, 32:48]), r=["MRS"], w=["MRS"])
    p.emit("v", lambda e: e.tensor_mul(sb["MRS"][:, 0:16], sb["MRS"][:, 0:16],
                                       sb["MRS"][:, 16:32]), r=["MRS"], w=["MRS"])
    p.emit("v", lambda e: e.tensor_sub(sb["MRS"][:, 0:16], sb["MRS"][:, 48:64],
                                       sb["MRS"][:, 0:16]), r=["MRS"], w=["MRS"])

    if stage == 2:
        p.emit("q", lambda e: e.dma_start(out=dmp_yst, in_=sb["yst"][:]),
               r=["yst"], w=["dmp"], dma=True, tag="dyst")
        p.emit("q", lambda e: e.dma_start(out=dmp_mrs, in_=sb["MRS"][:]),
               r=["MRS"], w=["dmp"], dma=True, tag="dmrs")
        p.emit("q", lambda e: e.dma_start(out=dmp_grow, in_=sb["grow"][:]),
               r=["grow"], w=["dmp"], dma=True, tag="dgrow")
    for g in range(4):
        t0 = g * 512
        for cc in range(4):
            for it in range(16):
                p.emit("q", (lambda g=g, cc=cc, it=it: lambda e: e.dma_start(
                    out=sb["yrl"][:, it * 512 + cc * 128:
                                  it * 512 + cc * 128 + 128],
                    in_=ydram[4 * g + cc][:, it * 128:it * 128 + 128]))(),
                    r=["yrl"], w=["yrl"], dma=True, tag="yrlld")
        for k in range(8):
            p.emit("q", (lambda k=k, t0=t0: lambda e: e.dma_start(
                out=sb[f"lnb{k}"][:, 0:512],
                in_=lnbdram[k * 128:(k + 1) * 128, t0:t0 + 512]))(),
                r=[f"lnb{k}"], w=[f"lnb{k}"], dma=True, tag="lnrld")
        for it in range(16):
            p.emit("s", (lambda it=it: lambda e: e.activation(
                sb["ycs"][:, 0:512], sb["yrl"][:, it * 512:it * 512 + 512],
                AF.Identity, bias=sb["MRS"][:, it:it + 1],
                scale=sb["MRS"][:, 16 + it:17 + it]))(),
                r=["yrl", "MRS"], w=["ycs"], tag="yn")
            wtn = f"wt{it % 2}"
            p.emit("g", (lambda it=it, wtn=wtn: lambda e: e.dma_start(
                out=sb[wtn][:, 0:1024], in_=wz_in[it]))(), r=[wtn], w=[wtn],
                dma=True, tag="lwz")
            for kt in range(8):
                p.emit("t", (lambda it=it, kt=kt, wtn=wtn: lambda e: e.matmul(
                    ps45[:, 0:512], sb[wtn][:, kt * 128:kt * 128 + 128],
                    sb[f"lnb{kt}"][:, 0:512],
                    start=(kt == 0), stop=(kt == 7)))(),
                    r=[wtn, f"lnb{kt}"], w=["PS45"], tag="zmm")
            p.emit("s", (lambda: lambda e: e.activation(
                sb["zt"][:], ps45[:, 0:512], AF.Silu))(),
                r=["PS45"], w=["zt"], tag="zsilu")
            p.emit("v", (lambda it=it: lambda e: e.tensor_mul(
                sb["urg"][:, it * 512:it * 512 + 512], sb["ycs"][:, 0:512],
                sb["zt"][:]))(), r=["ycs", "zt"], w=["urg"], tag="umul")
        if stage == 2 and g == 0:
            p.emit("q", (lambda: lambda e: e.dma_start(
                out=dmp_urg, in_=sb["urg"][:]))(),
                r=["urg"], w=["dmp"], dma=True, tag="durg")
        for m in range(8):
            wtn = f"wt{m % 2}"
            p.emit("g", (lambda m=m, wtn=wtn: lambda e: e.dma_start(
                out=sb[wtn][:], in_=wo_in[m]))(), r=[wtn], w=[wtn],
                dma=True, tag="lwo")
            for kt in range(16):
                p.emit("t", (lambda m=m, kt=kt, wtn=wtn: lambda e: e.matmul(
                    ps67[:, 0:512], sb[wtn][:, kt * 128:kt * 128 + 128],
                    sb["urg"][:, kt * 512:kt * 512 + 512],
                    start=(kt == 0), stop=(kt == 15)))(),
                    r=[wtn, "urg"], w=["PS67"], tag="womm")
            p.emit("v", (lambda m=m, t0=t0: lambda e: e.tensor_add(
                sb[f"res{m}"][:, t0:t0 + 512], sb[f"res{m}"][:, t0:t0 + 512],
                ps67[:, 0:512]))(), r=["PS67", f"res{m}"], w=[f"res{m}"],
                tag="woadd")
    if not merged:
        for i in range(8):
            p.emit("q", (lambda i=i: lambda e: e.dma_start(
                out=xout[i * 128:(i + 1) * 128, :], in_=sb[f"res{i}"][:]))(),
                r=[f"res{i}"], w=["xout_d"], dma=True, tag="xst")
        p.barrier(engs=("q",), on=("q", "g"))
        p.barrier(engs=("g",), on=("g",))
    return cx


def c2_host_inputs(inputs):
    T = lambda a: np.ascontiguousarray(np.asarray(a, np.float32).T)
    cwh = np.asarray(inputs["L1_attn_conv_w"], np.float32)[:, 0, :]
    cw = np.ascontiguousarray(cwh.reshape(16, 128, 4))
    cb = np.ascontiguousarray(
        np.asarray(inputs["L1_attn_conv_b"], np.float32).reshape(16, 128).T)
    gw = np.ascontiguousarray(
        np.asarray(inputs["L1_attn_gn_w"], np.float32).reshape(16, 128).T)
    gb = np.ascontiguousarray(
        np.asarray(inputs["L1_attn_gn_b"], np.float32).reshape(16, 128).T)
    def pk(w, nm):
        wT = T(w)
        kk, mm = wT.shape[0] // 128, wT.shape[1] // 128
        return np.ascontiguousarray(
            wT.reshape(kk, 128, mm, 128).transpose(2, 1, 0, 3)
            .reshape(mm, 128, kk * 128))
    wdtp = np.ascontiguousarray(
        T(inputs["L1_attn_Wdt"]).reshape(16, 128, 8).transpose(1, 0, 2)
        .reshape(128, 128))
    base = dict(
        wxt=pk(inputs["L1_attn_Wx"], "wx"), wzt=pk(inputs["L1_attn_Wz"], "wz"),
        cw=cw, cb=cb,
        wbt=pk(inputs["L1_attn_Wb"], "wb"), wct=pk(inputs["L1_attn_Wc"], "wc"),
        wdtt=wdtp,
        bdt=np.asarray(inputs["L1_attn_b_dt"], np.float32).reshape(1, 8),
        loga=np.asarray(inputs["L1_attn_log_a"], np.float32).reshape(1, 8),
        dparam=np.asarray(inputs["L1_attn_d_param"], np.float32).reshape(1, 8),
        gnw=gw, gnb=gb, woutt=pk(inputs["L1_attn_Wout"], "wo"))
    return base


def run_launch_c2(inputs, xmids):
    consts = host_constants()
    cx = build_launch_c2(consts)
    cx.p.seal(cx.es)
    cx.es.close()
    base = c2_host_inputs(inputs)
    in_maps = [dict(base, xin=xmids[c]) for c in range(NCORES)]
    return run_bass_kernel_spmd(cx.nc, in_maps, core_ids=list(range(NCORES)))



def build_launch_c3(consts, stage=99, cx=None, ext=None):
    merged = ext is not None
    if cx is None:
        cx = Ctx("C3")
    nc, p = cx.nc, cx.p
    if merged:
        xin = None
    else:
        xin = nc.dram_tensor("xin", [D, S], F32, kind="ExternalInput").ap()
    wq_in = nc.dram_tensor("wqp", [8, 128, 1024], F32, kind="ExternalInput").ap()
    wk_in = nc.dram_tensor("wkp", [8, 128, 1024], F32, kind="ExternalInput").ap()
    wv_in = nc.dram_tensor("wvp", [8, 128, 1024], F32, kind="ExternalInput").ap()
    wr_in = nc.dram_tensor("wrp", [128, 128], F32, kind="ExternalInput").ap()
    bq_in = nc.dram_tensor("bqc", [128, 8], F32, kind="ExternalInput").ap()
    bk_in = nc.dram_tensor("bkc", [128, 8], F32, kind="ExternalInput").ap()
    bv_in = nc.dram_tensor("bvc", [128, 8], F32, kind="ExternalInput").ap()
    br_in = nc.dram_tensor("brc", [16, 1], F32, kind="ExternalInput").ap()
    bo_in = nc.dram_tensor("boc", [128, 8], F32, kind="ExternalInput").ap()
    wo_in = nc.dram_tensor("wop", [2, 128, 1024], F32, kind="ExternalInput").ap()
    xout = nc.dram_tensor("xout", [D, S], F32, kind="ExternalOutput").ap()

    sb = cx.sb
    for i in range(8):
        cx.sbuf(f"res{i}", (128, S), F32)
    for i in range(8):
        cx.sbuf(f"lnb{i}", (128, S), BF16)
    lnpool = cx.sbuf("lnpool", (128, 6144), F32)
    pf = lnpool.ap()
    pb = pf.bitcast(BF16)
    cx.view("lnscr", pf[:, 0:2048])
    cx.view("ln_rb", pf[:, 2048:4096])
    cx.view("ln_nb", pf[:, 4096:6144])
    cx.view("vtm", pb[:, 0:4160])
    cx.view("otA", pb[:, 4160:6208])
    cx.view("otB", pb[:, 6208:8256])
    cx.view("usb", pb[:, 8256:8768])
    cx.view("otmp", pb[:, 8768:8832])
    cx.view("otm2", pb[0:64, 8832:8960])
    cx.sbuf("stM", (1, S), F32)
    cx.sbuf("stV", (1, S), F32)
    cx.sbuf("wball", (128, 8 * 1024), BF16)
    cx.sbuf("pis", (128, 8 * 256), BF16)
    cx.sbuf("WQE", (128, 8 * 256), BF16)
    cx.sbuf("WKE", (128, 8 * 256), BF16)
    cx.sbuf("WVE", (128, 8 * 256), BF16)
    cx.sbuf("qkt0", (128, S), BF16)
    cx.sbuf("qkt1", (128, S), BF16)
    cx.sbuf("kkt0", (128, S), BF16)
    cx.sbuf("kkt1", (128, S), BF16)
    cx.sbuf("vkt0", (128, S), BF16)
    cx.sbuf("vkt1", (128, S), BF16)
    cx.sbuf("wosb", (128, 2 * 1024), BF16)
    cx.sbuf("wrsb", (128, 128), BF16)
    cx.sbuf("bqcol", (128, 8), BF16)
    cx.sbuf("bkcol", (128, 8), BF16)
    cx.sbuf("bvcol", (128, 8), BF16)
    cx.sbuf("bocol", (128, 8), F32)
    cx.sbuf("beQ", (128, 2), F32)
    cx.sbuf("beK", (128, 2), F32)
    cx.sbuf("beV", (128, 2), F32)
    cx.sbuf("racc", (16, 1), F32)
    cx.sbuf("rtmp", (16, 1), F32)
    cx.sbuf("brcol", (16, 1), F32)
    cx.sbuf("rlcol", (16, 1), F32)
    cx.sbuf("rrow", (1, 16), F32)
    cx.sbuf("drow", (1, 16), F32)
    cx.sbuf("cur", (1, 16), F32)
    cx.sbuf("t16a", (1, 16), F32)
    cx.sbuf("t16b", (1, 16), F32)
    cx.sbuf("isbuf", (1, 64), F32)
    cx.sbuf("evs", (1, 8), F32)
    cx.sbuf("grow4", (1, 8), F32)
    cx.sbuf("mrow", (1, 2), F32)
    cx.sbuf("Erow", (1, 64), F32)
    cx.sbuf("ecols", (128, 8), F32)
    cx.sbuf("hg4", (128, 4), F32)
    cx.sbuf("rc", (128, 1), F32)
    ps01 = cx.psum("PS01", (128, 1024))
    ps23 = cx.psum("PS23", (128, 1024))
    ps45 = cx.psum("PS45", (128, 1024))
    ps67 = cx.psum("PS67", (128, 1024))
    cx.load_consts(consts, ["ones_col_f32", "ones_row_f32", "ident_bf",
                            "ident_f32", "lo_mask4_u8", "i64pat_f32"])
    res_names = [f"res{i}" for i in range(8)]
    lnb_names = [f"lnb{i}" for i in range(8)]

    if not merged:
        for i in range(8):
            p.emit("q", (lambda i=i: lambda e: e.dma_start(
                out=sb[f"res{i}"][:], in_=xin[i * 128:(i + 1) * 128, :]))(),
                w=[f"res{i}"], dma=True, tag="ldx")
    p.emit("g", lambda e: e.dma_start(out=sb["wrsb"][:], in_=wr_in),
           w=["wrsb"], dma=True)
    for nm, src_ap in (("bqcol", bq_in), ("bkcol", bk_in), ("bvcol", bv_in)):
        p.emit("g", (lambda nm=nm, src_ap=src_ap: lambda e: e.dma_start(
            out=sb[nm][:], in_=src_ap))(), w=[nm], dma=True)
    for nm, src_ap in (("bocol", bo_in), ("brcol", br_in)):
        p.emit("q", (lambda nm=nm, src_ap=src_ap: lambda e: e.dma_start(
            out=sb[nm][:], in_=src_ap))(), w=[nm], dma=True)
    for kt in range(2):
        p.emit("g", (lambda kt=kt: lambda e: e.dma_start(
            out=sb["wosb"][:, kt * 1024:(kt + 1) * 1024], in_=wo_in[kt]))(),
            r=["wosb"], w=["wosb"], dma=True, tag="lwo3")

    build_layernorm(cx, res_names, lnb_names, tag="ln3")

    p.emit("v", lambda e: e.memset(sb["racc"][:], 0.0), w=["racc"])
    for ch in range(4):
        for kt in range(8):
            p.emit("t", (lambda ch=ch, kt=kt: lambda e: e.matmul(
                ps45[0:16, 0:512], sb["wrsb"][:, kt * 16:kt * 16 + 16],
                sb[f"lnb{kt}"][:, ch * 512:ch * 512 + 512],
                start=(kt == 0), stop=(kt == 7)))(),
                r=["wrsb", f"lnb{kt}"], w=["PS45"], tag="rmm")
        p.emit("v", (lambda: lambda e: e.reduce_sum(
            sb["rtmp"][:], ps45[0:16, 0:512], axis=AX.X))(),
            r=["PS45"], w=["rtmp"])
        p.emit("v", (lambda: lambda e: e.tensor_add(
            sb["racc"][:], sb["racc"][:], sb["rtmp"][:]))(),
            r=["rtmp", "racc"], w=["racc"])
    p.emit("v", lambda e: e.tensor_scalar_mul(sb["rlcol"][:], sb["racc"][:],
                                              1.0 / S), r=["racc"], w=["rlcol"])
    p.emit("v", lambda e: e.tensor_add(sb["rlcol"][:], sb["rlcol"][:],
                                       sb["brcol"][:]),
           r=["rlcol", "brcol"], w=["rlcol"])
    p.emit("t", lambda e: e.matmul(ps45[0:1, 512:528], sb["rlcol"][:],
                                   cx.const_ap["ident_f32"][0:16, 0:16],
                                   start=True, stop=True),
           r=["rlcol", "c_ident_f32"], w=["PS45"], tag="rtr")
    p.emit("v", lambda e: e.tensor_copy(sb["rrow"][:], ps45[0:1, 512:528]),
           r=["PS45"], w=["rrow"])
    p.emit("v", lambda e: e.reduce_max(sb["mrow"][0:1, 0:1], sb["rrow"][:],
                                       axis=AX.X), r=["rrow"], w=["mrow"])
    p.emit("v", lambda e: e.tensor_scalar_mul(sb["mrow"][0:1, 1:2],
                                              sb["mrow"][0:1, 0:1], -1.0),
           r=["mrow"], w=["mrow"])
    p.emit("s", lambda e: e.activation(sb["drow"][:], sb["rrow"][:], AF.Exp,
                                       bias=sb["mrow"][0:1, 1:2]),
           r=["rrow", "mrow"], w=["drow"])
    p.emit("v", lambda e: e.reduce_sum(sb["mrow"][0:1, 0:1], sb["drow"][:],
                                       axis=AX.X), r=["drow"], w=["mrow"])
    p.emit("s", lambda e: e.activation(sb["mrow"][0:1, 0:1],
                                       sb["mrow"][0:1, 0:1], AF.Ln),
           r=["mrow"], w=["mrow"])
    p.emit("s", lambda e: e.activation(sb["mrow"][0:1, 0:1],
                                       sb["mrow"][0:1, 0:1], AF.Exp,
                                       scale=-1.0), r=["mrow"], w=["mrow"])
    p.emit("v", lambda e: e.tensor_scalar(sb["drow"][:], sb["drow"][:],
                                          sb["mrow"][0:1, 0:1], None, ALU.mult),
           r=["drow", "mrow"], w=["drow"])
    p.emit("v", lambda e: e.tensor_copy(sb["cur"][:], sb["drow"][:]),
           r=["drow"], w=["cur"])
    for k in range(4):
        p.emit("v", (lambda k=k: lambda e: e.reduce_max(
            sb["evs"][0:1, k:k + 1], sb["cur"][:], axis=AX.X))(),
            r=["cur"], w=["evs"])
        p.emit("v", (lambda k=k: lambda e: e.tensor_scalar(
            sb["t16a"][:], sb["cur"][:], sb["evs"][0:1, k:k + 1], None,
            ALU.is_ge))(), r=["cur", "evs"], w=["t16a"], tag="iseq")
        p.emit("v", lambda e: e.tensor_copy(sb["t16b"][:], sb["t16a"][:]),
               r=["t16a"], w=["t16b"])
        for i, sh in enumerate((1, 2, 4, 8)):
            src_nm = "t16b" if i % 2 == 0 else "cur2"
            if i == 0:
                src_nm, dst_nm = "t16b", "isbuf"
            pass
        p.emit("v", lambda e: e.tensor_copy(sb["isbuf"][0:1, 48:64],
                                            sb["t16b"][:]),
               r=["t16b"], w=["isbuf"])
        p.emit("v", lambda e: e.tensor_add(
            sb["isbuf"][0:1, 49:64], sb["isbuf"][0:1, 49:64],
            sb["t16b"][0:1, 0:15]), r=["t16b", "isbuf"], w=["isbuf"])
        p.emit("v", lambda e: e.tensor_copy(sb["t16b"][:],
                                            sb["isbuf"][0:1, 48:64]),
               r=["isbuf"], w=["t16b"])
        p.emit("v", lambda e: e.tensor_add(
            sb["t16b"][0:1, 2:16], sb["t16b"][0:1, 2:16],
            sb["isbuf"][0:1, 48:62]), r=["isbuf", "t16b"], w=["t16b"])
        p.emit("v", lambda e: e.tensor_copy(sb["isbuf"][0:1, 48:64],
                                            sb["t16b"][:]),
               r=["t16b"], w=["isbuf"])
        p.emit("v", lambda e: e.tensor_add(
            sb["isbuf"][0:1, 52:64], sb["isbuf"][0:1, 52:64],
            sb["t16b"][0:1, 0:12]), r=["t16b", "isbuf"], w=["isbuf"])
        p.emit("v", lambda e: e.tensor_copy(sb["t16b"][:],
                                            sb["isbuf"][0:1, 48:64]),
               r=["isbuf"], w=["t16b"])
        p.emit("v", lambda e: e.tensor_add(
            sb["t16b"][0:1, 8:16], sb["t16b"][0:1, 8:16],
            sb["isbuf"][0:1, 48:56]), r=["isbuf", "t16b"], w=["t16b"])
        p.emit("v", (lambda: lambda e: e.tensor_single_scalar(
            sb["t16b"][:], sb["t16b"][:], 1.0, ALU.is_equal))(),
            r=["t16b"], w=["t16b"])
        p.emit("v", (lambda k=k: lambda e: e.tensor_mul(
            sb["isbuf"][0:1, k * 16:k * 16 + 16], sb["t16a"][:],
            sb["t16b"][:]))(), r=["t16a", "t16b"], w=["isbuf"], tag="first")
        p.emit("v", (lambda k=k: lambda e: e.tensor_scalar(
            sb["t16a"][:], sb["isbuf"][0:1, k * 16:k * 16 + 16], 4.0, None,
            ALU.mult))(), r=["isbuf"], w=["t16a"])
        p.emit("v", (lambda: lambda e: e.tensor_sub(
            sb["cur"][:], sb["cur"][:], sb["t16a"][:]))(),
            r=["cur", "t16a"], w=["cur"])
    p.emit("s", lambda e: e.activation(sb["evs"][0:1, 4:8], sb["evs"][0:1, 0:4],
                                       AF.Exp), r=["evs"], w=["evs"])
    p.emit("v", lambda e: e.reduce_sum(sb["mrow"][0:1, 0:1],
                                       sb["evs"][0:1, 4:8], axis=AX.X),
           r=["evs"], w=["mrow"])
    p.emit("v", lambda e: e.tensor_scalar_add(sb["mrow"][0:1, 0:1],
                                              sb["mrow"][0:1, 0:1], 12.0),
           r=["mrow"], w=["mrow"])
    p.emit("s", lambda e: e.activation(sb["mrow"][0:1, 0:1],
                                       sb["mrow"][0:1, 0:1], AF.Ln),
           r=["mrow"], w=["mrow"])
    p.emit("s", lambda e: e.activation(sb["mrow"][0:1, 0:1],
                                       sb["mrow"][0:1, 0:1], AF.Exp,
                                       scale=-1.0), r=["mrow"], w=["mrow"])
    p.emit("v", lambda e: e.tensor_scalar(sb["grow4"][0:1, 0:4],
                                          sb["evs"][0:1, 4:8],
                                          sb["mrow"][0:1, 0:1], None, ALU.mult),
           r=["evs", "mrow"], w=["grow4"])
    p.emit("v", lambda e: e.memset(sb["Erow"][:], 0.0), w=["Erow"])
    for k in range(4):
        p.emit("v", (lambda k=k: lambda e: e.tensor_scalar(
            sb["Erow"].ap()[0:1, k::4], sb["isbuf"][0:1, k * 16:k * 16 + 16],
            sb["grow4"][0:1, k:k + 1], None, ALU.mult))(),
            r=["isbuf", "grow4"], w=["Erow"], tag="erow")

    for js in range(8):
        for half in range(2):
            p.emit("t", (lambda js=js, half=half: lambda e: e.matmul(
                ps45[:, 528 + half * 4:532 + half * 4],
                cx.const_ap["ones_row_f32"][0:1, :],
                sb["Erow"][0:1, (2 * js + half) * 4:(2 * js + half) * 4 + 4],
                start=True, stop=True))(),
                r=["Erow", "c_ones_row_f32"], w=["PS45"], tag="ebc")
        p.emit("v", (lambda: lambda e: e.tensor_copy(
            sb["ecols"][:], ps45[:, 528:536]))(), r=["PS45"], w=["ecols"])
        p.emit("v", (lambda: lambda e: e.select(
            sb["hg4"][:], cx.const_ap["lo_mask4_u8"][:], sb["ecols"][:, 0:4],
            sb["ecols"][:, 4:8]))(),
            r=["ecols", "c_lo_mask4_u8"], w=["hg4"])
        for k in range(4):
            p.emit("v", (lambda js=js, k=k: lambda e: e.tensor_scalar(
                sb["pis"][:, js * 256 + k * 64:js * 256 + k * 64 + 64],
                cx.const_ap["i64pat_f32"][:], sb["hg4"][:, k:k + 1], None,
                ALU.mult))(), r=["hg4", "c_i64pat_f32"], w=["pis"], tag="pik")

    for wnm, w_in, WE, bcol, bE in (
            ("q", wq_in, "WQE", "bqcol", "beQ"),
            ("k", wk_in, "WKE", "bkcol", "beK"),
            ("v", wv_in, "WVE", "bvcol", "beV")):
        for kt in range(8):
            p.emit("g", (lambda w_in=w_in, kt=kt: lambda e: e.dma_start(
                out=sb["wball"][:, kt * 1024:(kt + 1) * 1024],
                in_=w_in[kt]))(), r=["wball"], w=["wball"], dma=True,
                tag="lwe")
        for mt in range(8):
            for kt in range(8):
                p.emit("t", (lambda mt=mt, kt=kt: lambda e: e.matmul(
                    ps45[:, 0:256],
                    sb["wball"][:, kt * 1024 + mt * 128:
                                kt * 1024 + mt * 128 + 128],
                    sb["pis"][:, kt * 256:kt * 256 + 256],
                    start=(kt == 0), stop=(kt == 7)))(),
                    r=["wball", "pis"], w=["PS45"], tag="wemm")
            p.emit("s", (lambda WE=WE, mt=mt: lambda e: e.activation(
                sb[WE][:, mt * 256:mt * 256 + 256], ps45[:, 0:256],
                AF.Copy))(), r=["PS45"], w=[WE], tag="weev")
        for half in range(2):
            for kt in range(8):
                p.emit("t", (lambda half=half, kt=kt, bcol=bcol: lambda e:
                             e.matmul(
                    ps45[:, 256:257],
                    sb["pis"][:, kt * 256 + half * 128:
                              kt * 256 + half * 128 + 128],
                    sb[bcol][:, kt:kt + 1],
                    start=(kt == 0), stop=(kt == 7)))(),
                    r=["pis", bcol], w=["PS45"], tag="bemm")
            p.emit("v", (lambda bE=bE, half=half: lambda e: e.tensor_copy(
                sb[bE][:, half:half + 1], ps45[:, 256:257]))(),
                r=["PS45"], w=[bE])

    for WE, bE, o0, o1 in (("WQE", "beQ", "qkt0", "qkt1"),
                           ("WKE", "beK", "kkt0", "kkt1"),
                           ("WVE", "beV", "vkt0", "vkt1")):
        for half in range(2):
            onm = o0 if half == 0 else o1
            for ch in range(4):
                for mt in range(8):
                    p.emit("t", (lambda WE=WE, half=half, ch=ch, mt=mt:
                                 lambda e: e.matmul(
                        ps67[:, 0:512],
                        sb[WE][:, mt * 256 + half * 128:
                               mt * 256 + half * 128 + 128],
                        sb[f"lnb{mt}"][:, ch * 512:ch * 512 + 512],
                        start=(mt == 0), stop=(mt == 7)))(),
                        r=[WE, f"lnb{mt}"], w=["PS67"], tag="qkv")
                p.emit("s", (lambda onm=onm, bE=bE, half=half, ch=ch:
                             lambda e: e.activation(
                    sb[onm][:, ch * 512:ch * 512 + 512], ps67[:, 0:512],
                    AF.Identity, bias=sb[bE][:, half:half + 1]))(),
                    r=["PS67", bE], w=[onm], tag="qkvev")

    p.emit("v", lambda e: e.memset(sb["vtm"][:], 1.0), w=["vtm"])
    psb23 = ps23.ap().bitcast(BF16)
    for st in range(16):
        for sl in range(2):
            p.emit("t", (lambda st=st, sl=sl, psb23=psb23: lambda e:
                         e.transpose(
                psb23[:, 0:128], sb[f"vkt{sl}"][:, st * 128:st * 128 + 128],
                cx.const_ap["ident_bf"][:]))(),
                r=[f"vkt{sl}", "c_ident_bf"], w=["PS23"], tag="vtr")
            for q2 in range(2):
                k = 2 * sl + q2
                p.emit("v", (lambda st=st, k=k, q2=q2, psb23=psb23: lambda e:
                             e.tensor_copy(
                    sb["vtm"][:, st * 260 + k * 65:st * 260 + k * 65 + 64],
                    psb23[:, q2 * 64:q2 * 64 + 64]))(),
                    r=["PS23"], w=["vtm"], tag="vcp")

    PSO = [("PS45", 0), ("PS45", 512), ("PS67", 0), ("PS67", 512)]
    for k in range(4):
        pair = k // 2
        b0 = (k % 2) * 64
        qn, kn = f"qkt{pair}", f"kkt{pair}"
        on = "otA" if k < 2 else "otB"
        for qg in range(4):
            for st in range(16):
                p.emit("t", (lambda qn=qn, kn=kn, b0=b0, st=st, qg=qg:
                             lambda e: e.matmul(
                    ps01[:, 0:512],
                    sb[kn][b0:b0 + 64, st * 128:st * 128 + 128],
                    sb[qn][b0:b0 + 64, qg * 512:qg * 512 + 512],
                    start=True, stop=True))(),
                    r=[kn, qn], w=["PS01"], tag="sc")
                p.emit("s", (lambda: lambda e: e.activation(
                    sb["usb"][:], ps01[:, 0:512], AF.Exp, scale=0.125))(),
                    r=["PS01"], w=["usb"], tag="scexp")
                for qt in range(4):
                    pk, off = PSO[qt]
                    psx = ps45 if pk == "PS45" else ps67
                    p.emit("t", (lambda qt=qt, st=st, k=k, psx=psx, off=off:
                                 lambda e: e.matmul(
                        psx[:, off:off + 65],
                        sb["usb"][:, qt * 128:qt * 128 + 128],
                        sb["vtm"][:, st * 260 + k * 65:st * 260 + k * 65 + 65],
                        start=(st == 0), stop=(st == 15)))(),
                        r=["usb", "vtm"], w=[pk], tag="av")
            for qt in range(4):
                pk, off = PSO[qt]
                psx = ps45 if pk == "PS45" else ps67
                p.emit("s", (lambda psx=psx, off=off: lambda e: e.activation(
                    sb["rc"][:], psx[:, off + 64:off + 65], AF.Ln))(),
                    r=[pk], w=["rc"])
                p.emit("s", (lambda: lambda e: e.activation(
                    sb["rc"][:], sb["rc"][:], AF.Exp, scale=-1.0))(),
                    r=["rc"], w=["rc"])
                p.emit("v", (lambda psx=psx, off=off: lambda e: e.tensor_scalar(
                    sb["otmp"][:], psx[:, off:off + 64], sb["rc"][:, 0:1],
                    None, ALU.mult))(), r=[pk, "rc"], w=["otmp"], tag="osc")
                p.emit("t", (lambda psb23=psb23: lambda e: e.transpose(
                    psb23[0:64, 128:256], sb["otmp"][:],
                    cx.const_ap["ident_bf"][:]))(),
                    r=["otmp", "c_ident_bf"], w=["PS23"], tag="otr")
                p.emit("v", (lambda psb23=psb23: lambda e: e.tensor_copy(
                    sb["otm2"][:], psb23[0:64, 128:256]))(),
                    r=["PS23"], w=["otm2"])
                p.emit("g", (lambda on=on, b0=b0, qg=qg, qt=qt: lambda e:
                             e.dma_start(
                    out=sb[on][b0:b0 + 64,
                               (qg * 4 + qt) * 128:(qg * 4 + qt) * 128 + 128],
                    in_=sb["otm2"][:]))(), r=["otm2", on], w=[on], dma=True,
                    tag="omv")

    for mt in range(8):
        for ch in range(4):
            for kt in range(2):
                onm = "otA" if kt == 0 else "otB"
                p.emit("t", (lambda mt=mt, ch=ch, kt=kt, onm=onm: lambda e:
                             e.matmul(
                    ps01[:, 0:512],
                    sb["wosb"][:, kt * 1024 + mt * 128:
                               kt * 1024 + mt * 128 + 128],
                    sb[onm][:, ch * 512:ch * 512 + 512],
                    start=(kt == 0), stop=(kt == 1)))(),
                    r=["wosb", onm], w=["PS01"], tag="womm3")
            p.emit("v", (lambda mt=mt, ch=ch: lambda e: e.scalar_tensor_tensor(
                sb[f"res{mt}"][:, ch * 512:ch * 512 + 512], ps01[:, 0:512],
                sb["bocol"][:, mt:mt + 1], sb[f"res{mt}"][:, ch * 512:
                                                          ch * 512 + 512],
                ALU.add, ALU.add))(), r=["PS01", "bocol", f"res{mt}"],
                w=[f"res{mt}"], tag="woadd3")
    for i in range(8):
        p.emit("q", (lambda i=i: lambda e: e.dma_start(
            out=xout[i * 128:(i + 1) * 128, :], in_=sb[f"res{i}"][:]))(),
            r=[f"res{i}"], w=["xout_d"], dma=True, tag="xst3")
    p.barrier(engs=("q",), on=("q", "g"))
    p.barrier(engs=("g",), on=("g",))
    return cx


def c3_host_inputs(inputs):
    f32 = lambda k: np.asarray(inputs[k], np.float32)
    pk8 = lambda w: np.ascontiguousarray(
        np.asarray(w, np.float32).T.reshape(8, 128, 1024))
    col8 = lambda b: np.ascontiguousarray(
        np.asarray(b, np.float32).reshape(8, 128).T)
    wrp = np.ascontiguousarray(
        f32("L1_ffn_Wr").reshape(8, 128, 16).transpose(1, 0, 2)
        .reshape(128, 128))
    return dict(
        wqp=pk8(inputs["L1_ffn_Wq"]), wkp=pk8(inputs["L1_ffn_Wk"]),
        wvp=pk8(inputs["L1_ffn_Wv"]), wrp=wrp,
        bqc=col8(inputs["L1_ffn_bq"]), bkc=col8(inputs["L1_ffn_bk"]),
        bvc=col8(inputs["L1_ffn_bv"]), boc=col8(inputs["L1_ffn_bo"]),
        brc=np.ascontiguousarray(f32("L1_ffn_br").reshape(16, 1)),
        wop=np.ascontiguousarray(f32("L1_ffn_Wo").reshape(2, 128, 1024)))


def run_launch_c3(inputs, xl2s):
    consts = host_constants()
    cx = build_launch_c3(consts)
    cx.p.seal(cx.es)
    cx.es.close()
    base = c3_host_inputs(inputs)
    in_maps = [dict(base, xin=xl2s[c]) for c in range(NCORES)]
    return run_bass_kernel_spmd(cx.nc, in_maps, core_ids=list(range(NCORES)))



def build_launch_c123(consts):
    cx = Ctx("C123")
    nc, p = cx.nc, cx.p
    for nm in ("PS01", "PS23", "PS45", "PS67"):
        cx.psum(nm, (128, 1024))
    cx.load_consts(consts, ["ones_col_f32", "ones_row_f32", "ident_f32",
                            "ident_bf", "iota_col_f32", "iota16_f32",
                            "lt_incl_f32", "trimask_u8", "zero_f32_128",
                            "lo_mask4_u8", "i64pat_f32"])
    for i in range(8):
        cx.sbuf(f"res{i}", (128, S), F32)
    for i in range(8):
        cx.sbuf(f"lnb{i}", (128, S), BF16)
    shared = dict(cx.sb)
    ext1 = dict(
        xin=nc.dram_tensor("xin", [D, S], F32, kind="ExternalInput").ap(),
        lgown=nc.dram_tensor("lgown", [S, E], F32, kind="ExternalInput").ap(),
        yin=nc.dram_tensor("yin", [E * BLK, D], BF16,
                           kind="ExternalInput").ap(),
        swg=nc.dram_tensor("swg", [D, F], F32, kind="ExternalInput").ap(),
        swu=nc.dram_tensor("swu", [D, F], F32, kind="ExternalInput").ap(),
        swd=nc.dram_tensor("swd", [F, D], F32, kind="ExternalInput").ap())
    cx.push_scope("p1")
    cx.sb = dict(shared)
    build_launch_c(consts, cx=cx, ext=ext1)
    p.barrier()
    cx.pop_scope()
    cx.push_scope("p2")
    cx.sb = dict(shared)
    build_launch_c2(consts, cx=cx, ext={})
    p.barrier()
    cx.pop_scope()
    cx.push_scope("p3")
    cx.sb = dict(shared)
    build_launch_c3(consts, cx=cx, ext={})
    cx.pop_scope()
    return cx


def run_launch_c123(inputs, xouts, lgouts, yblks):
    consts = host_constants()
    cx = build_launch_c123(consts)
    cx.p.seal(cx.es)
    cx.es.close()
    swg = np.ascontiguousarray(np.asarray(inputs["L0_ffn_sw_gate"], np.float32))
    swu = np.ascontiguousarray(np.asarray(inputs["L0_ffn_sw_up"], np.float32))
    swd = np.ascontiguousarray(np.asarray(inputs["L0_ffn_sw_down"], np.float32))
    b2 = c2_host_inputs(inputs)
    b3 = c3_host_inputs(inputs)
    in_maps = []
    for c in range(NCORES):
        yin = np.concatenate(
            [yblks[e // 2][(e % 2) * NCORES * BLK + c * BLK:
                           (e % 2) * NCORES * BLK + (c + 1) * BLK]
             for e in range(E)], 0)
        in_maps.append(dict(b2, **b3, xin=xouts[c], lgown=lgouts[c],
                            yin=np.ascontiguousarray(yin),
                            swg=swg, swu=swu, swd=swd))
    return run_bass_kernel_spmd(cx.nc, in_maps, core_ids=list(range(NCORES)))



def _bfview(arr):
    return np.frombuffer(np.ascontiguousarray(arr).tobytes(),
                         dtype=ml_dtypes.bfloat16).reshape(arr.shape)


def kernel(**inputs):
    resA = run_launch_a(inputs)
    lgouts = [resA.results[c]["lgout"] for c in range(NCORES)]
    ablk = [_bfview(resA.results[c]["blkout"]) for c in range(NCORES)]
    xouts = [resA.results[c]["xout"] for c in range(NCORES)]
    lgfull = np.concatenate(lgouts, 0)
    resB = run_launch_b(inputs, lgfull, ablk)
    yblks = [_bfview(resB.results[c]["yblk"]) for c in range(NCORES)]
    resC = run_launch_c123(inputs, xouts, lgouts, yblks)
    return np.stack([np.ascontiguousarray(resC.results[c]["xout"].T)
                     for c in range(NCORES)], 0).astype(np.float32)


# version 07
# speedup vs baseline: 1.1923x; 1.1923x over previous
import math
import numpy as np
import ml_dtypes
from contextlib import ExitStack

import concourse.bass as bass
import concourse.mybir as mybir
from concourse.bass_utils import run_bass_kernel_spmd
from concourse.bass_core_rust import IndirectOffsetOnAxis as IOA

F32 = mybir.dt.float32
BF16 = mybir.dt.bfloat16
U32 = mybir.dt.uint32
U8 = mybir.dt.uint8
AF = mybir.ActivationFunctionType
ALU = mybir.AluOpType
AX = mybir.AxisListType

NCORES = 8
D = 1024
S = 2048
NHEAD = 16
DH = 64
E = 16
CAP = 2560
BLK = 640
F = 1408
INNER = 2048
L1H = 8
ST = 64
HD = 256
TC = 128


def _bf(x):
    return np.asarray(x).astype(ml_dtypes.bfloat16)


def alibi_slopes(h=NHEAD):
    x = (2.0 ** 8) ** (1.0 / h)
    return np.asarray([1.0 / x ** (i + 1) for i in range(h)], np.float64)



ENG = ("t", "v", "s", "g", "q")


class P:

    def __init__(self, nc):
        self.nc = nc
        self.ops = {e: [] for e in ENG}
        self.cnt = {e: 0 for e in ENG}
        self.lastw = {}
        self.lastr = {}
        self.waited = {e: {e2: 0 for e2 in ENG} for e in ENG}
        self.nop = 0

    def emit(self, eng, fn, r=(), w=(), dma=False, tag=""):
        amt = 16 if dma else 1
        waits = {}

        def dep(d):
            if d is None:
                return
            e2, c2 = d
            if self.waited[eng][e2] >= c2:
                return
            if e2 == eng and not dma:
                pass
            waits[e2] = max(waits.get(e2, 0), c2)

        for k in r:
            dep(self.lastw.get(k))
        for k in w:
            dep(self.lastw.get(k))
            for e2, c2 in self.lastr.get(k, {}).items():
                dep((e2, c2))
        for e2, c2 in waits.items():
            self.waited[eng][e2] = max(self.waited[eng][e2], c2)
        self.cnt[eng] += amt
        me = (eng, self.cnt[eng])
        for k in w:
            self.lastw[k] = me
            self.lastr.pop(k, None)
        for k in r:
            self.lastr.setdefault(k, {})[eng] = self.cnt[eng]
        self.ops[eng].append((fn, dict(waits), amt, tag))
        self.nop += 1

    def barrier(self, engs=ENG, on=ENG):
        snap = {e: self.cnt[e] for e in on}
        for eng in engs:
            waits = {}
            for e2, c2 in snap.items():
                if c2 > self.waited[eng][e2]:
                    waits[e2] = c2
                    self.waited[eng][e2] = c2
            if waits:
                self.ops[eng].append((None, waits, 0, "barrier"))

    def seal(self, es):
        nc = self.nc
        waited = {e: set() for e in ENG}
        for ename in ENG:
            for fn, waits, amt, tag in self.ops[ename]:
                for e2, c2 in waits.items():
                    waited[e2].add(c2)
        remap = {}
        marks = {}
        for e in ENG:
            m = {0: 0}
            mk = []
            c = new = 0
            for fn, waits, amt, tag in self.ops[e]:
                if fn is None:
                    mk.append(False)
                    continue
                c += amt
                inc = (amt != 1) or (c in waited[e])
                if inc:
                    new += amt
                mk.append(inc)
                m[c] = new
            remap[e] = m
            marks[e] = mk
        sems = {e: es.enter_context(nc.semaphore(f"sem_{e}")) for e in ENG}
        block = es.enter_context(nc.Block())
        secs = dict(t=block.tensor, v=block.vector, s=block.scalar,
                    g=block.gpsimd, q=block.sync)

        def runner(ename):
            def run(engine):
                for (fn, waits, amt, tag), inc in zip(self.ops[ename],
                                                      marks[ename]):
                    for e2, c2 in waits.items():
                        engine.wait_ge(sems[e2], remap[e2][c2])
                    if fn is None:
                        continue
                    inst = fn(engine)
                    if inc:
                        inst.then_inc(sems[ename], amt)
            return run

        for ename, sec in secs.items():
            sec(runner(ename))
        return sems



def host_constants():
    c = {}
    c["ones_col_f32"] = np.ones((128, 1), np.float32)
    c["ones_row_f32"] = np.ones((65, 128), np.float32)
    c["ones_row_bf"] = _bf(np.ones((1, 128)))
    c["ident_bf"] = _bf(np.eye(128))
    c["ident_f32"] = np.eye(128, dtype=np.float32)
    iot = np.zeros((65, 2048), np.float32)
    iot[0] = np.arange(2048); iot[64] = np.arange(2048)
    c["iota_row_bf"] = _bf(iot)
    c["iota_col_f32"] = np.arange(128, dtype=np.float32).reshape(128, 1)
    c["iota16_f32"] = np.tile(np.arange(16, dtype=np.float32), (128, 1))
    sidx = np.arange(128)
    c["lt_incl_f32"] = (sidx[:, None] <= sidx[None, :]).astype(np.float32)
    for off in (0, 128, 256, 384):
        i = np.arange(512)[None, :]
        p = np.arange(128)[:, None]
        c[f"cmask{off}"] = (i >= p + off).astype(np.uint8)
    slopes = alibi_slopes()
    ab = np.zeros((128, NHEAD * 16), np.float32)
    for h in range(NHEAD):
        for jt in range(16):
            ab[:, h * 16 + jt] = slopes[h] * (128 * jt + np.arange(128))
    c["alibi_cols"] = ab
    c["zero_bf_512"] = _bf(np.zeros((128, 512)))
    c["trimask_u8"] = (sidx[:, None] <= sidx[None, :]).astype(np.uint8)
    c["zero_f32_128"] = np.zeros((128, 128), np.float32)
    c["lo_mask4_u8"] = (sidx[:, None] < 64).astype(np.uint8) * np.ones(
        (1, 4), np.uint8)
    c["i64pat_f32"] = (np.arange(64)[None, :] == (sidx % 64)[:, None]
                       ).astype(np.float32)
    return c


class TView:

    def __init__(self, ap):
        self._ap = ap

    def ap(self):
        return self._ap

    def __getitem__(self, idx):
        return self._ap.__getitem__(idx)


class Ctx:

    def __init__(self, name):
        self.nc = bass.Bass("TRN2", target_bir_lowering=False, debug=False,
                            dynamic_dma_scratch_size=16384)
        self.p = P(self.nc)
        self.es = ExitStack()
        self.sb = {}
        self.ps = {}
        self.const_ap = {}

    def sbuf(self, name, shape, dtype):
        if name in self.sb:
            return self.sb[name]
        es = getattr(self, "cur_es", None) or self.es
        uname = name if not getattr(self, "scope_tag", "") else \
            f"{self.scope_tag}_{name}"
        t = es.enter_context(self.nc.sbuf_tensor(uname, list(shape), dtype))
        self.sb[name] = t
        return t

    def psum(self, name, shape, dtype=F32):
        if name in self.ps:
            return self.ps[name]
        t = self.es.enter_context(self.nc.psum_tensor(name, list(shape), dtype))
        self.ps[name] = t
        return t

    def view(self, name, ap):
        t = TView(ap)
        self.sb[name] = t
        return t

    def push_scope(self, tag):
        self.cur_es = ExitStack()
        self.scope_tag = tag

    def pop_scope(self):
        self.cur_es.close()
        self.cur_es = None
        self.scope_tag = ""

    def load_consts(self, consts, names):
        for nm in names:
            if nm in self.const_ap:
                continue
            arr = consts[nm]
            dt = {np.dtype(np.float32): F32, np.dtype(ml_dtypes.bfloat16): BF16,
                  np.dtype(np.uint8): U8, np.dtype(np.uint16): mybir.dt.uint16,
                  np.dtype(np.uint32): U32}[arr.dtype]
            dram = self.nc.inline_tensor(np.ascontiguousarray(arr), f"cd_{nm}").ap()
            tile = self.sbuf(f"c_{nm}", arr.shape, dt)
            self.p.emit("q", (lambda d=dram, t=tile: lambda e:
                              e.dma_start(out=t[:], in_=d))(),
                        w=[f"c_{nm}"], dma=True)
            self.const_ap[nm] = tile


def build_layernorm(cx, res_names, out_bf, out_f32=None, scratch="lnscr",
                    tag="ln"):
    p, sb = cx.p, cx.sb
    ones = cx.const_ap["ones_col_f32"]
    onesr = cx.const_ap["ones_row_f32"]
    s45, s67 = cx.ps["PS45"], cx.ps["PS67"]
    scr = sb[scratch]
    stM, stV = sb["stM"], sb["stV"]
    for half in range(2):
        hs = slice(half * 1024, half * 1024 + 1024)
        for i in range(8):
            for q in range(2):
                qs = slice(half * 1024 + q * 512, half * 1024 + q * 512 + 512)
                p.emit("t", (lambda i=i, q=q, qs=qs: lambda e:
                             e.matmul(s45[0:1, q * 512:q * 512 + 512],
                                      ones[:], sb[res_names[i]][:, qs],
                                      start=(i == 0), stop=(i == 7)))(),
                       r=[res_names[i], "c_ones_col_f32"], w=["PS45"], tag=tag + "su")
        for i in range(8):
            p.emit("s", (lambda i=i, hs=hs: lambda e:
                         e.activation(scr[:, 0:1024], sb[res_names[i]][:, hs],
                                      AF.Square))(),
                   r=[res_names[i]], w=[scratch], tag=tag + "sq")
            for q in range(2):
                p.emit("t", (lambda i=i, q=q: lambda e:
                             e.matmul(s67[0:1, q * 512:q * 512 + 512],
                                      ones[:], scr[:, q * 512:q * 512 + 512],
                                      start=(i == 0), stop=(i == 7)))(),
                       r=[scratch, "c_ones_col_f32"], w=["PS67"], tag=tag + "s2")
        p.emit("v", (lambda hs=hs: lambda e:
                     e.tensor_scalar_mul(stM[0:1, hs], s45[0:1, 0:1024],
                                         -1.0 / 1024))(),
               r=["PS45"], w=["lnstats"], tag=tag + "mean")
        p.emit("v", (lambda hs=hs: lambda e:
                     e.tensor_scalar_mul(stV[0:1, hs], s67[0:1, 0:1024],
                                         1.0 / 1024))(),
               r=["PS67"], w=["lnstats"], tag=tag + "msq")
    p.emit("v", lambda e: e.tensor_mul(scr[0:1, 0:2048], stM[:], stM[:]),
           r=["lnstats"], w=["lnscr"], tag=tag + "m2")
    p.emit("v", lambda e: e.tensor_sub(stV[:], stV[:], scr[0:1, 0:2048]),
           r=["lnstats", "lnscr"], w=["lnstats"], tag=tag + "var")
    p.emit("v", lambda e: e.tensor_scalar_add(stV[:], stV[:], 1e-5),
           r=["lnstats"], w=["lnstats"], tag=tag + "eps")
    p.emit("s", lambda e: e.activation(stV[:], stV[:], AF.Ln),
           r=["lnstats"], w=["lnstats"], tag=tag + "ln")
    p.emit("s", lambda e: e.activation(stV[:], stV[:], AF.Exp, scale=-0.5),
           r=["lnstats"], w=["lnstats"], tag=tag + "exp")
    p.emit("v", lambda e: e.tensor_mul(stM[:], stM[:], stV[:]),
           r=["lnstats"], w=["lnstats"], tag=tag + "nmr")
    rb, nb = sb["ln_rb"], sb["ln_nb"]
    for half in range(2):
        for q in range(2):
            qs = slice(half * 1024 + q * 512, half * 1024 + q * 512 + 512)
            p.emit("t", (lambda qs=qs, q=q: lambda e:
                         e.matmul(s45[:, q * 512:q * 512 + 512], onesr[0:1, :],
                                  stV[0:1, qs], start=True, stop=True))(),
                   r=["lnstats", "c_ones_row_f32"], w=["PS45"], tag=tag + "bc")
            p.emit("t", (lambda qs=qs, q=q: lambda e:
                         e.matmul(s67[:, q * 512:q * 512 + 512], onesr[0:1, :],
                                  stM[0:1, qs], start=True, stop=True))(),
                   r=["lnstats", "c_ones_row_f32"], w=["PS67"], tag=tag + "bc")
        p.emit("v", (lambda half=half: lambda e:
                     e.tensor_copy(rb[:, half * 1024:half * 1024 + 1024],
                                   s45[:, 0:1024]))(),
               r=["PS45"], w=["ln_rb"], tag=tag + "cb")
        p.emit("v", (lambda half=half: lambda e:
                     e.tensor_copy(nb[:, half * 1024:half * 1024 + 1024],
                                   s67[:, 0:1024]))(),
               r=["PS67"], w=["ln_nb"], tag=tag + "cb")
    for i in range(8):
        p.emit("v", (lambda i=i: lambda e:
                     e.tensor_mul(scr[:], sb[res_names[i]][:], rb[:]))(),
               r=[res_names[i], "ln_rb"], w=[scratch], tag=tag + "nm")
        p.emit("v", (lambda i=i: lambda e:
                     e.tensor_add(sb[out_bf[i]][:], scr[:], nb[:]))(),
               r=[scratch, "ln_nb"], w=[out_bf[i]], tag=tag + "na")
        if out_f32 is not None:
            p.emit("v", (lambda i=i: lambda e:
                         e.tensor_add(sb[out_f32[i]][:], scr[:], nb[:]))(),
                   r=[scratch, "ln_nb"], w=[out_f32[i]], tag=tag + "nf")




def build_launch_a(consts, stage=99):
    cx = Ctx("A")
    nc, p = cx.nc, cx.p
    x_in = nc.dram_tensor("x", [S, D], F32, kind="ExternalInput").ap()
    watt = nc.dram_tensor("watt", [8, 128, 3072], F32, kind="ExternalInput").ap()
    rw_in = nc.dram_tensor("rw", [D, E], F32, kind="ExternalInput").ap()
    rb_in = nc.dram_tensor("rb", [1, E], F32, kind="ExternalInput").ap()
    xout = nc.dram_tensor("xout", [D, S], F32, kind="ExternalOutput").ap()
    lgout = nc.dram_tensor("lgout", [S, E], F32, kind="ExternalOutput").ap()
    NBLKROW = E * BLK + 2 * S
    blkout = nc.dram_tensor("blkout", [NBLKROW, D], BF16, kind="ExternalOutput").ap()

    sb = cx.sb
    for i in range(8):
        cx.sbuf(f"res{i}", (128, S), F32)
    for i in range(8):
        cx.sbuf(f"lnb{i}", (128, S), BF16)
    for j in range(2):
        cx.sbuf(f"qop{j}", (128, S), BF16)
        cx.sbuf(f"kop{j}", (128, S), BF16)
        cx.sbuf(f"usb{j}", (128, 1024), BF16)
        cx.sbuf(f"usbm{j}", (128, 512), BF16)
        cx.sbuf(f"vbuf{j}", (128, 1040), BF16)
    cx.sbuf("wbuf0", (128, 3072), BF16)
    cx.sbuf("wbuf1", (128, 3072), BF16)
    cx.sbuf("lnscr", (128, S), F32)
    cx.sbuf("ln_rb", (128, S), F32)
    cx.sbuf("ln_nb", (128, S), F32)
    cx.sbuf("stM", (1, S), F32)
    cx.sbuf("stV", (1, S), F32)
    cx.view("lnftmp", cx.sb["lnscr"].ap())
    cx.sbuf("lntm", (128, 1024), BF16)
    cx.sbuf("rwsb", (128, 8 * E), F32)
    cx.sbuf("rbsb", (1, E), F32)
    cx.sbuf("crA", (1, 1024), F32)
    cx.sbuf("crB", (1, 1024), F32)
    cx.sbuf("rcpb", (128, 1024), F32)
    cx.sbuf("atmp", (128, 1024), F32)
    cx.sbuf("lgsb", (128, 16 * E), F32)
    cx.sbuf("RW", (128, 10 * E), F32)
    cx.sbuf("RC", (128, 16), F32)
    cx.sbuf("RU", (128, 4), U8)
    cx.sbuf("IDX", (128, 4), U32)
    cx.sbuf("CR", (1, E), F32)
    cx.sbuf("augscal", (65, 1), F32)

    ps01 = cx.psum("PS01", (128, 1024))
    ps23 = cx.psum("PS23", (128, 1024))
    ps45 = cx.psum("PS45", (128, 1024))
    ps67 = cx.psum("PS67", (128, 1024))

    cx.load_consts(consts, ["ones_col_f32", "ones_row_f32", "ident_f32",
                            "ident_bf", "iota_row_bf", "iota_col_f32",
                            "iota16_f32", "lt_incl_f32", "cmask0", "cmask128",
                            "cmask256", "cmask384", "alibi_cols", "zero_bf_512"])

    p.emit("q", lambda e: e.dma_start(
        out=sb["rwsb"].ap().rearrange("p (k e) -> p k e", e=E),
        in_=rw_in.rearrange("(k p) e -> p k e", p=128)),
        w=["rwsb"], dma=True)
    p.emit("q", lambda e: e.dma_start(out=sb["rbsb"][:], in_=rb_in),
           w=["rbsb"], dma=True)

    stg = sb["lnftmp"]
    for tt in range(16):
        p.emit("q", (lambda tt=tt: lambda e: e.dma_start(
            out=stg[:, 0:1024], in_=x_in[tt * 128:(tt + 1) * 128, :]))(),
            r=["lnftmp"], w=["lnftmp"], dma=True, tag="ldx")
        psx = ps01 if tt % 2 == 0 else ps23
        pkey = "PS01" if tt % 2 == 0 else "PS23"
        for i in range(8):
            p.emit("t", (lambda i=i, psx=psx: lambda e: e.transpose(
                psx[:, i * 128:(i + 1) * 128],
                stg[:, i * 128:(i + 1) * 128],
                cx.const_ap["ident_f32"][:]))(),
                r=["lnftmp", "c_ident_f32"], w=[pkey], tag="trx")
        for i in range(8):
            p.emit("v", (lambda i=i, tt=tt, psx=psx: lambda e: e.tensor_copy(
                sb[f"res{i}"][:, tt * 128:(tt + 1) * 128],
                psx[:, i * 128:(i + 1) * 128]))(),
                r=[pkey], w=[f"res{i}", "lnftmp"], tag="evx")

    res_names = [f"res{i}" for i in range(8)]
    lnb_names = [f"lnb{i}" for i in range(8)]

    if stage >= 2:
        build_layernorm(cx, res_names, lnb_names, tag="ln0")
    if stage == 2:
        for i in range(8):
            p.emit("v", (lambda i=i: lambda e: e.tensor_copy(
                sb["lnscr"][:], sb[f"lnb{i}"][:]))(),
                r=[f"lnb{i}"], w=["lnscr"], tag="dbg")
            p.emit("q", (lambda i=i: lambda e: e.dma_start(
                out=xout[i * 128:(i + 1) * 128, :], in_=sb["lnscr"][:]))(),
                r=["lnscr"], w=["xout_d"], dma=True, tag="dbg")
    if stage >= 3:

        slopes = alibi_slopes()
        for pair in range(8):
            wb = sb[f"wbuf{pair % 2}"]
            wkey = f"wbuf{pair % 2}"
            p.emit("g", (lambda pair=pair, wb=wb: lambda e: e.dma_start(
                out=wb[:], in_=watt[pair]))(),
                r=[wkey], w=[wkey], dma=True, tag="ldw")
            for qk in range(2):
                dst = "qop" if qk == 0 else "kop"
                for hh in range(2):
                    for kt in range(8):
                        for ch in range(4):
                            psx = ps01 if ch < 2 else ps23
                            c0 = qk * 1024 + kt * 128 + hh * 64
                            p.emit("t", (lambda kt=kt, ch=ch, psx=psx, wb=wb, c0=c0:
                                         lambda e: e.matmul(
                                             psx[0:64, (ch % 2) * 512:(ch % 2) * 512 + 512],
                                             wb[:, c0:c0 + 64],
                                             sb[f"lnb{kt}"][:, ch * 512:ch * 512 + 512],
                                             start=(kt == 0), stop=(kt == 7)))(),
                                   r=[wkey, f"lnb{kt}"],
                                   w=["PS01" if ch < 2 else "PS23"], tag="qkproj")
                    for ch2 in range(2):
                        psx = ps01 if ch2 == 0 else ps23
                        p.emit("s", (lambda hh=hh, ch2=ch2, psx=psx, dst=dst:
                                     lambda e: e.activation(
                                         sb[f"{dst}{hh}"][0:64, ch2 * 1024:ch2 * 1024 + 1024],
                                         psx[0:64, 0:1024], AF.Copy))(),
                               r=["PS01" if ch2 == 0 else "PS23"],
                               w=[f"{dst}{hh}"], tag="qkevac")
            for hh in range(2):
                p.emit("v", (lambda hh=hh, pair=pair: lambda e: e.memset(
                    sb["augscal"][64:65, 0:1],
                    -32.0 * float(slopes[2 * pair + hh])))(),
                    w=["augscal"], tag="qaugc")
                p.emit("v", (lambda hh=hh: lambda e: e.tensor_scalar_mul(
                    sb[f"qop{hh}"][64:65, :], cx.const_ap["iota_row_bf"][64:65, :],
                    sb["augscal"][64:65, 0:1]))(),
                    r=["c_iota_row_bf", "augscal"], w=[f"qop{hh}"], tag="qaug")
                p.emit("v", (lambda hh=hh: lambda e: e.memset(
                    sb[f"kop{hh}"][64:65, :], 1.0))(),
                    w=[f"kop{hh}"], tag="kaug")
            for hh in range(2):
                p.emit("v", (lambda hh=hh: lambda e: e.memset(sb[f"vbuf{hh}"][:], 1.0))(),
                       w=[f"vbuf{hh}"], tag="vones")
            vfm = sb["lnftmp"].ap().bitcast(BF16)
            for ch in range(4):
                reg = (ch % 2) * 512
                for kt in range(8):
                    p.emit("t", (lambda kt=kt, ch=ch, reg=reg, wb=wb: lambda e:
                                 e.matmul(
                                     ps45[:, reg:reg + 512],
                                     wb[:, 2048 + kt * 128:2048 + kt * 128 + 128],
                                     sb[f"lnb{kt}"][:, ch * 512:ch * 512 + 512],
                                     start=(kt == 0), stop=(kt == 7)))(),
                           r=[wkey, f"lnb{kt}"], w=["PS45"], tag="vproj")
                p.emit("s", (lambda ch=ch, reg=reg, vfm=vfm: lambda e:
                             e.activation(
                                 vfm[:, ch * 512:ch * 512 + 512],
                                 ps45[:, reg:reg + 512], AF.Copy))(),
                       r=["PS45"], w=["lnftmp"], tag="vfev")
            psb23v = ps23.ap().bitcast(BF16)
            for tt in range(16):
                p.emit("t", (lambda tt=tt, vfm=vfm, psb23v=psb23v: lambda e:
                             e.transpose(
                                 psb23v[:, (tt % 4) * 128:(tt % 4) * 128 + 128],
                                 vfm[:, tt * 128:tt * 128 + 128],
                                 cx.const_ap["ident_bf"][:]))(),
                       r=["lnftmp", "c_ident_bf"], w=["PS23"], tag="vtr")
                for hh in range(2):
                    p.emit("v", (lambda tt=tt, hh=hh, psb23v=psb23v: lambda e:
                                 e.tensor_copy(
                        sb[f"vbuf{hh}"][:, tt * 65:tt * 65 + 64],
                        psb23v[:, (tt % 4) * 128 + hh * 64:
                               (tt % 4) * 128 + hh * 64 + 64]))(),
                        r=["PS23"], w=[f"vbuf{hh}"], tag="vevac")
            if stage == 31:
                for nm, r0 in (("qop0", 0), ("kop0", 128), ("vbuf0", 256)):
                    wdt = 1040 if nm == "vbuf0" else 2048
                    p.emit("v", (lambda nm=nm, wdt=wdt: lambda e: e.tensor_copy(
                        sb["lnscr"][:, 0:wdt], sb[nm][:, 0:wdt]))(),
                        r=[nm], w=["lnscr"], tag="dbg31")
                    p.emit("q", (lambda nm=nm, r0=r0, wdt=wdt: lambda e: e.dma_start(
                        out=xout[r0:r0 + 128, 0:wdt],
                        in_=sb["lnscr"][:, 0:wdt]))(),
                        r=["lnscr"], w=["xout_d"], dma=True, tag="dbg31")
                continue
            for hh in range(2):
                h = 2 * pair + hh
                qo, ko, vb = sb[f"qop{hh}"], sb[f"kop{hh}"], sb[f"vbuf{hh}"]
                qkk, kkk, vkk = f"qop{hh}", f"kop{hh}", f"vbuf{hh}"
                for qh in range(2):
                    for jt in range(8 * qh + 8):
                        jtc0 = jt * 128
                        ub = jt % 2
                        psx = ps01 if ub == 0 else ps23
                        pk = "PS01" if ub == 0 else "PS23"
                        chunks = [c for c in range(2)
                                  if 1024 * qh + 512 * c + 512 > jtc0]
                        for c in chunks:
                            p.emit("t", (lambda c=c, jtc0=jtc0, qh=qh, psx=psx,
                                         qo=qo, ko=ko: lambda e: e.matmul(
                                             psx[:, c * 512:c * 512 + 512],
                                             ko[0:65, jtc0:jtc0 + 128],
                                             qo[0:65, 1024 * qh + 512 * c:
                                                1024 * qh + 512 * c + 512],
                                             start=True, stop=True))(),
                                   r=[qkk, kkk], w=[pk], tag="smm")
                        lo = min(chunks) * 512
                        hi = max(chunks) * 512 + 512
                        p.emit("s", (lambda ub=ub, psx=psx, h=h, jt=jt, lo=lo,
                                     hi=hi: lambda e: e.activation(
                                         sb[f"usb{ub}"][:, lo:hi],
                                         psx[:, lo:hi],
                                         AF.Exp,
                                         bias=cx.const_ap["alibi_cols"]
                                         [:, h * 16 + jt:h * 16 + jt + 1],
                                         scale=1.0 / 32.0))(),
                               r=[pk, "c_alibi_cols"], w=[f"usb{ub}"], tag="exp")
                        doff = jtc0 - 1024 * qh
                        dc = doff // 512 if 0 <= doff < 1024 else -1
                        if dc >= 0:
                            off = doff % 512
                            p.emit("v", (lambda ub=ub, dc=dc, off=off: lambda e:
                                         e.select(
                                             sb[f"usbm{ub}"][:],
                                             cx.const_ap[f"cmask{off}"][:],
                                             sb[f"usb{ub}"][:, dc * 512:dc * 512 + 512],
                                             cx.const_ap["zero_bf_512"][:]))(),
                                   r=[f"usb{ub}", f"c_cmask{off}", "c_zero_bf_512"],
                                   w=[f"usbm{ub}"], tag="csel")
                        for c in chunks:
                            jt_max = min(8 * qh + 7, (1024 * qh + 512 * c + 511) // 128)
                            src_m = (c == dc)
                            p.emit("t", (lambda c=c, jt=jt, ub=ub, vb=vb,
                                         jt_max=jt_max, src_m=src_m: lambda e: e.matmul(
                                             ps45[0:65, c * 512:c * 512 + 512],
                                             vb[:, 65 * jt:65 * jt + 65],
                                             sb[f"usbm{ub}"][:] if src_m else
                                             sb[f"usb{ub}"][:, c * 512:c * 512 + 512],
                                             start=(jt == 0), stop=(jt == jt_max)))(),
                                   r=[(f"usbm{ub}" if src_m else f"usb{ub}"), vkk],
                                   w=["PS45"], tag="avmm")
                    p.emit("v", lambda e: e.tensor_copy(sb["crA"][:],
                                                        ps45[64:65, 0:1024]),
                           r=["PS45"], w=["crow"], tag="csum")
                    p.emit("s", lambda e: e.activation(sb["crB"][:],
                                                       sb["crA"][:], AF.Ln),
                           r=["crow"], w=["crow"], tag="rln")
                    p.emit("s", lambda e: e.activation(sb["crB"][:],
                                                       sb["crB"][:], AF.Exp,
                                                       scale=-1.0),
                           r=["crow"], w=["crow"], tag="rexp")
                    for c in range(2):
                        p.emit("t", (lambda c=c: lambda e: e.matmul(
                            ps67[0:64, c * 512:c * 512 + 512],
                            cx.const_ap["ones_row_f32"][0:1, 0:64],
                            sb["crB"][0:1, c * 512:c * 512 + 512],
                            start=True, stop=True))(),
                            r=["crow", "c_ones_row_f32"], w=["PS67"], tag="rbc")
                    slab = h // 2
                    r0 = (h % 2) * 64
                    p.emit("v", lambda e: e.tensor_copy(
                        sb["rcpb"][0:64, :], ps67[0:64, :]),
                        r=["PS67"], w=["rcpb"], tag="rcp")
                    p.emit("v", lambda e: e.tensor_mul(
                        sb["atmp"][0:64, :], ps45[0:64, :], sb["rcpb"][0:64, :]),
                        r=["PS45", "rcpb"], w=["atmp"], tag="oscale")
                    p.emit("g", (lambda slab=slab, r0=r0, qh=qh: lambda e:
                                 e.dma_start(
                                     out=sb[f"res{slab}"][r0:r0 + 64,
                                                          qh * 1024:qh * 1024 + 1024],
                                     in_=sb["atmp"][0:64, :],
                                     accum_op=ALU.add))(),
                           r=["atmp", f"res{slab}"], w=[f"res{slab}"], dma=True,
                           tag="resadd")


    if stage >= 4 and not (30 <= stage < 99):
        def router_hook(i):
            for tt in range(16):
                p.emit("t", (lambda i=i, tt=tt: lambda e: e.matmul(
                    ps01[:, tt * 16:tt * 16 + 16],
                    sb["lnftmp"][:, tt * 128:tt * 128 + 128],
                    sb["rwsb"][:, i * 16:i * 16 + 16],
                    start=True, stop=True))(),
                    r=["lnftmp", "rwsb"], w=["PS01"], tag="router")
            if i == 0:
                p.emit("v", lambda e: e.tensor_copy(sb["lgsb"][:], ps01[:, 0:256]),
                       r=["PS01"], w=["lgsb"], tag="lgcp")
            else:
                p.emit("v", (lambda: lambda e: e.tensor_add(
                    sb["lgsb"][:], sb["lgsb"][:], ps01[:, 0:256]))(),
                    r=["PS01", "lgsb"], w=["lgsb"], tag="lgadd")

        build_layernorm_hooked(cx, res_names, lnb_names, router_hook, tag="ln1")
        p.emit("t", lambda e: e.matmul(
            ps01[:, 0:16], cx.const_ap["ones_row_f32"][0:1, :],
            sb["rbsb"][:], start=True, stop=True),
            r=["rbsb", "c_ones_row_f32"], w=["PS01"], tag="rbias")
        for tt in range(16):
            p.emit("v", (lambda tt=tt: lambda e: e.tensor_add(
                sb["lgsb"][:, tt * 16:tt * 16 + 16],
                sb["lgsb"][:, tt * 16:tt * 16 + 16], ps01[:, 0:16]))(),
                r=["PS01", "lgsb"], w=["lgsb"], tag="rbadd")
        p.emit("q", lambda e: e.dma_start(
            out=lgout.rearrange("(tt p) e -> p tt e", p=128),
            in_=sb["lgsb"].ap().rearrange("p (tt e) -> p tt e", e=E)),
            r=["lgsb"], w=["lgout_d"], dma=True, tag="lgst")

        p.emit("v", lambda e: e.memset(sb["CR"][:], 0.0), w=["CR"], tag="cr0")
        RW, RC, RU, IDX = sb["RW"], sb["RC"], sb["RU"], sb["IDX"]
        zt = RW.ap()[:, 0:16]
        eq = RW.ap()[:, 16:32]
        ca = RW.ap()[:, 32:48]
        cb = RW.ap()[:, 48:64]
        is1 = RW.ap()[:, 64:80]
        zm = RW.ap()[:, 80:96]
        is2 = RW.ap()[:, 96:112]
        sel = RW.ap()[:, 112:128]
        pos = RW.ap()[:, 128:144]
        tmp16 = RW.ap()[:, 144:160]

        def first_max_mask(src_ap, out_ap, mcol, tagp):
            p.emit("v", (lambda: lambda e: e.reduce_max(
                RC[:, mcol:mcol + 1], src_ap, axis=AX.X))(),
                r=["RW"], w=["RC"], tag=tagp + "max")
            p.emit("v", (lambda: lambda e: e.tensor_scalar(
                eq, src_ap, RC[:, mcol:mcol + 1], None, ALU.is_equal))(),
                r=["RW", "RC"], w=["RW"], tag=tagp + "eq")
            p.emit("v", lambda e: e.tensor_copy(ca, eq), r=["RW"], w=["RW"])
            p.emit("v", lambda e: e.memset(cb, 0.0), w=["RW"])
            sh = 1
            cur, nxt = ca, cb
            for stp in range(4):
                p.emit("v", (lambda cur=cur, nxt=nxt, sh=sh: lambda e:
                             e.tensor_copy(nxt[:, 0:sh], cur[:, 0:sh]))(),
                       r=["RW"], w=["RW"])
                p.emit("v", (lambda cur=cur, nxt=nxt, sh=sh: lambda e:
                             e.tensor_add(nxt[:, sh:16], cur[:, sh:16],
                                          cur[:, 0:16 - sh]))(),
                       r=["RW"], w=["RW"])
                cur, nxt = nxt, cur
                sh *= 2
            p.emit("v", (lambda cur=cur: lambda e: e.tensor_single_scalar(
                tmp16, cur, 1.0, ALU.is_equal))(), r=["RW"], w=["RW"])
            p.emit("v", (lambda: lambda e: e.tensor_mul(out_ap, eq, tmp16))(),
                   r=["RW"], w=["RW"], tag=tagp + "is")

        for tt in range(16):
            lg_t = sb["lgsb"].ap()[:, tt * 16:tt * 16 + 16]
            p.emit("v", (lambda lg_t=lg_t: lambda e: e.reduce_max(
                RC[:, 0:1], lg_t, axis=AX.X))(), r=["lgsb"], w=["RC"], tag="rmax")
            p.emit("v", (lambda: lambda e: e.tensor_scalar_mul(
                RC[:, 1:2], RC[:, 0:1], -1.0))(), r=["RC"], w=["RC"])
            p.emit("s", (lambda lg_t=lg_t: lambda e: e.activation(
                zt, lg_t, AF.Exp, bias=RC[:, 1:2]))(),
                r=["lgsb", "RC"], w=["RW"], tag="zexp")
            p.emit("v", (lambda: lambda e: e.reduce_sum(
                RC[:, 2:3], zt, axis=AX.X))(), r=["RW"], w=["RC"], tag="zsum")
            first_max_mask(zt, is1, 3, "t1")
            p.emit("v", (lambda: lambda e: e.scalar_tensor_tensor(
                zm, is1, -1e30, zt, ALU.mult, ALU.add))(), r=["RW"], w=["RW"])
            first_max_mask(zm, is2, 4, "t2")
            p.emit("v", lambda e: e.tensor_add(sel, is1, is2), r=["RW"], w=["RW"])
            p.emit("v", (lambda: lambda e: e.tensor_add(
                RC[:, 5:6], RC[:, 3:4], RC[:, 4:5]))(), r=["RC"], w=["RC"])
            p.emit("v", (lambda: lambda e: e.scalar_tensor_tensor(
                RC[:, 5:6], RC[:, 2:3], 1e-9, RC[:, 5:6], ALU.mult, ALU.add))(),
                r=["RC"], w=["RC"])
            p.emit("t", (lambda: lambda e: e.matmul(
                ps45[:, 0:16], cx.const_ap["lt_incl_f32"][:], sel,
                start=True, stop=False))(),
                r=["RW", "c_lt_incl_f32"], w=["PS45"], tag="posmm")
            p.emit("t", (lambda: lambda e: e.matmul(
                ps45[:, 0:16], cx.const_ap["ones_row_f32"][0:1, :], sb["CR"][:],
                start=False, stop=True))(),
                r=["CR", "c_ones_row_f32"], w=["PS45"], tag="poscr")
            p.emit("v", (lambda: lambda e: e.tensor_copy(pos, ps45[:, 0:16]))(),
                   r=["PS45"], w=["RW"], tag="posev")
            p.emit("t", (lambda: lambda e: e.matmul(
                ps67[0:1, 0:16], cx.const_ap["ones_col_f32"][:], sel,
                start=True, stop=True))(),
                r=["RW", "c_ones_col_f32"], w=["PS67"], tag="cntmm")
            p.emit("v", (lambda: lambda e: e.tensor_add(
                sb["CR"][:], sb["CR"][:], ps67[0:1, 0:16]))(),
                r=["PS67", "CR"], w=["CR"], tag="cradd")
            psb = ps23.ap().bitcast(BF16)
            for i in range(8):
                p.emit("t", (lambda i=i, tt=tt, psb=psb: lambda e: e.transpose(
                    psb[:, i * 128:i * 128 + 128],
                    sb[f"lnb{i}"][:, tt * 128:tt * 128 + 128],
                    cx.const_ap["ident_bf"][:]))(),
                    r=[f"lnb{i}"], w=["PS23"], tag="trln")
            p.emit("v", (lambda psb=psb: lambda e: e.tensor_copy(
                sb["lntm"][:], psb[:, 0:1024]))(),
                r=["PS23"], w=["lntm"], tag="trev")
            for k in range(2):
                ismask = is1 if k == 0 else is2
                p.emit("v", (lambda ismask=ismask: lambda e: e.tensor_mul(
                    tmp16, ismask, cx.const_ap["iota16_f32"][:]))(),
                    r=["RW", "c_iota16_f32"], w=["RW"])
                p.emit("v", (lambda k=k: lambda e: e.reduce_sum(
                    RC[:, 6 + k:7 + k], tmp16, axis=AX.X))(), r=["RW"], w=["RC"])
                p.emit("v", (lambda ismask=ismask: lambda e: e.tensor_mul(
                    tmp16, ismask, pos))(), r=["RW"], w=["RW"])
                p.emit("v", (lambda k=k: lambda e: e.reduce_sum(
                    RC[:, 8 + k:9 + k], tmp16, axis=AX.X))(), r=["RW"], w=["RC"])
                p.emit("v", (lambda k=k: lambda e: e.tensor_scalar_add(
                    RC[:, 8 + k:9 + k], RC[:, 8 + k:9 + k], -1.0))(),
                    r=["RC"], w=["RC"])
                p.emit("v", (lambda k=k: lambda e: e.scalar_tensor_tensor(
                    RC[:, 10 + k:11 + k], RC[:, 6 + k:7 + k], float(BLK),
                    RC[:, 8 + k:9 + k], ALU.mult, ALU.add))(), r=["RC"], w=["RC"])
                p.emit("v", (lambda k=k: lambda e: e.tensor_single_scalar(
                    RC[:, 12 + k:13 + k], RC[:, 8 + k:9 + k], float(BLK),
                    ALU.is_lt))(), r=["RC"], w=["RC"])
                p.emit("v", (lambda k=k: lambda e: e.tensor_copy(
                    RU[:, k:k + 1], RC[:, 12 + k:13 + k]))(), r=["RC"], w=["RU"])
                p.emit("v", (lambda k=k, tt=tt: lambda e: e.tensor_scalar(
                    RC[:, 14:15], cx.const_ap["iota_col_f32"][:], 2.0,
                    float(E * BLK + 256 * tt + k), ALU.mult, ALU.add))(),
                    r=["c_iota_col_f32"], w=["RC"])
                p.emit("v", (lambda k=k: lambda e: e.select(
                    RC[:, 15:16], RU[:, k:k + 1], RC[:, 10 + k:11 + k],
                    RC[:, 14:15]))(), r=["RC", "RU"], w=["RC"])
                p.emit("v", (lambda k=k: lambda e: e.tensor_copy(
                    IDX[:, k:k + 1], RC[:, 15:16]))(), r=["RC"], w=["IDX"])
                p.emit("g", (lambda k=k: lambda e: e.indirect_dma_start(
                    out=blkout, out_offset=IOA(IDX[:, k:k + 1], 0),
                    in_=sb["lntm"][:], in_offset=None))(),
                    r=["IDX", "lntm"], w=["blk_d"], dma=True, tag="scat")
            p.barrier(engs=("v",), on=("g",))


    for i in range(8 if stage not in (2, 30, 31, 32) else 0):
        p.emit("q", (lambda i=i: lambda e: e.dma_start(
            out=xout[i * 128:(i + 1) * 128, :], in_=sb[f"res{i}"][:]))(),
            r=[f"res{i}"], w=["xout_d"], dma=True, tag="xst")
    p.barrier(engs=("q",), on=("q", "g"))
    p.barrier(engs=("g",), on=("g",))
    return cx


def build_layernorm_hooked(cx, res_names, out_bf, hook, tag="ln"):
    p, sb = cx.p, cx.sb
    ones = cx.const_ap["ones_col_f32"]
    onesr = cx.const_ap["ones_row_f32"]
    s45, s67 = cx.ps["PS45"], cx.ps["PS67"]
    scr = sb["lnscr"]
    stM, stV = sb["stM"], sb["stV"]
    for half in range(2):
        hs = slice(half * 1024, half * 1024 + 1024)
        for i in range(8):
            for q in range(2):
                qs = slice(half * 1024 + q * 512, half * 1024 + q * 512 + 512)
                p.emit("t", (lambda i=i, q=q, qs=qs: lambda e:
                             e.matmul(s45[0:1, q * 512:q * 512 + 512],
                                      ones[:], sb[res_names[i]][:, qs],
                                      start=(i == 0), stop=(i == 7)))(),
                       r=[res_names[i], "c_ones_col_f32"], w=["PS45"], tag=tag + "su")
        for i in range(8):
            p.emit("s", (lambda i=i, hs=hs: lambda e:
                         e.activation(scr[:, 0:1024], sb[res_names[i]][:, hs],
                                      AF.Square))(),
                   r=[res_names[i]], w=["lnscr"], tag=tag + "sq")
            for q in range(2):
                p.emit("t", (lambda i=i, q=q: lambda e:
                             e.matmul(s67[0:1, q * 512:q * 512 + 512],
                                      ones[:], scr[:, q * 512:q * 512 + 512],
                                      start=(i == 0), stop=(i == 7)))(),
                       r=["lnscr", "c_ones_col_f32"], w=["PS67"], tag=tag + "s2")
        p.emit("v", (lambda hs=hs: lambda e:
                     e.tensor_scalar_mul(stM[0:1, hs], s45[0:1, 0:1024],
                                         -1.0 / 1024))(),
               r=["PS45"], w=["lnstats"], tag=tag + "mean")
        p.emit("v", (lambda hs=hs: lambda e:
                     e.tensor_scalar_mul(stV[0:1, hs], s67[0:1, 0:1024],
                                         1.0 / 1024))(),
               r=["PS67"], w=["lnstats"], tag=tag + "msq")
    p.emit("v", lambda e: e.tensor_mul(scr[0:1, 0:2048], stM[:], stM[:]),
           r=["lnstats"], w=["lnscr"], tag=tag + "m2")
    p.emit("v", lambda e: e.tensor_sub(stV[:], stV[:], scr[0:1, 0:2048]),
           r=["lnstats", "lnscr"], w=["lnstats"], tag=tag + "var")
    p.emit("v", lambda e: e.tensor_scalar_add(stV[:], stV[:], 1e-5),
           r=["lnstats"], w=["lnstats"], tag=tag + "eps")
    p.emit("s", lambda e: e.activation(stV[:], stV[:], AF.Ln),
           r=["lnstats"], w=["lnstats"], tag=tag + "ln")
    p.emit("s", lambda e: e.activation(stV[:], stV[:], AF.Exp, scale=-0.5),
           r=["lnstats"], w=["lnstats"], tag=tag + "exp")
    p.emit("v", lambda e: e.tensor_mul(stM[:], stM[:], stV[:]),
           r=["lnstats"], w=["lnstats"], tag=tag + "nmr")
    rb, nb = sb["ln_rb"], sb["ln_nb"]
    for half in range(2):
        for q in range(2):
            qs = slice(half * 1024 + q * 512, half * 1024 + q * 512 + 512)
            p.emit("t", (lambda qs=qs, q=q: lambda e:
                         e.matmul(s45[:, q * 512:q * 512 + 512], onesr[0:1, :],
                                  stV[0:1, qs], start=True, stop=True))(),
                   r=["lnstats", "c_ones_row_f32"], w=["PS45"], tag=tag + "bc")
            p.emit("t", (lambda qs=qs, q=q: lambda e:
                         e.matmul(s67[:, q * 512:q * 512 + 512], onesr[0:1, :],
                                  stM[0:1, qs], start=True, stop=True))(),
                   r=["lnstats", "c_ones_row_f32"], w=["PS67"], tag=tag + "bc")
        p.emit("v", (lambda half=half: lambda e:
                     e.tensor_copy(rb[:, half * 1024:half * 1024 + 1024],
                                   s45[:, 0:1024]))(),
               r=["PS45"], w=["ln_rb"], tag=tag + "cb")
        p.emit("v", (lambda half=half: lambda e:
                     e.tensor_copy(nb[:, half * 1024:half * 1024 + 1024],
                                   s67[:, 0:1024]))(),
               r=["PS67"], w=["ln_nb"], tag=tag + "cb")
    for i in range(8):
        p.emit("v", (lambda i=i: lambda e:
                     e.tensor_mul(scr[:], sb[res_names[i]][:], rb[:]))(),
               r=[res_names[i], "ln_rb"], w=["lnscr"], tag=tag + "nm")
        p.emit("v", (lambda i=i: lambda e:
                     e.tensor_add(sb[out_bf[i]][:], scr[:], nb[:]))(),
               r=["lnscr", "ln_nb"], w=[out_bf[i]], tag=tag + "na")
        p.emit("v", (lambda i=i: lambda e:
                     e.tensor_add(sb["lnftmp"][:], scr[:], nb[:]))(),
               r=["lnscr", "ln_nb"], w=["lnftmp"], tag=tag + "nf")
        hook(i)



def _prep_watt(w_kqv):
    w = np.asarray(w_kqv, np.float32)
    out = np.zeros((8, 128, 3072), np.float32)
    for pair in range(8):
        cp = 128 * pair
        for kt in range(8):
            rows = slice(128 * kt, 128 * kt + 128)
            cc = kt * 128
            out[pair, :, cc:cc + 128] = w[rows, 1024 + cp:1024 + cp + 128]
            out[pair, :, 1024 + cc:1024 + cc + 128] = w[rows, cp:cp + 128]
            out[pair, :, 2048 + cc:2048 + cc + 128] = w[rows, 2048 + cp:2048 + cp + 128]
    return out


def run_launch_a(inputs, stage=99):
    consts = host_constants()
    cx = build_launch_a(consts, stage=stage)
    cx.p.seal(cx.es)
    cx.es.close()
    x = np.asarray(inputs["L0_attn_x"], np.float32)
    watt = _prep_watt(inputs["L0_attn_w_kqv"])
    rw = np.ascontiguousarray(np.asarray(inputs["L0_ffn_router_w"], np.float32))
    rb = np.asarray(inputs["L0_ffn_router_b"], np.float32).reshape(1, E)
    in_maps = [dict(x=np.ascontiguousarray(x[c]), watt=watt, rw=rw, rb=rb)
               for c in range(NCORES)]
    return run_bass_kernel_spmd(cx.nc, in_maps, core_ids=list(range(NCORES)))



def build_launch_b(consts):
    cx = Ctx("B")
    nc, p = cx.nc, cx.p
    lgf = nc.dram_tensor("lgfull", [NCORES * S, E], F32, kind="ExternalInput").ap()
    xblk = nc.dram_tensor("xblk", [2 * NCORES * BLK, D], BF16,
                          kind="ExternalInput").ap()
    wg_in = nc.dram_tensor("wg", [2, 11, 128, 1024], F32,
                           kind="ExternalInput").ap()
    wu_in = nc.dram_tensor("wu", [2, 11, 128, 1024], F32,
                           kind="ExternalInput").ap()
    wd_in = nc.dram_tensor("wd", [2, F, D], F32, kind="ExternalInput").ap()
    yblk = nc.dram_tensor("yblk", [2 * NCORES * BLK + 128, D], BF16,
                          kind="ExternalOutput").ap()
    esel_in = [nc.dram_tensor(f"eselin{i}", [1, 128], F32,
                              kind="ExternalInput").ap() for i in range(2)]

    sb = cx.sb
    cx.sbuf("lgt", (128, 128 * E), F32)
    cx.sbuf("RW", (128, 10 * E), F32)
    cx.sbuf("RC", (128, 16), F32)
    cx.sbuf("CNT", (1, 128), F32)
    cx.sbuf("RWB", (1, 128), F32)
    cx.sbuf("esel0", (1, 128), F32)
    cx.sbuf("esel1", (1, 128), F32)
    cx.sbuf("OFF", (128, 8), F32)
    cx.sbuf("CO", (1, 64), F32)
    cx.sbuf("COEF", (128, 8), F32)
    cx.sbuf("TCOL", (128, 1), F32)
    cx.sbuf("GID", (128, 1), F32)
    cx.sbuf("M8", (128, 8), F32)
    cx.sbuf("IDXF", (128, 2), F32)
    cx.sbuf("IDX", (128, 20 * 2), U32)
    cx.sbuf("RU", (128, 1), U8)
    cx.sbuf("xg", (128, 4 * 1024), BF16)
    cx.sbuf("xfm", (128, 8 * 512), BF16)
    cx.sbuf("hfm", (128, 11 * 512), BF16)
    cx.sbuf("ht", (128, 512), BF16)
    cx.sbuf("yout", (128, 4 * 1024), BF16)
    for i in range(2):
        cx.sbuf(f"wgsb{i}", (128, 8 * 128), BF16)
        cx.sbuf(f"wusb{i}", (128, 8 * 128), BF16)
        cx.sbuf(f"wdsb{i}", (128, 1024), BF16)
    ps01 = cx.psum("PS01", (128, 1024))
    ps23 = cx.psum("PS23", (128, 1024))
    ps45 = cx.psum("PS45", (128, 1024))
    ps67 = cx.psum("PS67", (128, 1024))
    cx.load_consts(consts, ["ones_col_f32", "ones_row_f32", "ident_bf",
                            "iota_col_f32", "iota16_f32"])

    p.emit("q", lambda e: e.dma_start(
        out=sb["lgt"].ap().rearrange("p (tt e) -> p tt e", e=E),
        in_=lgf.rearrange("(tt p) e -> p tt e", p=128)),
        w=["lgt"], dma=True)
    for i in range(2):
        p.emit("q", (lambda i=i: lambda e: e.dma_start(
            out=sb[f"esel{i}"][:], in_=esel_in[i]))(), w=["esel"], dma=True)

    RW, RC = sb["RW"], sb["RC"]
    zt = RW.ap()[:, 0:16]
    eq = RW.ap()[:, 16:32]
    ca = RW.ap()[:, 32:48]
    cb = RW.ap()[:, 48:64]
    is1 = RW.ap()[:, 64:80]
    zm = RW.ap()[:, 80:96]
    is2 = RW.ap()[:, 96:112]
    sel = RW.ap()[:, 112:128]
    tmp16 = RW.ap()[:, 144:160]

    def first_max(src_ap, out_ap, mcol):
        p.emit("v", (lambda: lambda e: e.reduce_max(
            RC[:, mcol:mcol + 1], src_ap, axis=AX.X))(), r=["RW"], w=["RC"])
        p.emit("v", (lambda: lambda e: e.tensor_scalar(
            eq, src_ap, RC[:, mcol:mcol + 1], None, ALU.is_equal))(),
            r=["RW", "RC"], w=["RW"])
        p.emit("v", lambda e: e.tensor_copy(ca, eq), r=["RW"], w=["RW"])
        sh = 1
        cur, nxt = ca, cb
        for stp in range(4):
            p.emit("v", (lambda cur=cur, nxt=nxt, sh=sh: lambda e:
                         e.tensor_copy(nxt[:, 0:sh], cur[:, 0:sh]))(),
                   r=["RW"], w=["RW"])
            p.emit("v", (lambda cur=cur, nxt=nxt, sh=sh: lambda e:
                         e.tensor_add(nxt[:, sh:16], cur[:, sh:16],
                                      cur[:, 0:16 - sh]))(),
                   r=["RW"], w=["RW"])
            cur, nxt = nxt, cur
            sh *= 2
        p.emit("v", (lambda cur=cur: lambda e: e.tensor_single_scalar(
            tmp16, cur, 1.0, ALU.is_equal))(), r=["RW"], w=["RW"])
        p.emit("v", (lambda: lambda e: e.tensor_mul(out_ap, eq, tmp16))(),
               r=["RW"], w=["RW"])

    for tt in range(128):
        c = tt // 16
        lg_t = sb["lgt"].ap()[:, tt * 16:tt * 16 + 16]
        p.emit("v", (lambda lg_t=lg_t: lambda e: e.reduce_max(
            RC[:, 0:1], lg_t, axis=AX.X))(), r=["lgt"], w=["RC"])
        p.emit("v", (lambda: lambda e: e.tensor_scalar_mul(
            RC[:, 1:2], RC[:, 0:1], -1.0))(), r=["RC"], w=["RC"])
        p.emit("s", (lambda lg_t=lg_t: lambda e: e.activation(
            zt, lg_t, AF.Exp, bias=RC[:, 1:2]))(), r=["lgt", "RC"], w=["RW"])
        first_max(zt, is1, 3)
        p.emit("v", (lambda: lambda e: e.scalar_tensor_tensor(
            zm, is1, -1e30, zt, ALU.mult, ALU.add))(), r=["RW"], w=["RW"])
        first_max(zm, is2, 4)
        p.emit("v", lambda e: e.tensor_add(sel, is1, is2), r=["RW"], w=["RW"])
        p.emit("t", (lambda c=c, tt=tt: lambda e: e.matmul(
            ps67[0:1, c * 16:c * 16 + 16], cx.const_ap["ones_col_f32"][:], sel,
            start=(tt % 16 == 0), stop=(tt % 16 == 15)))(),
            r=["RW", "c_ones_col_f32"], w=["PS67"], tag="cnt")
    p.emit("v", lambda e: e.tensor_copy(sb["CNT"][:], ps67[0:1, 0:128]),
           r=["PS67"], w=["CNT"], tag="cntcp")

    CO = sb["CO"]
    for ei in range(2):
        p.emit("v", (lambda ei=ei: lambda e: e.tensor_mul(
            sb["RWB"][:], sb["CNT"][:], sb[f"esel{ei}"][:]))(),
            r=["CNT", "esel"], w=["RWB"])
        p.emit("v", (lambda: lambda e: e.reduce_sum(
            CO[0:1, 0:8],
            sb["RWB"].ap()[0:1, 0:128].rearrange("a (c e) -> a c e", e=16),
            axis=AX.X))(), r=["RWB"], w=["CO"])
        p.emit("v", lambda e: e.memset(CO[0:1, 8:16], 0.0), w=["CO"])
        for c in range(1, 8):
            p.emit("v", (lambda c=c: lambda e: e.tensor_add(
                CO[0:1, 8 + c:9 + c], CO[0:1, 7 + c:8 + c],
                CO[0:1, c - 1:c]))(), r=["CO"], w=["CO"])
        p.emit("v", lambda e: e.tensor_add(CO[0:1, 16:17], CO[0:1, 15:16],
                                           CO[0:1, 7:8]), r=["CO"], w=["CO"])
        p.emit("v", lambda e: e.memset(CO[0:1, 24:25], 640.0), w=["CO"])
        for c in range(1, 8):
            p.emit("v", (lambda c=c: lambda e: e.tensor_scalar(
                CO[0:1, 24 + c:25 + c], CO[0:1, c - 1:c], -1.0, 640.0,
                ALU.mult, ALU.add))(), r=["CO"], w=["CO"])
        for c in range(8):
            p.emit("t", (lambda c=c: lambda e: e.matmul(
                ps45[:, c:c + 1], cx.const_ap["ones_row_f32"][0:1, :],
                CO[0:1, 8 + c:9 + c], start=True, stop=True))(),
                r=["CO", "c_ones_row_f32"], w=["PS45"])
            p.emit("t", (lambda c=c: lambda e: e.matmul(
                ps45[:, 8 + c:9 + c], cx.const_ap["ones_row_f32"][0:1, :],
                CO[0:1, 24 + c:25 + c], start=True, stop=True))(),
                r=["CO", "c_ones_row_f32"], w=["PS45"])
        p.emit("t", lambda e: e.matmul(
            ps45[:, 16:17], cx.const_ap["ones_row_f32"][0:1, :],
            CO[0:1, 16:17], start=True, stop=True),
            r=["CO", "c_ones_row_f32"], w=["PS45"])
        p.emit("v", lambda e: e.tensor_copy(sb["OFF"][:], ps45[:, 0:8]),
               r=["PS45"], w=["OFF"])
        p.emit("v", lambda e: e.tensor_copy(sb["COEF"][:], ps45[:, 8:16]),
               r=["PS45"], w=["COEF"])
        p.emit("v", lambda e: e.tensor_copy(sb["TCOL"][:], ps45[:, 16:17]),
               r=["PS45"], w=["TCOL"])
        for j in range(20):
            p.emit("v", (lambda j=j: lambda e: e.tensor_scalar_add(
                sb["GID"][:], cx.const_ap["iota_col_f32"][:], float(128 * j)))(),
                r=["c_iota_col_f32"], w=["GID"])
            p.emit("v", (lambda: lambda e: e.tensor_scalar(
                sb["M8"][:], sb["OFF"][:], sb["GID"][:], None, ALU.is_le))(),
                r=["OFF", "GID"], w=["M8"])
            p.emit("v", (lambda: lambda e: e.tensor_mul(
                sb["M8"][:], sb["M8"][:], sb["COEF"][:]))(),
                r=["M8", "COEF"], w=["M8"])
            p.emit("v", (lambda: lambda e: e.reduce_sum(
                sb["IDXF"][:, 0:1], sb["M8"][:], axis=AX.X))(),
                r=["M8"], w=["IDXF"])
            base = ei * NCORES * BLK - BLK
            p.emit("v", (lambda base=base: lambda e: e.scalar_tensor_tensor(
                sb["IDXF"][:, 0:1], sb["GID"][:], 1.0, sb["IDXF"][:, 0:1],
                ALU.mult, ALU.add))(), r=["GID", "IDXF"], w=["IDXF"])
            p.emit("v", (lambda base=base: lambda e: e.tensor_scalar_add(
                sb["IDXF"][:, 0:1], sb["IDXF"][:, 0:1], float(base)))(),
                r=["IDXF"], w=["IDXF"])
            p.emit("v", (lambda: lambda e: e.tensor_scalar(
                sb["IDXF"][:, 1:2], sb["GID"][:], sb["TCOL"][:], None,
                ALU.is_lt))(), r=["GID", "TCOL"], w=["IDXF"])
            p.emit("v", lambda e: e.tensor_copy(sb["RU"][:], sb["IDXF"][:, 1:2]),
                   r=["IDXF"], w=["RU"])
            p.emit("v", (lambda: lambda e: e.tensor_scalar_add(
                sb["GID"][:], cx.const_ap["iota_col_f32"][:],
                float(2 * NCORES * BLK)))(), r=["c_iota_col_f32"], w=["GID"])
            p.emit("v", (lambda: lambda e: e.select(
                sb["IDXF"][:, 0:1], sb["RU"][:], sb["IDXF"][:, 0:1],
                sb["GID"][:]))(), r=["IDXF", "RU", "GID"], w=["IDXF"])
            p.emit("v", (lambda j=j, ei=ei: lambda e: e.tensor_copy(
                sb["IDX"][:, ei * 20 + j:ei * 20 + j + 1],
                sb["IDXF"][:, 0:1]))(), r=["IDXF"], w=["IDX"])
        for grp in range(5):
            for tj in range(4):
                j = grp * 4 + tj
                p.emit("g", (lambda j=j, tj=tj, ei=ei: lambda e:
                             e.indirect_dma_start(
                                 out=sb["xg"][:, tj * 1024:tj * 1024 + 1024],
                                 out_offset=None, in_=xblk,
                                 in_offset=IOA(sb["IDX"][:, ei * 20 + j:
                                                         ei * 20 + j + 1], 0)))(),
                       r=["IDX"], w=["xg"], dma=True, tag="gat")
            for tj in range(4):
                psb = (ps01 if tj % 2 == 0 else ps23).ap().bitcast(BF16)
                pk = "PS01" if tj % 2 == 0 else "PS23"
                for kt in range(8):
                    p.emit("t", (lambda tj=tj, kt=kt, psb=psb: lambda e:
                                 e.transpose(
                                     psb[:, kt * 128:kt * 128 + 128],
                                     sb["xg"][:, tj * 1024 + kt * 128:
                                              tj * 1024 + kt * 128 + 128],
                                     cx.const_ap["ident_bf"][:]))(),
                           r=["xg", "c_ident_bf"], w=[pk], tag="xtr")
                p.emit("v", (lambda tj=tj, psb=psb: lambda e: e.tensor_copy(
                    sb["xfm"].ap().rearrange("p (k t) -> p k t", t=512)
                    [:, 0:8, tj * 128:tj * 128 + 128],
                    psb.rearrange("p (k t) -> p k t", t=128)[:, 0:8, :]))(),
                    r=[pk], w=["xfm"], tag="xtre")
            for f in range(11):
                gn, un = f"wgsb{f % 2}", f"wusb{f % 2}"
                for wnm, win in ((gn, wg_in), (un, wu_in)):
                    p.emit("g", (lambda wnm=wnm, win=win, f=f, ei=ei: lambda e:
                                 e.dma_start(
                                     out=sb[wnm][:], in_=win[ei][f]))(),
                           r=[wnm], w=[wnm], dma=True, tag="ldw")
                for kt in range(8):
                    p.emit("t", (lambda kt=kt, gn=gn: lambda e: e.matmul(
                        ps45[:, 0:512], sb[gn][:, kt * 128:kt * 128 + 128],
                        sb["xfm"][:, kt * 512:kt * 512 + 512],
                        start=(kt == 0), stop=(kt == 7)))(),
                        r=[gn, "xfm"], w=["PS45"], tag="gmm")
                    p.emit("t", (lambda kt=kt, un=un: lambda e: e.matmul(
                        ps67[:, 0:512], sb[un][:, kt * 128:kt * 128 + 128],
                        sb["xfm"][:, kt * 512:kt * 512 + 512],
                        start=(kt == 0), stop=(kt == 7)))(),
                        r=[un, "xfm"], w=["PS67"], tag="umm")
                p.emit("s", (lambda f=f: lambda e: e.activation(
                    sb["ht"][:], ps45[:, 0:512], AF.Silu))(),
                    r=["PS45"], w=["ht"], tag="silu")
                p.emit("v", (lambda f=f: lambda e: e.tensor_mul(
                    sb["hfm"][:, f * 512:f * 512 + 512], sb["ht"][:],
                    ps67[:, 0:512]))(), r=["ht", "PS67"], w=["hfm"], tag="hmul")
            PSL = [ps01, ps23, ps45, ps67]
            PKL = ["PS01", "PS23", "PS45", "PS67"]
            for f in range(11):
                dn = f"wdsb{f % 2}"
                p.emit("g", (lambda f=f, ei=ei, dn=dn: lambda e: e.dma_start(
                    out=sb[dn][:], in_=wd_in[ei][f * 128:f * 128 + 128, :]))(),
                    r=[dn], w=[dn], dma=True, tag="lwd")
                for tj in range(4):
                    psY = PSL[tj]
                    pk = PKL[tj]
                    for ch in range(2):
                        p.emit("t", (lambda f=f, tj=tj, ch=ch, psY=psY, dn=dn:
                                     lambda e: e.matmul(
                                         psY[:, ch * 512:ch * 512 + 512],
                                         sb["hfm"][:, f * 512 + tj * 128:
                                                   f * 512 + tj * 128 + 128],
                                         sb[dn][:, ch * 512:ch * 512 + 512],
                                         start=(f == 0), stop=(f == 10)))(),
                               r=["hfm", dn], w=[pk], tag="dmm")
            for tj in range(4):
                psY = PSL[tj]
                pk = PKL[tj]
                p.emit("s", (lambda tj=tj, psY=psY: lambda e: e.activation(
                    sb["yout"][:, tj * 1024:tj * 1024 + 1024], psY[:],
                    AF.Copy))(), r=[pk], w=["yout"], tag="yev")
                j = grp * 4 + tj
                p.emit("g", (lambda j=j, tj=tj, ei=ei: lambda e:
                             e.indirect_dma_start(
                                 out=yblk, out_offset=IOA(
                                     sb["IDX"][:, ei * 20 + j:ei * 20 + j + 1], 0),
                                 in_=sb["yout"][:, tj * 1024:tj * 1024 + 1024],
                                 in_offset=None))(),
                       r=["IDX", "yout"], w=["yblk_d"], dma=True, tag="scat")
    p.barrier(engs=("g",), on=("g",))
    p.barrier(engs=("q",), on=("q", "g"))
    return cx


def run_launch_b(inputs, lgfull, ablk):
    consts = host_constants()
    cx = build_launch_b(consts)
    cx.p.seal(cx.es)
    cx.es.close()
    wg = np.asarray(inputs["L0_ffn_w_gate"], np.float32)
    wu = np.asarray(inputs["L0_ffn_w_up"], np.float32)
    wd = np.asarray(inputs["L0_ffn_w_down"], np.float32)
    in_maps = []
    for pcore in range(NCORES):
        ea, eb = 2 * pcore, 2 * pcore + 1
        xb = np.concatenate(
            [np.concatenate([ablk[c][e * BLK:(e + 1) * BLK] for c in range(8)], 0)
             for e in (ea, eb)], 0)
        es0 = np.zeros((1, 16), np.float32); es0[0, ea] = 1.0
        es1 = np.zeros((1, 16), np.float32); es1[0, eb] = 1.0
        es0 = np.tile(es0, (1, 8)); es1 = np.tile(es1, (1, 8))
        pkb = lambda w: np.ascontiguousarray(
            w.reshape(2, 8, 128, 11, 128).transpose(0, 3, 2, 1, 4)
            .reshape(2, 11, 128, 1024))
        in_maps.append(dict(
            eselin0=es0, eselin1=es1,
            lgfull=lgfull, xblk=np.ascontiguousarray(xb),
            wg=pkb(wg[ea:eb + 1]),
            wu=pkb(wu[ea:eb + 1]),
            wd=np.ascontiguousarray(wd[ea:eb + 1])))
    return run_bass_kernel_spmd(cx.nc, in_maps, core_ids=list(range(NCORES)))



def build_launch_c(consts, stage=99, cx=None, ext=None):
    merged = ext is not None
    if cx is None:
        cx = Ctx("C")
    nc, p = cx.nc, cx.p
    if merged:
        xin, lgown, yin = ext["xin"], ext["lgown"], ext["yin"]
        swg_in, swu_in, swd_in = ext["swg"], ext["swu"], ext["swd"]
        xmid = None
    else:
        xin = nc.dram_tensor("xin", [D, S], F32, kind="ExternalInput").ap()
        lgown = nc.dram_tensor("lgown", [S, E], F32, kind="ExternalInput").ap()
        yin = nc.dram_tensor("yin", [E * BLK, D], BF16,
                             kind="ExternalInput").ap()
        swg_in = nc.dram_tensor("swg", [11, 128, 1024], F32,
                                kind="ExternalInput").ap()
        swu_in = nc.dram_tensor("swu", [11, 128, 1024], F32,
                                kind="ExternalInput").ap()
        swd_in = nc.dram_tensor("swd", [F, D], F32, kind="ExternalInput").ap()
        xmid = nc.dram_tensor("xmid", [D, S], F32, kind="ExternalOutput").ap()

    sb = cx.sb
    for i in range(8):
        cx.sbuf(f"res{i}", (128, S), F32)
    for i in range(8):
        cx.sbuf(f"lnb{i}", (128, S), BF16)
    cx.sbuf("lnscr", (128, S), F32)
    cx.sbuf("ln_rb", (128, S), F32)
    cx.sbuf("ln_nb", (128, S), F32)
    cx.sbuf("stM", (1, S), F32)
    cx.sbuf("stV", (1, S), F32)
    cx.sbuf("lnftmp", (128, S), F32)
    cx.sbuf("lgsb", (128, 16 * E), F32)
    cx.sbuf("RW", (128, 10 * E), F32)
    cx.sbuf("RC", (128, 16), F32)
    cx.sbuf("RU", (128, 4), U8)
    cx.sbuf("IDX", (128, 4), U32)
    cx.sbuf("CR", (1, E), F32)
    cx.sbuf("yg", (128, 2 * 1024), BF16)
    cx.sbuf("mo", (128, 1024), F32)
    ps01 = cx.psum("PS01", (128, 1024))
    ps23 = cx.psum("PS23", (128, 1024))
    ps45 = cx.psum("PS45", (128, 1024))
    ps67 = cx.psum("PS67", (128, 1024))
    cx.load_consts(consts, ["ones_col_f32", "ones_row_f32", "ident_f32",
                            "ident_bf", "iota_col_f32", "iota16_f32",
                            "lt_incl_f32"])

    res_names = [f"res{i}" for i in range(8)]
    lnb_names = [f"lnb{i}" for i in range(8)]

    for i in range(8):
        p.emit("q", (lambda i=i: lambda e: e.dma_start(
            out=sb[f"res{i}"][:], in_=xin[i * 128:(i + 1) * 128, :]))(),
            w=[f"res{i}"], dma=True, tag="ldx")
    p.emit("q", lambda e: e.dma_start(
        out=sb["lgsb"].ap().rearrange("p (tt e) -> p tt e", e=E),
        in_=lgown.rearrange("(tt p) e -> p tt e", p=128)),
        w=["lgsb"], dma=True)

    build_layernorm(cx, res_names, lnb_names, tag="c1")
    for i in range(8):
        p.emit("v", (lambda i=i: lambda e: e.tensor_mul(
            sb["lnscr"][:], sb[f"res{i}"][:], sb["ln_rb"][:]))(),
            r=[f"res{i}", "ln_rb"], w=["lnscr"], tag="lt1")
        p.emit("v", (lambda i=i: lambda e: e.tensor_add(
            sb["lnscr"][:], sb["lnscr"][:], sb["ln_nb"][:]))(),
            r=["lnscr", "ln_nb"], w=["lnscr"], tag="lt2")
        p.emit("v", (lambda i=i: lambda e: e.tensor_add(
            sb[f"res{i}"][:], sb[f"res{i}"][:], sb["lnscr"][:]))(),
            r=["lnscr", f"res{i}"], w=[f"res{i}"], tag="lt3")

    RW, RC, RU, IDX = sb["RW"], sb["RC"], sb["RU"], sb["IDX"]
    zt = RW.ap()[:, 0:16]
    eq = RW.ap()[:, 16:32]
    ca = RW.ap()[:, 32:48]
    cb = RW.ap()[:, 48:64]
    is1 = RW.ap()[:, 64:80]
    zm = RW.ap()[:, 80:96]
    is2 = RW.ap()[:, 96:112]
    sel = RW.ap()[:, 112:128]
    pos = RW.ap()[:, 128:144]
    tmp16 = RW.ap()[:, 144:160]

    def first_max(src_ap, out_ap, mcol):
        p.emit("v", (lambda: lambda e: e.reduce_max(
            RC[:, mcol:mcol + 1], src_ap, axis=AX.X))(), r=["RW"], w=["RC"])
        p.emit("v", (lambda: lambda e: e.tensor_scalar(
            eq, src_ap, RC[:, mcol:mcol + 1], None, ALU.is_equal))(),
            r=["RW", "RC"], w=["RW"])
        p.emit("v", lambda e: e.tensor_copy(ca, eq), r=["RW"], w=["RW"])
        sh = 1
        cur, nxt = ca, cb
        for stp in range(4):
            p.emit("v", (lambda cur=cur, nxt=nxt, sh=sh: lambda e:
                         e.tensor_copy(nxt[:, 0:sh], cur[:, 0:sh]))(),
                   r=["RW"], w=["RW"])
            p.emit("v", (lambda cur=cur, nxt=nxt, sh=sh: lambda e:
                         e.tensor_add(nxt[:, sh:16], cur[:, sh:16],
                                      cur[:, 0:16 - sh]))(), r=["RW"], w=["RW"])
            cur, nxt = nxt, cur
            sh *= 2
        p.emit("v", (lambda cur=cur: lambda e: e.tensor_single_scalar(
            tmp16, cur, 1.0, ALU.is_equal))(), r=["RW"], w=["RW"])
        p.emit("v", (lambda: lambda e: e.tensor_mul(out_ap, eq, tmp16))(),
               r=["RW"], w=["RW"])

    p.emit("v", lambda e: e.memset(sb["CR"][:], 0.0), w=["CR"])
    for tt in range(16):
        lg_t = sb["lgsb"].ap()[:, tt * 16:tt * 16 + 16]
        p.emit("v", (lambda lg_t=lg_t: lambda e: e.reduce_max(
            RC[:, 0:1], lg_t, axis=AX.X))(), r=["lgsb"], w=["RC"])
        p.emit("v", (lambda: lambda e: e.tensor_scalar_mul(
            RC[:, 1:2], RC[:, 0:1], -1.0))(), r=["RC"], w=["RC"])
        p.emit("s", (lambda lg_t=lg_t: lambda e: e.activation(
            zt, lg_t, AF.Exp, bias=RC[:, 1:2]))(), r=["lgsb", "RC"], w=["RW"])
        p.emit("v", (lambda: lambda e: e.reduce_sum(
            RC[:, 2:3], zt, axis=AX.X))(), r=["RW"], w=["RC"])
        first_max(zt, is1, 3)
        p.emit("v", (lambda: lambda e: e.scalar_tensor_tensor(
            zm, is1, -1e30, zt, ALU.mult, ALU.add))(), r=["RW"], w=["RW"])
        first_max(zm, is2, 4)
        p.emit("v", lambda e: e.tensor_add(sel, is1, is2), r=["RW"], w=["RW"])
        p.emit("v", (lambda: lambda e: e.tensor_add(
            RC[:, 5:6], RC[:, 3:4], RC[:, 4:5]))(), r=["RC"], w=["RC"])
        p.emit("v", (lambda: lambda e: e.scalar_tensor_tensor(
            RC[:, 5:6], RC[:, 2:3], 1e-9, RC[:, 5:6], ALU.mult, ALU.add))(),
            r=["RC"], w=["RC"])
        p.emit("s", lambda e: e.activation(RC[:, 6:7], RC[:, 5:6], AF.Ln),
               r=["RC"], w=["RC"])
        p.emit("s", lambda e: e.activation(RC[:, 6:7], RC[:, 6:7], AF.Exp,
                                           scale=-1.0), r=["RC"], w=["RC"])
        p.emit("v", (lambda: lambda e: e.tensor_mul(
            RC[:, 7:8], RC[:, 3:4], RC[:, 6:7]))(), r=["RC"], w=["RC"])
        p.emit("v", (lambda: lambda e: e.tensor_mul(
            RC[:, 8:9], RC[:, 4:5], RC[:, 6:7]))(), r=["RC"], w=["RC"])
        p.emit("t", (lambda: lambda e: e.matmul(
            ps45[:, 0:16], cx.const_ap["lt_incl_f32"][:], sel,
            start=True, stop=False))(),
            r=["RW", "c_lt_incl_f32"], w=["PS45"])
        p.emit("t", (lambda: lambda e: e.matmul(
            ps45[:, 0:16], cx.const_ap["ones_row_f32"][0:1, :], sb["CR"][:],
            start=False, stop=True))(), r=["CR", "c_ones_row_f32"], w=["PS45"])
        p.emit("v", (lambda: lambda e: e.tensor_copy(pos, ps45[:, 0:16]))(),
               r=["PS45"], w=["RW"])
        p.emit("t", (lambda: lambda e: e.matmul(
            ps67[0:1, 0:16], cx.const_ap["ones_col_f32"][:], sel,
            start=True, stop=True))(), r=["RW", "c_ones_col_f32"], w=["PS67"])
        p.emit("v", (lambda: lambda e: e.tensor_add(
            sb["CR"][:], sb["CR"][:], ps67[0:1, 0:16]))(),
            r=["PS67", "CR"], w=["CR"])
        for k in range(2):
            ismask = is1 if k == 0 else is2
            p.emit("v", (lambda ismask=ismask: lambda e: e.tensor_mul(
                tmp16, ismask, cx.const_ap["iota16_f32"][:]))(),
                r=["RW", "c_iota16_f32"], w=["RW"])
            p.emit("v", (lambda k=k: lambda e: e.reduce_sum(
                RC[:, 9 + k:10 + k], tmp16, axis=AX.X))(), r=["RW"], w=["RC"])
            p.emit("v", (lambda ismask=ismask: lambda e: e.tensor_mul(
                tmp16, ismask, pos))(), r=["RW"], w=["RW"])
            p.emit("v", (lambda k=k: lambda e: e.reduce_sum(
                RC[:, 11 + k:12 + k], tmp16, axis=AX.X))(), r=["RW"], w=["RC"])
            p.emit("v", (lambda k=k: lambda e: e.tensor_scalar_add(
                RC[:, 11 + k:12 + k], RC[:, 11 + k:12 + k], -1.0))(),
                r=["RC"], w=["RC"])
            p.emit("v", (lambda k=k: lambda e: e.scalar_tensor_tensor(
                RC[:, 13:14], RC[:, 9 + k:10 + k], float(BLK),
                RC[:, 11 + k:12 + k], ALU.mult, ALU.add))(), r=["RC"], w=["RC"])
            p.emit("v", (lambda k=k: lambda e: e.tensor_single_scalar(
                RC[:, 14:15], RC[:, 11 + k:12 + k], float(BLK), ALU.is_lt))(),
                r=["RC"], w=["RC"])
            p.emit("v", (lambda k=k: lambda e: e.tensor_copy(
                RU[:, k:k + 1], RC[:, 14:15]))(), r=["RC"], w=["RU"])
            p.emit("v", (lambda k=k: lambda e: e.tensor_mul(
                RC[:, 7 + k:8 + k], RC[:, 7 + k:8 + k], RC[:, 14:15]))(),
                r=["RC"], w=["RC"])
            p.emit("v", (lambda k=k: lambda e: e.tensor_mul(
                RC[:, 13:14], RC[:, 13:14], RC[:, 14:15]))(), r=["RC"], w=["RC"])
            p.emit("v", (lambda k=k: lambda e: e.tensor_copy(
                IDX[:, k:k + 1], RC[:, 13:14]))(), r=["RC"], w=["IDX"])
            p.emit("g", (lambda k=k: lambda e: e.indirect_dma_start(
                out=sb["yg"][:, k * 1024:k * 1024 + 1024], out_offset=None,
                in_=yin, in_offset=IOA(IDX[:, k:k + 1], 0)))(),
                r=["IDX"], w=["yg"], dma=True, tag="ygat")
        p.emit("v", (lambda: lambda e: e.tensor_scalar(
            sb["mo"][:], sb["yg"][:, 0:1024], RC[:, 7:8], None, ALU.mult))(),
            r=["yg", "RC"], w=["mo"])
        p.emit("v", (lambda: lambda e: e.scalar_tensor_tensor(
            sb["mo"][:], sb["yg"][:, 1024:2048], RC[:, 8:9], sb["mo"][:],
            ALU.mult, ALU.add))(), r=["yg", "RC", "mo"], w=["mo"])
        for i in range(8):
            psx = ps01 if i % 2 == 0 else ps23
            pk = "PS01" if i % 2 == 0 else "PS23"
            p.emit("t", (lambda i=i, psx=psx: lambda e: e.transpose(
                psx[:, 0:128], sb["mo"][:, i * 128:i * 128 + 128],
                cx.const_ap["ident_f32"][:]))(),
                r=["mo", "c_ident_f32"], w=[pk], tag="motr")
            p.emit("v", (lambda i=i, tt=tt, psx=psx: lambda e: e.tensor_add(
                sb[f"res{i}"][:, tt * 128:tt * 128 + 128],
                sb[f"res{i}"][:, tt * 128:tt * 128 + 128],
                psx[:, 0:128]))(), r=[pk, f"res{i}"], w=[f"res{i}"], tag="moadd")

    cx.sbuf("wgsb0", (128, 8 * 128), BF16)
    cx.sbuf("wusb0", (128, 8 * 128), BF16)
    cx.view("wgsb1", cx.sb["yg"].ap()[:, 0:1024])
    cx.view("wusb1", cx.sb["yg"].ap()[:, 1024:2048])
    cx.sbuf("wdall", (128, 11 * 1024), BF16)
    cx.sbuf("hfm", (128, 11 * 1024), BF16)
    cx.view("ht", cx.sb["mo"].ap().bitcast(BF16)[:, 0:1024])
    for half in range(2):
        t0 = half * 1024
        for f in range(11):
            gn, un = f"wgsb{f % 2}", f"wusb{f % 2}"
            for wnm, win in ((gn, swg_in), (un, swu_in)):
                p.emit("g", (lambda wnm=wnm, win=win, f=f: lambda e:
                             e.dma_start(out=sb[wnm][:], in_=win[f]))(),
                       r=[wnm], w=[wnm] + (["yg"] if wnm.endswith("1")
                                           else []), dma=True, tag="lsw")
            for kt in range(8):
                for ch in range(2):
                    p.emit("t", (lambda kt=kt, ch=ch, t0=t0, gn=gn: lambda e:
                                 e.matmul(
                        ps45[:, ch * 512:ch * 512 + 512],
                        sb[gn][:, kt * 128:kt * 128 + 128],
                        sb[f"lnb{kt}"][:, t0 + ch * 512:t0 + ch * 512 + 512],
                        start=(kt == 0), stop=(kt == 7)))(),
                        r=[gn, f"lnb{kt}"], w=["PS45"], tag="sg")
                    p.emit("t", (lambda kt=kt, ch=ch, t0=t0, un=un: lambda e:
                                 e.matmul(
                        ps67[:, ch * 512:ch * 512 + 512],
                        sb[un][:, kt * 128:kt * 128 + 128],
                        sb[f"lnb{kt}"][:, t0 + ch * 512:t0 + ch * 512 + 512],
                        start=(kt == 0), stop=(kt == 7)))(),
                        r=[un, f"lnb{kt}"], w=["PS67"], tag="su")
            p.emit("s", (lambda f=f: lambda e: e.activation(
                sb["ht"][:], ps45[:, 0:1024], AF.Silu))(),
                r=["PS45"], w=["ht"], tag="ssilu")
            p.emit("v", (lambda f=f: lambda e: e.tensor_mul(
                sb["hfm"][:, f * 1024:f * 1024 + 1024], sb["ht"][:],
                ps67[:, 0:1024]))(), r=["ht", "PS67"], w=["hfm"], tag="shmul")
        if half == 0:
            for f in range(11):
                p.emit("g", (lambda f=f: lambda e: e.dma_start(
                    out=sb["wdall"][:, f * 1024:f * 1024 + 1024],
                    in_=swd_in[f * 128:f * 128 + 128, :]))(),
                    r=["wdall"], w=["wdall"], dma=True, tag="lswd")
        for m in range(8):
            psD = ps01 if m % 2 == 0 else ps23
            pk = "PS01" if m % 2 == 0 else "PS23"
            for f in range(11):
                for ch in range(2):
                    p.emit("t", (lambda f=f, m=m, ch=ch, psD=psD, t0=t0:
                                 lambda e: e.matmul(
                                     psD[:, ch * 512:ch * 512 + 512],
                                     sb["wdall"][:, f * 1024 + m * 128:
                                                 f * 1024 + m * 128 + 128],
                                     sb["hfm"][:, f * 1024 + ch * 512:
                                               f * 1024 + ch * 512 + 512],
                                     start=(f == 0), stop=(f == 10)))(),
                           r=["wdall", "hfm"], w=[pk], tag="sd")
            p.emit("v", (lambda m=m, psD=psD, t0=t0: lambda e:
                         e.tensor_add(
                             sb[f"res{m}"][:, t0:t0 + 1024],
                             sb[f"res{m}"][:, t0:t0 + 1024],
                             psD[:, 0:1024]))(),
                   r=[pk, f"res{m}"], w=[f"res{m}"], tag="sdadd")
    if not merged:
        for i in range(8):
            p.emit("q", (lambda i=i: lambda e: e.dma_start(
                out=xmid[i * 128:(i + 1) * 128, :], in_=sb[f"res{i}"][:]))(),
                r=[f"res{i}"], w=["xmid_d"], dma=True, tag="xst")
        p.barrier(engs=("q",), on=("q", "g"))
        p.barrier(engs=("g",), on=("g",))
    return cx


def run_launch_c1(inputs, xouts, lgouts, yblks):
    consts = host_constants()
    cx = build_launch_c(consts)
    cx.p.seal(cx.es)
    cx.es.close()
    pks = lambda w: np.ascontiguousarray(
        np.asarray(w, np.float32).reshape(8, 128, 11, 128)
        .transpose(2, 1, 0, 3).reshape(11, 128, 1024))
    swg = pks(inputs["L0_ffn_sw_gate"])
    swu = pks(inputs["L0_ffn_sw_up"])
    swd = np.ascontiguousarray(np.asarray(inputs["L0_ffn_sw_down"], np.float32))
    in_maps = []
    for c in range(NCORES):
        yin = np.concatenate(
            [yblks[e // 2][(e % 2) * NCORES * BLK + c * BLK:
                           (e % 2) * NCORES * BLK + (c + 1) * BLK] for e in range(E)], 0)
        in_maps.append(dict(xin=xouts[c], lgown=lgouts[c],
                            yin=np.ascontiguousarray(yin),
                            swg=swg, swu=swu, swd=swd))
    return run_bass_kernel_spmd(cx.nc, in_maps, core_ids=list(range(NCORES)))



def build_layernorm2(cx, res_names, out_bf, hook=None, tag="ln"):
    p, sb = cx.p, cx.sb
    ones = cx.const_ap["ones_col_f32"]
    onesr = cx.const_ap["ones_row_f32"]
    s45, s67 = cx.ps["PS45"], cx.ps["PS67"]
    scr = sb["lnscr"]
    stM, stV, stR = sb["stM"], sb["stV"], sb["stR"]
    rb, nb = sb["ln_rb"], sb["ln_nb"]
    for half in range(2):
        hs = slice(half * 1024, half * 1024 + 1024)
        for i in range(8):
            for q in range(2):
                qs = slice(half * 1024 + q * 512, half * 1024 + q * 512 + 512)
                p.emit("t", (lambda i=i, q=q, qs=qs: lambda e:
                             e.matmul(s45[0:1, q * 512:q * 512 + 512],
                                      ones[:], sb[res_names[i]][:, qs],
                                      start=(i == 0), stop=(i == 7)))(),
                       r=[res_names[i], "c_ones_col_f32"], w=["PS45"], tag=tag + "su")
        for i in range(8):
            p.emit("s", (lambda i=i, hs=hs: lambda e:
                         e.activation(scr[:], sb[res_names[i]][:, hs],
                                      AF.Square))(),
                   r=[res_names[i]], w=["lnscr"], tag=tag + "sq")
            for q in range(2):
                p.emit("t", (lambda i=i, q=q: lambda e:
                             e.matmul(s67[0:1, q * 512:q * 512 + 512],
                                      ones[:], scr[:, q * 512:q * 512 + 512],
                                      start=(i == 0), stop=(i == 7)))(),
                       r=["lnscr", "c_ones_col_f32"], w=["PS67"], tag=tag + "s2")
        p.emit("v", (lambda hs=hs: lambda e:
                     e.tensor_scalar_mul(stM[0:1, hs], s45[0:1, 0:1024],
                                         -1.0 / 1024))(),
               r=["PS45"], w=["lnstats"], tag=tag + "mean")
        p.emit("v", (lambda hs=hs: lambda e:
                     e.tensor_scalar_mul(stV[0:1, hs], s67[0:1, 0:1024],
                                         1.0 / 1024))(),
               r=["PS67"], w=["lnstats"], tag=tag + "msq")
    p.emit("v", lambda e: e.tensor_mul(stR[:], stM[:], stM[:]),
           r=["lnstats"], w=["lnstats"], tag=tag + "m2")
    p.emit("v", lambda e: e.tensor_sub(stV[:], stV[:], stR[:]),
           r=["lnstats"], w=["lnstats"], tag=tag + "var")
    p.emit("v", lambda e: e.tensor_scalar_add(stV[:], stV[:], 1e-5),
           r=["lnstats"], w=["lnstats"], tag=tag + "eps")
    p.emit("s", lambda e: e.activation(stR[:], stV[:], AF.Ln),
           r=["lnstats"], w=["lnstats"], tag=tag + "ln")
    p.emit("s", lambda e: e.activation(stR[:], stR[:], AF.Exp, scale=-0.5),
           r=["lnstats"], w=["lnstats"], tag=tag + "exp")
    p.emit("v", lambda e: e.tensor_mul(stM[:], stM[:], stR[:]),
           r=["lnstats"], w=["lnstats"], tag=tag + "nmr")
    for half in range(2):
        hs = slice(half * 1024, half * 1024 + 1024)
        for q in range(2):
            qs = slice(half * 1024 + q * 512, half * 1024 + q * 512 + 512)
            p.emit("t", (lambda qs=qs, q=q: lambda e:
                         e.matmul(s45[:, q * 512:q * 512 + 512], onesr[0:1, :],
                                  sb["stR"][0:1, qs], start=True, stop=True))(),
                   r=["lnstats", "c_ones_row_f32"], w=["PS45"], tag=tag + "bc")
            p.emit("t", (lambda qs=qs, q=q: lambda e:
                         e.matmul(s67[:, q * 512:q * 512 + 512], onesr[0:1, :],
                                  sb["stM"][0:1, qs], start=True, stop=True))(),
                   r=["lnstats", "c_ones_row_f32"], w=["PS67"], tag=tag + "bc")
        p.emit("v", lambda e: e.tensor_copy(sb["ln_rb"][:], s45[:, 0:1024]),
               r=["PS45"], w=["ln_rb"], tag=tag + "cb")
        p.emit("v", lambda e: e.tensor_copy(sb["ln_nb"][:], s67[:, 0:1024]),
               r=["PS67"], w=["ln_nb"], tag=tag + "cb")
        for i in range(8):
            p.emit("v", (lambda i=i, hs=hs: lambda e:
                         e.tensor_mul(scr[:], sb[res_names[i]][:, hs],
                                      sb["ln_rb"][:]))(),
                   r=[res_names[i], "ln_rb"], w=["lnscr"], tag=tag + "nm")
            p.emit("v", (lambda i=i, hs=hs: lambda e:
                         e.tensor_add(sb[out_bf[i]][:, hs], scr[:],
                                      sb["ln_nb"][:]))(),
                   r=["lnscr", "ln_nb"], w=[out_bf[i]], tag=tag + "na")
            if hook is not None:
                hook(i, half)


def build_launch_c2(consts, stage=99, cx=None, ext=None):
    merged = ext is not None
    if cx is None:
        cx = Ctx("C2")
    nc, p = cx.nc, cx.p
    if merged:
        xin = None
    else:
        xin = nc.dram_tensor("xin", [D, S], F32, kind="ExternalInput").ap()
    wx_in = nc.dram_tensor("wxt", [16, 128, 1024], F32, kind="ExternalInput").ap()
    wz_in = nc.dram_tensor("wzt", [16, 128, 1024], F32, kind="ExternalInput").ap()
    cw_in = nc.dram_tensor("cw", [16, 128, 4], F32, kind="ExternalInput").ap()
    cb_in = nc.dram_tensor("cb", [128, 16], F32, kind="ExternalInput").ap()
    wb_in = nc.dram_tensor("wbt", [4, 128, 2048], F32, kind="ExternalInput").ap()
    wc_in = nc.dram_tensor("wct", [4, 128, 2048], F32, kind="ExternalInput").ap()
    wdt_in = nc.dram_tensor("wdtt", [128, 128], F32, kind="ExternalInput").ap()
    bdt_in = nc.dram_tensor("bdt", [1, L1H], F32, kind="ExternalInput").ap()
    la_in = nc.dram_tensor("loga", [1, L1H], F32, kind="ExternalInput").ap()
    dp_in = nc.dram_tensor("dparam", [1, L1H], F32, kind="ExternalInput").ap()
    gw_in = nc.dram_tensor("gnw", [128, 16], F32, kind="ExternalInput").ap()
    gb_in = nc.dram_tensor("gnb", [128, 16], F32, kind="ExternalInput").ap()
    wo_in = nc.dram_tensor("woutt", [8, 128, 2048], F32, kind="ExternalInput").ap()
    xout = None if merged else nc.dram_tensor("xout", [D, S], F32,
                                              kind="ExternalOutput").ap()
    ykind = "ExternalOutput" if stage == 2 else "Internal"
    ydram = nc.dram_tensor("ydram", [16, 128, S], BF16, kind=ykind).ap()
    lnbdram = nc.dram_tensor("lnbdram", [D, S], BF16, kind="Internal").ap()
    if stage == 2:
        dmp_ln = nc.dram_tensor("dmp_ln", [D, S], BF16,
                                kind="ExternalOutput").ap()
        dmp_x = nc.dram_tensor("dmp_x", [INNER, S], BF16,
                               kind="ExternalOutput").ap()
        dmp_bc = nc.dram_tensor("dmp_bc", [8 * 128, S], BF16,
                                kind="ExternalOutput").ap()
        dmp_dt = nc.dram_tensor("dmp_dt", [128, 128], F32,
                                kind="ExternalOutput").ap()
        dmp_yst = nc.dram_tensor("dmp_yst", [128, 32], F32,
                                 kind="ExternalOutput").ap()
        dmp_mrs = nc.dram_tensor("dmp_mrs", [128, 64], F32,
                                 kind="ExternalOutput").ap()
        dmp_grow = nc.dram_tensor("dmp_grow", [1, 64], F32,
                                  kind="ExternalOutput").ap()
        dmp_urg = nc.dram_tensor("dmp_urg", [128, 8192], BF16,
                                 kind="ExternalOutput").ap()

    sb = cx.sb
    for i in range(8):
        cx.sbuf(f"res{i}", (128, S), F32)
    for i in range(8):
        cx.sbuf(f"lnb{i}", (128, S), BF16)
    pool = cx.sbuf("pool", (128, 16 * S), BF16)
    pbf = pool.ap()
    pf32 = pbf.bitcast(F32)
    for i in range(16):
        cx.view(f"xcv{i}", pbf[:, i * S:(i + 1) * S])
    cx.view("lnscr", pf32[:, 0:1024])
    cx.view("ln_rb", pf32[:, 1024:2048])
    cx.view("ln_nb", pf32[:, 2048:3072])
    cx.view("stM", pf32[0:1, 3072:5120])
    cx.view("stV", pf32[0:1, 5120:7168])
    cx.view("stR", pf32[0:1, 7168:9216])
    cx.view("yrl", pbf[:, 0:8192])
    cx.view("urg", pbf[:, 8192:16384])
    cx.view("zt", pbf[:, 16384:16896])
    cx.sbuf("wt0", (128, INNER), BF16)
    cx.sbuf("wt1", (128, INNER), BF16)
    cx.sbuf("cwd", (128, 4 * 128), BF16)
    cx.sbuf("cbcol", (128, 16), F32)
    cx.sbuf("btm", (128, 512), BF16)
    cx.sbuf("xtm", (128, S), BF16)
    cx.sbuf("ycs", (128, S), BF16)
    cx.sbuf("dtp", (128, 8), F32)
    cx.sbuf("csT", (128, 8), F32)
    cx.sbuf("nacs", (128, 8), F32)
    cx.sbuf("acsrow", (1, 128), F32)
    cx.sbuf("erow", (1, 128), F32)
    cx.sbuf("arow", (1, 8), F32)
    cx.sbuf("arowp", (1, 8), F32)
    cx.sbuf("AVB2", (128, 8), F32)
    cx.sbuf("DB", (128, 8), F32)
    cx.sbuf("wcol", (128, 2), F32)
    cx.sbuf("atc", (128, 1), F32)
    cx.sbuf("SST", (128, 4 * HD), F32)
    cx.sbuf("sstt", (128, HD), F32)
    cx.sbuf("et", (128, 128), F32)
    cx.sbuf("etm", (128, 128), F32)
    cx.sbuf("mtb", (128, 128), BF16)
    cx.sbuf("cts", (128, 128), F32)
    cx.sbuf("cbxb", (128, 128), F32)
    cx.sbuf("bsbE", (128, 128), BF16)
    cx.sbuf("bsbO", (128, 128), BF16)
    ps01 = cx.psum("PS01", (128, 1024))
    ps23 = cx.psum("PS23", (128, 1024))
    ps45 = cx.psum("PS45", (128, 1024))
    ps67 = cx.psum("PS67", (128, 1024))
    cx.load_consts(consts, ["ones_col_f32", "ones_row_f32", "ident_bf",
                            "ident_f32", "lt_incl_f32", "trimask_u8",
                            "zero_f32_128"])

    res_names = [f"res{i}" for i in range(8)]
    lnb_names = [f"lnb{i}" for i in range(8)]
    if not merged:
        for i in range(8):
            p.emit("q", (lambda i=i: lambda e: e.dma_start(
                out=sb[f"res{i}"][:], in_=xin[i * 128:(i + 1) * 128, :]))(),
                w=[f"res{i}"], dma=True, tag="ldx")
    p.emit("q", lambda e: e.dma_start(out=sb["arow"][:], in_=la_in),
           w=["arow"], dma=True)
    p.emit("q", lambda e: e.dma_start(out=sb["cbcol"][:], in_=cb_in),
           w=["cbcol"], dma=True)
    p.emit("s", lambda e: e.activation(sb["arow"][:], sb["arow"][:], AF.Exp,
                                       scale=1.0), r=["arow"], w=["arow"])
    p.emit("v", lambda e: e.tensor_scalar_mul(sb["arowp"][:], sb["arow"][:], 1.0),
           r=["arow"], w=["arowp"])
    p.emit("v", lambda e: e.tensor_scalar_mul(sb["arow"][:], sb["arow"][:], -1.0),
           r=["arow"], w=["arow"])
    p.emit("t", lambda e: e.matmul(ps45[:, 0:8],
                                   cx.const_ap["ones_row_f32"][0:1, :],
                                   sb["arowp"][:], start=True, stop=True),
           r=["arowp", "c_ones_row_f32"], w=["PS45"])
    p.emit("v", lambda e: e.tensor_copy(sb["AVB2"][:], ps45[:, 0:8]),
           r=["PS45"], w=["AVB2"])
    p.emit("g", lambda e: e.dma_start(out=sb["grow"][0:1, 0:8], in_=dp_in),
           w=["grow"], dma=True)
    p.emit("t", lambda e: e.matmul(ps45[:, 8:16],
                                   cx.const_ap["ones_row_f32"][0:1, :],
                                   sb["grow"][0:1, 0:8], start=True, stop=True),
           r=["grow", "c_ones_row_f32"], w=["PS45"])
    p.emit("v", lambda e: e.tensor_copy(sb["DB"][:], ps45[:, 8:16]),
           r=["PS45"], w=["DB"])
    p.emit("g", lambda e: e.dma_start(out=sb["grow"][0:1, 8:16], in_=bdt_in),
           w=["grow"], dma=True)

    build_layernorm2(cx, res_names, lnb_names, tag="ln2")
    for i in range(8):
        p.emit("q", (lambda i=i: lambda e: e.dma_start(
            out=lnbdram[i * 128:(i + 1) * 128, :], in_=sb[f"lnb{i}"][:]))(),
            r=[f"lnb{i}"], w=["lnbd"], dma=True, tag="lnsp")
    if stage == 2:
        for i in range(8):
            p.emit("q", (lambda i=i: lambda e: e.dma_start(
                out=dmp_ln[i * 128:(i + 1) * 128, :], in_=sb[f"lnb{i}"][:]))(),
                r=[f"lnb{i}"], w=["dmp"], dma=True, tag="dln")

    cx.sbuf("dtfull", (128, 16 * 8), F32)
    cx.sbuf("BDTB", (128, 8), F32)
    cx.sbuf("yst", (128, 32), F32)
    cx.sbuf("MRS", (128, 64), F32)
    cx.sbuf("grow", (1, 64), F32)
    cx.sbuf("wdts", (128, 16 * 8), BF16)
    for m in range(16):
        wtn = f"wt{m % 2}"
        p.emit("g", (lambda m=m, wtn=wtn: lambda e: e.dma_start(
            out=sb[wtn][:, 0:1024], in_=wx_in[m]))(), r=[wtn], w=[wtn],
            dma=True, tag="lwx")
        for ch in range(4):
            psx = ps45 if ch % 2 == 0 else ps67
            pk = "PS45" if ch % 2 == 0 else "PS67"
            for kt in range(8):
                p.emit("t", (lambda ch=ch, kt=kt, psx=psx, wtn=wtn: lambda e:
                             e.matmul(psx[:, (ch // 2) * 512:(ch // 2) * 512 + 512],
                                      sb[wtn][:, kt * 128:kt * 128 + 128],
                                      sb[f"lnb{kt}"][:, ch * 512:ch * 512 + 512],
                                      start=(kt == 0), stop=(kt == 7)))(),
                       r=[wtn, f"lnb{kt}"], w=[pk], tag="xproj")
            p.emit("s", (lambda m=m, ch=ch, psx=psx: lambda e: e.activation(
                sb[f"xcv{m}"][:, ch * 512:ch * 512 + 512],
                psx[:, (ch // 2) * 512:(ch // 2) * 512 + 512], AF.Copy))(),
                r=[pk], w=[f"xcv{m}"], tag="xevac")
        p.emit("q", (lambda m=m: lambda e: e.dma_start(
            out=sb["dtp"][:, 0:4], in_=cw_in[m]))(),
            r=["dtp"], w=["dtp"], dma=True, tag="lcw")
        for k in range(4):
            p.emit("v", (lambda k=k: lambda e: e.tensor_scalar(
                sb["et"][:], cx.const_ap["ident_f32"][:], sb["dtp"][:, k:k + 1],
                None, ALU.mult))(), r=["dtp", "c_ident_f32"], w=["et"])
            p.emit("v", (lambda k=k: lambda e: e.tensor_copy(
                sb["cwd"][:, k * 128:k * 128 + 128], sb["et"][:]))(),
                r=["et"], w=["cwd"])
        for ch in range(4):
            psx = ps01 if ch < 2 else ps23
            pk = "PS01" if ch < 2 else "PS23"
            reg = (ch % 2) * 512
            for k in (3, 2, 1, 0):
                sh = 3 - k
                src0 = ch * 512 - sh
                lo = max(0, src0)
                width = 512 - (lo - src0)
                o0 = reg + (lo - src0)
                p.emit("t", (lambda k=k, psx=psx, lo=lo, width=width, o0=o0,
                             m=m: lambda e: e.matmul(
                                 psx[:, o0:o0 + width],
                                 sb["cwd"][:, k * 128:k * 128 + 128],
                                 sb[f"xcv{m}"][:, lo:lo + width],
                                 start=(k == 3), stop=(k == 0)))(),
                       r=["cwd", f"xcv{m}"], w=[pk], tag="conv")
        for ch in range(4):
            psx = ps01 if ch < 2 else ps23
            pk = "PS01" if ch < 2 else "PS23"
            reg = (ch % 2) * 512
            p.emit("s", (lambda m=m, ch=ch, psx=psx, reg=reg: lambda e:
                         e.activation(
                             sb[f"xcv{m}"][:, ch * 512:ch * 512 + 512],
                             psx[:, reg:reg + 512], AF.Silu,
                             bias=sb["cbcol"][:, m:m + 1]))(),
                   r=[pk, "cbcol"], w=[f"xcv{m}"], tag="csilu")

    for which, w_in, dst0 in (("b", wb_in, 0), ("c", wc_in, 4)):
        for sl in range(4):
            wtn = f"wt{sl % 2}"
            p.emit("g", (lambda w_in=w_in, sl=sl, wtn=wtn: lambda e: e.dma_start(
                out=sb[wtn][:], in_=w_in[sl]))(), r=[wtn], w=[wtn],
                dma=True, tag="lwb")
            for ch in range(4):
                psx = ps45 if ch % 2 == 0 else ps67
                pk = "PS45" if ch % 2 == 0 else "PS67"
                for kt in range(16):
                    p.emit("t", (lambda ch=ch, kt=kt, psx=psx, wtn=wtn: lambda e:
                                 e.matmul(
                                     psx[:, (ch // 2) * 512:(ch // 2) * 512 + 512],
                                     sb[wtn][:, kt * 128:kt * 128 + 128],
                                     sb[f"xcv{kt}"][:, ch * 512:ch * 512 + 512],
                                     start=(kt == 0), stop=(kt == 15)))(),
                           r=[wtn, f"xcv{kt}"], w=[pk], tag="bcproj")
                p.emit("s", (lambda sl=sl, ch=ch, psx=psx, dst0=dst0: lambda e:
                             e.activation(
                                 sb[f"lnb{dst0 + sl}"][:, ch * 512:ch * 512 + 512],
                                 psx[:, (ch // 2) * 512:(ch // 2) * 512 + 512],
                                 AF.Copy))(),
                       r=[pk], w=[f"lnb{dst0 + sl}"], tag="bcevac")

    p.emit("g", lambda e: e.dma_start(out=sb["wdts"][:], in_=wdt_in),
           w=["wdts"], dma=True, tag="lwdt")
    p.emit("t", lambda e: e.matmul(ps45[:, 16:24],
                                   cx.const_ap["ones_row_f32"][0:1, :],
                                   sb["grow"][0:1, 8:16], start=True, stop=True),
           r=["grow", "c_ones_row_f32"], w=["PS45"])
    p.emit("v", lambda e: e.tensor_copy(sb["BDTB"][:], ps45[:, 16:24]),
           r=["PS45"], w=["BDTB"])
    for tt in range(16):
        for kt in range(16):
            p.emit("t", (lambda tt=tt, kt=kt: lambda e: e.matmul(
                ps67[:, 0:8], sb[f"xcv{kt}"][:, tt * 128:tt * 128 + 128],
                sb["wdts"][:, kt * 8:kt * 8 + 8],
                start=(kt == 0), stop=(kt == 15)))(),
                r=[f"xcv{kt}", "wdts"], w=["PS67"], tag="dtmm")
        p.emit("v", (lambda tt=tt: lambda e: e.tensor_add(
            sb["dtp"][:], ps67[:, 0:8], sb["BDTB"][:]))(),
            r=["PS67", "BDTB"], w=["dtp"], tag="dtadd")
        p.emit("s", (lambda: lambda e: e.activation(
            sb["csT"][:], sb["dtp"][:], AF.Abs))(),
            r=["dtp"], w=["csT"])
        p.emit("s", (lambda: lambda e: e.activation(
            sb["csT"][:], sb["csT"][:], AF.Exp, scale=-1.0))(),
            r=["csT"], w=["csT"])
        p.emit("v", (lambda: lambda e: e.tensor_scalar_add(
            sb["csT"][:], sb["csT"][:], 1.0))(), r=["csT"], w=["csT"])
        p.emit("s", (lambda: lambda e: e.activation(
            sb["csT"][:], sb["csT"][:], AF.Ln))(), r=["csT"], w=["csT"])
        p.emit("v", (lambda: lambda e: e.tensor_scalar_max(
            sb["dtp"][:], sb["dtp"][:], 0.0))(), r=["dtp"], w=["dtp"])
        p.emit("v", (lambda tt=tt: lambda e: e.tensor_add(
            sb["dtfull"][:, tt * 8:tt * 8 + 8], sb["dtp"][:], sb["csT"][:]))(),
            r=["dtp", "csT"], w=["dtfull"], tag="dtsp")

    if stage == 2:
        for i in range(16):
            p.emit("q", (lambda i=i: lambda e: e.dma_start(
                out=dmp_x[i * 128:(i + 1) * 128, :], in_=sb[f"xcv{i}"][:]))(),
                r=[f"xcv{i}"], w=["dmp"], dma=True, tag="dxc")
        for i in range(8):
            p.emit("q", (lambda i=i: lambda e: e.dma_start(
                out=dmp_bc[i * 128:(i + 1) * 128, :], in_=sb[f"lnb{i}"][:]))(),
                r=[f"lnb{i}"], w=["dmp"], dma=True, tag="dbc")
        p.emit("q", lambda e: e.dma_start(out=dmp_dt, in_=sb["dtfull"][:]),
               r=["dtfull"], w=["dmp"], dma=True, tag="ddt")
    p.emit("v", lambda e: e.memset(sb["SST"][:], 0.0), w=["SST"])
    p.emit("v", lambda e: e.memset(sb["yst"][:], 0.0), w=["yst"])
    p.emit("v", lambda e: e.memset(sb["bsbE"][:], 0.0), w=["bsbE"])
    p.emit("v", lambda e: e.memset(sb["bsbO"][:], 0.0), w=["bsbO"])
    for c in range(16):
        dtc = sb["dtfull"].ap()[:, c * 8:c * 8 + 8]
        p.emit("t", (lambda dtc=dtc: lambda e: e.matmul(
            ps45[:, 24:32], cx.const_ap["lt_incl_f32"][:], dtc,
            start=True, stop=True))(), r=["dtfull", "c_lt_incl_f32"],
            w=["PS45"], tag="csmm")
        p.emit("v", lambda e: e.tensor_copy(sb["csT"][:], ps45[:, 24:32]),
               r=["PS45"], w=["csT"])
        p.emit("v", lambda e: e.tensor_mul(sb["nacs"][:], sb["csT"][:],
                                           sb["AVB2"][:]),
               r=["csT", "AVB2"], w=["nacs"])
        psb23 = ps23.ap().bitcast(BF16)
        for it in range(16):
            p.emit("t", (lambda it=it, c=c, psb23=psb23: lambda e: e.transpose(
                psb23[:, it * 128:it * 128 + 128],
                sb[f"xcv{it}"][:, c * 128:c * 128 + 128],
                cx.const_ap["ident_bf"][:]))(),
                r=[f"xcv{it}", "c_ident_bf"], w=["PS23"], tag="xtmtr")
        p.emit("v", (lambda psb23=psb23: lambda e: e.tensor_copy(
            sb["xtm"][:], psb23[:, 0:2048]))(), r=["PS23"], w=["xtm"])
        psb01 = ps01.ap().bitcast(BF16)
        for sl in range(4):
            p.emit("t", (lambda sl=sl, c=c, psb01=psb01: lambda e: e.transpose(
                psb01[:, sl * 128:sl * 128 + 128],
                sb[f"lnb{sl}"][:, c * 128:c * 128 + 128],
                cx.const_ap["ident_bf"][:]))(),
                r=[f"lnb{sl}", "c_ident_bf"], w=["PS01"], tag="btmtr")
        p.emit("v", (lambda psb01=psb01: lambda e: e.tensor_copy(
            sb["btm"][:], psb01[:, 0:512]))(), r=["PS01"], w=["btm"])
        for h in range(8):
            sl = h // 2
            b0 = (h % 2) * 64
            p.emit("t", (lambda h=h, dtc=dtc: lambda e: e.matmul(
                ps67[0:1, 512:640], dtc[:, h:h + 1],
                cx.const_ap["lt_incl_f32"][:], start=True, stop=True))(),
                r=["dtfull", "c_lt_incl_f32"], w=["PS67"], tag="csr")
            p.emit("v", (lambda h=h: lambda e: e.tensor_scalar(
                sb["acsrow"][:], ps67[0:1, 512:640],
                sb["arow"][0:1, h:h + 1], None, ALU.mult))(),
                r=["PS67", "arow"], w=["acsrow"])
            p.emit("s", (lambda: lambda e: e.activation(
                sb["erow"][:], sb["acsrow"][:], AF.Exp))(),
                r=["acsrow"], w=["erow"])
            p.emit("t", (lambda sl=sl, b0=b0, c=c: lambda e: e.matmul(
                ps45[:, 0:128], sb[f"lnb{sl}"][b0:b0 + 64, c * 128:c * 128 + 128],
                sb[f"lnb{4 + sl}"][b0:b0 + 64, c * 128:c * 128 + 128],
                start=True, stop=True))(),
                r=[f"lnb{sl}", f"lnb{4 + sl}"], w=["PS45"], tag="gmm")
            p.emit("t", (lambda h=h: lambda e: e.matmul(
                ps45[:, 512:640], cx.const_ap["ones_row_f32"][0:1, :],
                sb["acsrow"][0:1, 0:128],
                start=True, stop=True))(),
                r=["acsrow", "c_ones_row_f32"], w=["PS45"], tag="emm")
            p.emit("s", (lambda h=h: lambda e: e.activation(
                sb["et"][:], ps45[:, 512:640], AF.Exp,
                bias=sb["nacs"][:, h:h + 1]))(),
                r=["PS45", "nacs"], w=["et"], tag="eexp")
            p.emit("v", (lambda: lambda e: e.select(
                sb["etm"][:], cx.const_ap["trimask_u8"][:], sb["et"][:],
                cx.const_ap["zero_f32_128"][:]))(),
                r=["et", "c_trimask_u8", "c_zero_f32_128"], w=["etm"])
            p.emit("v", (lambda: lambda e: e.tensor_mul(
                sb["et"][:], ps45[:, 0:128], sb["etm"][:]))(),
                r=["PS45", "etm"], w=["et"])
            p.emit("v", (lambda h=h, c=c: lambda e: e.tensor_scalar(
                sb["mtb"][:], sb["et"][:],
                sb["dtfull"][:, c * 8 + h:c * 8 + h + 1],
                None, ALU.mult))(), r=["et", "dtfull"], w=["mtb"], tag="mts")
            p.emit("t", (lambda h=h: lambda e: e.matmul(
                ps67[:, 512:640], cx.const_ap["ones_row_f32"][0:1, 0:128],
                sb["erow"][0:1, 0:128],
                start=True, stop=True))(),
                r=["erow", "c_ones_row_f32"], w=["PS67"], tag="cbc")
            p.emit("v", (lambda: lambda e: e.tensor_copy(
                sb["cbxb"][:], ps67[:, 512:640]))(), r=["PS67"], w=["cbxb"])
            p.emit("v", (lambda sl=sl, b0=b0, c=c: lambda e: e.tensor_mul(
                sb["cts"][b0:b0 + 64, :],
                sb[f"lnb{4 + sl}"][b0:b0 + 64, c * 128:c * 128 + 128],
                sb["cbxb"][b0:b0 + 64, :]))(),
                r=[f"lnb{4 + sl}", "cbxb"], w=["cts"], tag="ctsm")
            for half in range(2):
                reg = 0 if half == 0 else 512
                b0h = (h % 2) * 64
                c0h = (h // 2) * 256
                p.emit("t", (lambda b0h=b0h, c0h=c0h, reg=reg, half=half:
                             lambda e: e.matmul(
                    ps67[:, reg:reg + 128],
                    sb["SST"][b0h:b0h + 64, c0h + half * 128:
                              c0h + half * 128 + 128],
                    sb["cts"][b0h:b0h + 64, :], start=True, stop=False))(),
                    r=["SST", "cts"], w=["PS67"], tag="yinter")
                p.emit("t", (lambda h=h, reg=reg, half=half: lambda e: e.matmul(
                    ps67[:, reg:reg + 128],
                    sb["xtm"][:, h * 256 + half * 128:h * 256 + half * 128 + 128],
                    sb["mtb"][:], start=False, stop=True))(),
                    r=["xtm", "mtb"], w=["PS67"], tag="yintra")
                it = 2 * h + half
                p.emit("v", (lambda it=it, c=c, h=h, reg=reg: lambda e:
                             e.scalar_tensor_tensor(
                                 sb["ycs"][:, it * 128:it * 128 + 128],
                                 sb[f"xcv{it}"][:, c * 128:c * 128 + 128],
                                 sb["DB"][:, h:h + 1], ps67[:, reg:reg + 128],
                                 ALU.mult, ALU.add))(),
                       r=[f"xcv{it}", "DB", "PS67"], w=["ycs"], tag="yev")
            p.emit("t", (lambda h=h: lambda e: e.matmul(
                ps45[:, 128:129], cx.const_ap["ones_row_f32"][0:1, :],
                sb["acsrow"][0:1, 127:128],
                start=True, stop=True))(),
                r=["acsrow", "c_ones_row_f32"], w=["PS45"], tag="atlb")
            p.emit("v", (lambda h=h: lambda e: e.tensor_add(
                sb["wcol"][:, 0:1], ps45[:, 128:129], sb["nacs"][:, h:h + 1]))(),
                r=["PS45", "nacs"], w=["wcol"])
            p.emit("s", (lambda: lambda e: e.activation(
                sb["wcol"][:, 0:1], sb["wcol"][:, 0:1], AF.Exp))(),
                r=["wcol"], w=["wcol"])
            p.emit("v", (lambda h=h, c=c: lambda e: e.tensor_mul(
                sb["wcol"][:, 1:2], sb["wcol"][:, 0:1],
                sb["dtfull"][:, c * 8 + h:c * 8 + h + 1]))(),
                r=["wcol", "dtfull"], w=["wcol"])
            p.emit("t", (lambda h=h: lambda e: e.matmul(
                ps45[:, 128:129], cx.const_ap["ones_row_f32"][0:1, :],
                sb["erow"][0:1, 127:128],
                start=True, stop=True))(),
                r=["erow", "c_ones_row_f32"], w=["PS45"], tag="atb")
            p.emit("v", (lambda: lambda e: e.tensor_copy(
                sb["atc"][:], ps45[:, 128:129]))(), r=["PS45"], w=["atc"])
            b0 = (h % 2) * 64
            c0 = (h // 2) * 256
            bk = "bsbE" if h % 2 == 0 else "bsbO"
            p.emit("v", (lambda h=h, b0=b0, sl=sl, bk=bk: lambda e:
                         e.tensor_scalar(
                sb[bk][:, b0:b0 + 64],
                sb["btm"][:, sl * 128 + b0:sl * 128 + b0 + 64],
                sb["wcol"][:, 1:2], None, ALU.mult))(),
                r=["btm", "wcol"], w=[bk], tag="bscale")
            p.emit("t", (lambda h=h, bk=bk: lambda e: e.matmul(
                ps45[:, 256:512], sb[bk][:],
                sb["xtm"][:, h * 256:h * 256 + 256], start=True, stop=True))(),
                r=[bk, "xtm"], w=["PS45"], tag="dsmm")
            p.emit("s", (lambda b0=b0, c0=c0: lambda e: e.activation(
                sb["sstt"][b0:b0 + 64, :], sb["SST"][b0:b0 + 64, c0:c0 + 256],
                AF.Identity, scale=sb["atc"][b0:b0 + 64, 0:1]))(),
                r=["SST", "atc"], w=["sstt"], tag="sdec")
            p.emit("v", (lambda b0=b0, c0=c0: lambda e: e.tensor_add(
                sb["SST"][b0:b0 + 64, c0:c0 + 256], sb["sstt"][b0:b0 + 64, :],
                ps45[b0:b0 + 64, 256:512]))(),
                r=["sstt", "PS45"], w=["SST"], tag="supd")
        p.emit("q", (lambda c=c: lambda e: e.dma_start(
            out=ydram[c], in_=sb["ycs"][:]))(),
            r=["ycs"], w=["ydram_d"], dma=True, tag="ysp")
        p.emit("s", (lambda: lambda e: e.activation(
            sb["xtm"][:], sb["ycs"][:], AF.Square))(), r=["ycs", "xtm"],
            w=["xtm"])
        p.emit("v", (lambda: lambda e: e.reduce_sum(
            sb["MRS"][:, 0:16],
            sb["ycs"].ap().rearrange("p (it t) -> p it t", t=128), axis=AX.X))(),
            r=["ycs"], w=["MRS"])
        p.emit("v", (lambda: lambda e: e.tensor_add(
            sb["yst"][:, 0:16], sb["yst"][:, 0:16], sb["MRS"][:, 0:16]))(),
            r=["MRS", "yst"], w=["yst"])
        p.emit("v", (lambda: lambda e: e.reduce_sum(
            sb["MRS"][:, 0:16],
            sb["xtm"].ap().rearrange("p (it t) -> p it t", t=128), axis=AX.X))(),
            r=["xtm"], w=["MRS"])
        p.emit("v", (lambda: lambda e: e.tensor_add(
            sb["yst"][:, 16:32], sb["yst"][:, 16:32], sb["MRS"][:, 0:16]))(),
            r=["MRS", "yst"], w=["yst"])

    p.emit("t", lambda e: e.matmul(ps45[0:1, 0:32],
                                   cx.const_ap["ones_col_f32"][:],
                                   sb["yst"][:], start=True, stop=True),
           r=["yst", "c_ones_col_f32"], w=["PS45"], tag="gsum")
    p.emit("v", lambda e: e.tensor_copy(sb["grow"][0:1, 0:32], ps45[0:1, 0:32]),
           r=["PS45"], w=["grow"])
    p.emit("v", lambda e: e.tensor_add(
        sb["grow"][0:1, 32:40],
        sb["grow"].ap()[0:1, 0:16][0:1, 0::2],
        sb["grow"].ap()[0:1, 0:16][0:1, 1::2]), r=["grow"], w=["grow"])
    p.emit("v", lambda e: e.tensor_add(
        sb["grow"][0:1, 40:48],
        sb["grow"].ap()[0:1, 16:32][0:1, 0::2],
        sb["grow"].ap()[0:1, 16:32][0:1, 1::2]), r=["grow"], w=["grow"])
    NEL = float(S * HD)
    p.emit("v", lambda e: e.tensor_scalar_mul(sb["grow"][0:1, 32:40],
                                              sb["grow"][0:1, 32:40], 1.0 / NEL),
           r=["grow"], w=["grow"])
    p.emit("v", lambda e: e.tensor_scalar_mul(sb["grow"][0:1, 40:48],
                                              sb["grow"][0:1, 40:48], 1.0 / NEL),
           r=["grow"], w=["grow"])
    p.emit("v", lambda e: e.tensor_mul(sb["grow"][0:1, 48:56],
                                       sb["grow"][0:1, 32:40],
                                       sb["grow"][0:1, 32:40]),
           r=["grow"], w=["grow"])
    p.emit("v", lambda e: e.tensor_sub(sb["grow"][0:1, 40:48],
                                       sb["grow"][0:1, 40:48],
                                       sb["grow"][0:1, 48:56]),
           r=["grow"], w=["grow"])
    p.emit("v", lambda e: e.tensor_scalar_add(sb["grow"][0:1, 40:48],
                                              sb["grow"][0:1, 40:48], 1e-5),
           r=["grow"], w=["grow"])
    p.emit("s", lambda e: e.activation(sb["grow"][0:1, 40:48],
                                       sb["grow"][0:1, 40:48], AF.Ln),
           r=["grow"], w=["grow"])
    p.emit("s", lambda e: e.activation(sb["grow"][0:1, 40:48],
                                       sb["grow"][0:1, 40:48], AF.Exp,
                                       scale=-0.5), r=["grow"], w=["grow"])
    for src, dst in ((32, 0), (40, 16)):
        p.emit("v", (lambda src=src: lambda e: e.tensor_copy(
            sb["erow"].ap()[0:1, 0:16][0:1, 0::2],
            sb["grow"][0:1, src:src + 8]))(), r=["grow"], w=["erow"])
        p.emit("v", (lambda src=src: lambda e: e.tensor_copy(
            sb["erow"].ap()[0:1, 0:16][0:1, 1::2],
            sb["grow"][0:1, src:src + 8]))(), r=["grow"], w=["erow"])
        p.emit("t", (lambda dst=dst: lambda e: e.matmul(
            ps45[:, 32:48], cx.const_ap["ones_row_f32"][0:1, :],
            sb["erow"][0:1, 0:16], start=True, stop=True))(),
            r=["erow", "c_ones_row_f32"], w=["PS45"])
        p.emit("v", (lambda dst=dst: lambda e: e.tensor_copy(
            sb["MRS"][:, dst:dst + 16], ps45[:, 32:48]))(),
            r=["PS45"], w=["MRS"])
    p.emit("q", lambda e: e.dma_start(out=sb["MRS"][:, 32:48], in_=gw_in),
           r=["MRS"], w=["MRS"], dma=True)
    p.emit("q", lambda e: e.dma_start(out=sb["MRS"][:, 48:64], in_=gb_in),
           r=["MRS"], w=["MRS"], dma=True)
    p.emit("v", lambda e: e.tensor_mul(sb["MRS"][:, 16:32], sb["MRS"][:, 16:32],
                                       sb["MRS"][:, 32:48]), r=["MRS"], w=["MRS"])
    p.emit("v", lambda e: e.tensor_mul(sb["MRS"][:, 0:16], sb["MRS"][:, 0:16],
                                       sb["MRS"][:, 16:32]), r=["MRS"], w=["MRS"])
    p.emit("v", lambda e: e.tensor_sub(sb["MRS"][:, 0:16], sb["MRS"][:, 48:64],
                                       sb["MRS"][:, 0:16]), r=["MRS"], w=["MRS"])

    if stage == 2:
        p.emit("q", lambda e: e.dma_start(out=dmp_yst, in_=sb["yst"][:]),
               r=["yst"], w=["dmp"], dma=True, tag="dyst")
        p.emit("q", lambda e: e.dma_start(out=dmp_mrs, in_=sb["MRS"][:]),
               r=["MRS"], w=["dmp"], dma=True, tag="dmrs")
        p.emit("q", lambda e: e.dma_start(out=dmp_grow, in_=sb["grow"][:]),
               r=["grow"], w=["dmp"], dma=True, tag="dgrow")
    for g in range(4):
        t0 = g * 512
        for cc in range(4):
            for it in range(16):
                p.emit("q", (lambda g=g, cc=cc, it=it: lambda e: e.dma_start(
                    out=sb["yrl"][:, it * 512 + cc * 128:
                                  it * 512 + cc * 128 + 128],
                    in_=ydram[4 * g + cc][:, it * 128:it * 128 + 128]))(),
                    r=["yrl"], w=["yrl"], dma=True, tag="yrlld")
        for k in range(8):
            p.emit("q", (lambda k=k, t0=t0: lambda e: e.dma_start(
                out=sb[f"lnb{k}"][:, 0:512],
                in_=lnbdram[k * 128:(k + 1) * 128, t0:t0 + 512]))(),
                r=[f"lnb{k}"], w=[f"lnb{k}"], dma=True, tag="lnrld")
        for it in range(16):
            p.emit("s", (lambda it=it: lambda e: e.activation(
                sb["ycs"][:, 0:512], sb["yrl"][:, it * 512:it * 512 + 512],
                AF.Identity, bias=sb["MRS"][:, it:it + 1],
                scale=sb["MRS"][:, 16 + it:17 + it]))(),
                r=["yrl", "MRS"], w=["ycs"], tag="yn")
            wtn = f"wt{it % 2}"
            p.emit("g", (lambda it=it, wtn=wtn: lambda e: e.dma_start(
                out=sb[wtn][:, 0:1024], in_=wz_in[it]))(), r=[wtn], w=[wtn],
                dma=True, tag="lwz")
            for kt in range(8):
                p.emit("t", (lambda it=it, kt=kt, wtn=wtn: lambda e: e.matmul(
                    ps45[:, 0:512], sb[wtn][:, kt * 128:kt * 128 + 128],
                    sb[f"lnb{kt}"][:, 0:512],
                    start=(kt == 0), stop=(kt == 7)))(),
                    r=[wtn, f"lnb{kt}"], w=["PS45"], tag="zmm")
            p.emit("s", (lambda: lambda e: e.activation(
                sb["zt"][:], ps45[:, 0:512], AF.Silu))(),
                r=["PS45"], w=["zt"], tag="zsilu")
            p.emit("v", (lambda it=it: lambda e: e.tensor_mul(
                sb["urg"][:, it * 512:it * 512 + 512], sb["ycs"][:, 0:512],
                sb["zt"][:]))(), r=["ycs", "zt"], w=["urg"], tag="umul")
        if stage == 2 and g == 0:
            p.emit("q", (lambda: lambda e: e.dma_start(
                out=dmp_urg, in_=sb["urg"][:]))(),
                r=["urg"], w=["dmp"], dma=True, tag="durg")
        for m in range(8):
            wtn = f"wt{m % 2}"
            p.emit("g", (lambda m=m, wtn=wtn: lambda e: e.dma_start(
                out=sb[wtn][:], in_=wo_in[m]))(), r=[wtn], w=[wtn],
                dma=True, tag="lwo")
            for kt in range(16):
                p.emit("t", (lambda m=m, kt=kt, wtn=wtn: lambda e: e.matmul(
                    ps67[:, 0:512], sb[wtn][:, kt * 128:kt * 128 + 128],
                    sb["urg"][:, kt * 512:kt * 512 + 512],
                    start=(kt == 0), stop=(kt == 15)))(),
                    r=[wtn, "urg"], w=["PS67"], tag="womm")
            p.emit("v", (lambda m=m, t0=t0: lambda e: e.tensor_add(
                sb[f"res{m}"][:, t0:t0 + 512], sb[f"res{m}"][:, t0:t0 + 512],
                ps67[:, 0:512]))(), r=["PS67", f"res{m}"], w=[f"res{m}"],
                tag="woadd")
    if not merged:
        for i in range(8):
            p.emit("q", (lambda i=i: lambda e: e.dma_start(
                out=xout[i * 128:(i + 1) * 128, :], in_=sb[f"res{i}"][:]))(),
                r=[f"res{i}"], w=["xout_d"], dma=True, tag="xst")
        p.barrier(engs=("q",), on=("q", "g"))
        p.barrier(engs=("g",), on=("g",))
    return cx


def c2_host_inputs(inputs):
    T = lambda a: np.ascontiguousarray(np.asarray(a, np.float32).T)
    cwh = np.asarray(inputs["L1_attn_conv_w"], np.float32)[:, 0, :]
    cw = np.ascontiguousarray(cwh.reshape(16, 128, 4))
    cb = np.ascontiguousarray(
        np.asarray(inputs["L1_attn_conv_b"], np.float32).reshape(16, 128).T)
    gw = np.ascontiguousarray(
        np.asarray(inputs["L1_attn_gn_w"], np.float32).reshape(16, 128).T)
    gb = np.ascontiguousarray(
        np.asarray(inputs["L1_attn_gn_b"], np.float32).reshape(16, 128).T)
    def pk(w, nm):
        wT = T(w)
        kk, mm = wT.shape[0] // 128, wT.shape[1] // 128
        return np.ascontiguousarray(
            wT.reshape(kk, 128, mm, 128).transpose(2, 1, 0, 3)
            .reshape(mm, 128, kk * 128))
    wdtp = np.ascontiguousarray(
        T(inputs["L1_attn_Wdt"]).reshape(16, 128, 8).transpose(1, 0, 2)
        .reshape(128, 128))
    base = dict(
        wxt=pk(inputs["L1_attn_Wx"], "wx"), wzt=pk(inputs["L1_attn_Wz"], "wz"),
        cw=cw, cb=cb,
        wbt=pk(inputs["L1_attn_Wb"], "wb"), wct=pk(inputs["L1_attn_Wc"], "wc"),
        wdtt=wdtp,
        bdt=np.asarray(inputs["L1_attn_b_dt"], np.float32).reshape(1, 8),
        loga=np.asarray(inputs["L1_attn_log_a"], np.float32).reshape(1, 8),
        dparam=np.asarray(inputs["L1_attn_d_param"], np.float32).reshape(1, 8),
        gnw=gw, gnb=gb, woutt=pk(inputs["L1_attn_Wout"], "wo"))
    return base


def run_launch_c2(inputs, xmids):
    consts = host_constants()
    cx = build_launch_c2(consts)
    cx.p.seal(cx.es)
    cx.es.close()
    base = c2_host_inputs(inputs)
    in_maps = [dict(base, xin=xmids[c]) for c in range(NCORES)]
    return run_bass_kernel_spmd(cx.nc, in_maps, core_ids=list(range(NCORES)))



def build_launch_c3(consts, stage=99, cx=None, ext=None):
    merged = ext is not None
    if cx is None:
        cx = Ctx("C3")
    nc, p = cx.nc, cx.p
    if merged:
        xin = None
    else:
        xin = nc.dram_tensor("xin", [D, S], F32, kind="ExternalInput").ap()
    wq_in = nc.dram_tensor("wqp", [8, 128, 1024], F32, kind="ExternalInput").ap()
    wk_in = nc.dram_tensor("wkp", [8, 128, 1024], F32, kind="ExternalInput").ap()
    wv_in = nc.dram_tensor("wvp", [8, 128, 1024], F32, kind="ExternalInput").ap()
    wr_in = nc.dram_tensor("wrp", [128, 128], F32, kind="ExternalInput").ap()
    bq_in = nc.dram_tensor("bqc", [128, 8], F32, kind="ExternalInput").ap()
    bk_in = nc.dram_tensor("bkc", [128, 8], F32, kind="ExternalInput").ap()
    bv_in = nc.dram_tensor("bvc", [128, 8], F32, kind="ExternalInput").ap()
    br_in = nc.dram_tensor("brc", [16, 1], F32, kind="ExternalInput").ap()
    bo_in = nc.dram_tensor("boc", [128, 8], F32, kind="ExternalInput").ap()
    wo_in = nc.dram_tensor("wop", [2, 128, 1024], F32, kind="ExternalInput").ap()
    xout = nc.dram_tensor("xout", [D, S], F32, kind="ExternalOutput").ap()

    sb = cx.sb
    for i in range(8):
        cx.sbuf(f"res{i}", (128, S), F32)
    for i in range(8):
        cx.sbuf(f"lnb{i}", (128, S), BF16)
    lnpool = cx.sbuf("lnpool", (128, 6144), F32)
    pf = lnpool.ap()
    pb = pf.bitcast(BF16)
    cx.view("lnscr", pf[:, 0:2048])
    cx.view("ln_rb", pf[:, 2048:4096])
    cx.view("ln_nb", pf[:, 4096:6144])
    cx.view("vtm", pb[:, 0:4160])
    cx.view("otA", pb[:, 4160:6208])
    cx.view("otB", pb[:, 6208:8256])
    cx.view("usb", pb[:, 8256:8768])
    cx.view("otmp", pb[:, 8768:8832])
    cx.view("otm2", pb[0:64, 8832:8960])
    cx.sbuf("stM", (1, S), F32)
    cx.sbuf("stV", (1, S), F32)
    cx.sbuf("wball", (128, 8 * 1024), BF16)
    cx.sbuf("pis", (128, 8 * 256), BF16)
    cx.sbuf("WQE", (128, 8 * 256), BF16)
    cx.sbuf("WKE", (128, 8 * 256), BF16)
    cx.sbuf("WVE", (128, 8 * 256), BF16)
    cx.sbuf("qkt0", (128, S), BF16)
    cx.sbuf("qkt1", (128, S), BF16)
    cx.sbuf("kkt0", (128, S), BF16)
    cx.sbuf("kkt1", (128, S), BF16)
    cx.sbuf("vkt0", (128, S), BF16)
    cx.sbuf("vkt1", (128, S), BF16)
    cx.sbuf("wosb", (128, 2 * 1024), BF16)
    cx.sbuf("wrsb", (128, 128), BF16)
    cx.sbuf("bqcol", (128, 8), BF16)
    cx.sbuf("bkcol", (128, 8), BF16)
    cx.sbuf("bvcol", (128, 8), BF16)
    cx.sbuf("bocol", (128, 8), F32)
    cx.sbuf("beQ", (128, 2), F32)
    cx.sbuf("beK", (128, 2), F32)
    cx.sbuf("beV", (128, 2), F32)
    cx.sbuf("racc", (16, 1), F32)
    cx.sbuf("rtmp", (16, 1), F32)
    cx.sbuf("brcol", (16, 1), F32)
    cx.sbuf("rlcol", (16, 1), F32)
    cx.sbuf("rrow", (1, 16), F32)
    cx.sbuf("drow", (1, 16), F32)
    cx.sbuf("cur", (1, 16), F32)
    cx.sbuf("t16a", (1, 16), F32)
    cx.sbuf("t16b", (1, 16), F32)
    cx.sbuf("isbuf", (1, 64), F32)
    cx.sbuf("evs", (1, 8), F32)
    cx.sbuf("grow4", (1, 8), F32)
    cx.sbuf("mrow", (1, 2), F32)
    cx.sbuf("Erow", (1, 64), F32)
    cx.sbuf("ecols", (128, 8), F32)
    cx.sbuf("hg4", (128, 4), F32)
    cx.sbuf("rc", (128, 1), F32)
    ps01 = cx.psum("PS01", (128, 1024))
    ps23 = cx.psum("PS23", (128, 1024))
    ps45 = cx.psum("PS45", (128, 1024))
    ps67 = cx.psum("PS67", (128, 1024))
    cx.load_consts(consts, ["ones_col_f32", "ones_row_f32", "ident_bf",
                            "ident_f32", "lo_mask4_u8", "i64pat_f32"])
    res_names = [f"res{i}" for i in range(8)]
    lnb_names = [f"lnb{i}" for i in range(8)]

    if not merged:
        for i in range(8):
            p.emit("q", (lambda i=i: lambda e: e.dma_start(
                out=sb[f"res{i}"][:], in_=xin[i * 128:(i + 1) * 128, :]))(),
                w=[f"res{i}"], dma=True, tag="ldx")
    p.emit("g", lambda e: e.dma_start(out=sb["wrsb"][:], in_=wr_in),
           w=["wrsb"], dma=True)
    for nm, src_ap in (("bqcol", bq_in), ("bkcol", bk_in), ("bvcol", bv_in)):
        p.emit("g", (lambda nm=nm, src_ap=src_ap: lambda e: e.dma_start(
            out=sb[nm][:], in_=src_ap))(), w=[nm], dma=True)
    for nm, src_ap in (("bocol", bo_in), ("brcol", br_in)):
        p.emit("q", (lambda nm=nm, src_ap=src_ap: lambda e: e.dma_start(
            out=sb[nm][:], in_=src_ap))(), w=[nm], dma=True)
    for kt in range(2):
        p.emit("g", (lambda kt=kt: lambda e: e.dma_start(
            out=sb["wosb"][:, kt * 1024:(kt + 1) * 1024], in_=wo_in[kt]))(),
            r=["wosb"], w=["wosb"], dma=True, tag="lwo3")

    build_layernorm(cx, res_names, lnb_names, tag="ln3")

    p.emit("v", lambda e: e.memset(sb["racc"][:], 0.0), w=["racc"])
    for ch in range(4):
        for kt in range(8):
            p.emit("t", (lambda ch=ch, kt=kt: lambda e: e.matmul(
                ps45[0:16, 0:512], sb["wrsb"][:, kt * 16:kt * 16 + 16],
                sb[f"lnb{kt}"][:, ch * 512:ch * 512 + 512],
                start=(kt == 0), stop=(kt == 7)))(),
                r=["wrsb", f"lnb{kt}"], w=["PS45"], tag="rmm")
        p.emit("v", (lambda: lambda e: e.reduce_sum(
            sb["rtmp"][:], ps45[0:16, 0:512], axis=AX.X))(),
            r=["PS45"], w=["rtmp"])
        p.emit("v", (lambda: lambda e: e.tensor_add(
            sb["racc"][:], sb["racc"][:], sb["rtmp"][:]))(),
            r=["rtmp", "racc"], w=["racc"])
    p.emit("v", lambda e: e.tensor_scalar_mul(sb["rlcol"][:], sb["racc"][:],
                                              1.0 / S), r=["racc"], w=["rlcol"])
    p.emit("v", lambda e: e.tensor_add(sb["rlcol"][:], sb["rlcol"][:],
                                       sb["brcol"][:]),
           r=["rlcol", "brcol"], w=["rlcol"])
    p.emit("t", lambda e: e.matmul(ps45[0:1, 512:528], sb["rlcol"][:],
                                   cx.const_ap["ident_f32"][0:16, 0:16],
                                   start=True, stop=True),
           r=["rlcol", "c_ident_f32"], w=["PS45"], tag="rtr")
    p.emit("v", lambda e: e.tensor_copy(sb["rrow"][:], ps45[0:1, 512:528]),
           r=["PS45"], w=["rrow"])
    p.emit("v", lambda e: e.reduce_max(sb["mrow"][0:1, 0:1], sb["rrow"][:],
                                       axis=AX.X), r=["rrow"], w=["mrow"])
    p.emit("v", lambda e: e.tensor_scalar_mul(sb["mrow"][0:1, 1:2],
                                              sb["mrow"][0:1, 0:1], -1.0),
           r=["mrow"], w=["mrow"])
    p.emit("s", lambda e: e.activation(sb["drow"][:], sb["rrow"][:], AF.Exp,
                                       bias=sb["mrow"][0:1, 1:2]),
           r=["rrow", "mrow"], w=["drow"])
    p.emit("v", lambda e: e.reduce_sum(sb["mrow"][0:1, 0:1], sb["drow"][:],
                                       axis=AX.X), r=["drow"], w=["mrow"])
    p.emit("s", lambda e: e.activation(sb["mrow"][0:1, 0:1],
                                       sb["mrow"][0:1, 0:1], AF.Ln),
           r=["mrow"], w=["mrow"])
    p.emit("s", lambda e: e.activation(sb["mrow"][0:1, 0:1],
                                       sb["mrow"][0:1, 0:1], AF.Exp,
                                       scale=-1.0), r=["mrow"], w=["mrow"])
    p.emit("v", lambda e: e.tensor_scalar(sb["drow"][:], sb["drow"][:],
                                          sb["mrow"][0:1, 0:1], None, ALU.mult),
           r=["drow", "mrow"], w=["drow"])
    p.emit("v", lambda e: e.tensor_copy(sb["cur"][:], sb["drow"][:]),
           r=["drow"], w=["cur"])
    for k in range(4):
        p.emit("v", (lambda k=k: lambda e: e.reduce_max(
            sb["evs"][0:1, k:k + 1], sb["cur"][:], axis=AX.X))(),
            r=["cur"], w=["evs"])
        p.emit("v", (lambda k=k: lambda e: e.tensor_scalar(
            sb["t16a"][:], sb["cur"][:], sb["evs"][0:1, k:k + 1], None,
            ALU.is_ge))(), r=["cur", "evs"], w=["t16a"], tag="iseq")
        p.emit("v", lambda e: e.tensor_copy(sb["t16b"][:], sb["t16a"][:]),
               r=["t16a"], w=["t16b"])
        for i, sh in enumerate((1, 2, 4, 8)):
            src_nm = "t16b" if i % 2 == 0 else "cur2"
            if i == 0:
                src_nm, dst_nm = "t16b", "isbuf"
            pass
        p.emit("v", lambda e: e.tensor_copy(sb["isbuf"][0:1, 48:64],
                                            sb["t16b"][:]),
               r=["t16b"], w=["isbuf"])
        p.emit("v", lambda e: e.tensor_add(
            sb["isbuf"][0:1, 49:64], sb["isbuf"][0:1, 49:64],
            sb["t16b"][0:1, 0:15]), r=["t16b", "isbuf"], w=["isbuf"])
        p.emit("v", lambda e: e.tensor_copy(sb["t16b"][:],
                                            sb["isbuf"][0:1, 48:64]),
               r=["isbuf"], w=["t16b"])
        p.emit("v", lambda e: e.tensor_add(
            sb["t16b"][0:1, 2:16], sb["t16b"][0:1, 2:16],
            sb["isbuf"][0:1, 48:62]), r=["isbuf", "t16b"], w=["t16b"])
        p.emit("v", lambda e: e.tensor_copy(sb["isbuf"][0:1, 48:64],
                                            sb["t16b"][:]),
               r=["t16b"], w=["isbuf"])
        p.emit("v", lambda e: e.tensor_add(
            sb["isbuf"][0:1, 52:64], sb["isbuf"][0:1, 52:64],
            sb["t16b"][0:1, 0:12]), r=["t16b", "isbuf"], w=["isbuf"])
        p.emit("v", lambda e: e.tensor_copy(sb["t16b"][:],
                                            sb["isbuf"][0:1, 48:64]),
               r=["isbuf"], w=["t16b"])
        p.emit("v", lambda e: e.tensor_add(
            sb["t16b"][0:1, 8:16], sb["t16b"][0:1, 8:16],
            sb["isbuf"][0:1, 48:56]), r=["isbuf", "t16b"], w=["t16b"])
        p.emit("v", (lambda: lambda e: e.tensor_single_scalar(
            sb["t16b"][:], sb["t16b"][:], 1.0, ALU.is_equal))(),
            r=["t16b"], w=["t16b"])
        p.emit("v", (lambda k=k: lambda e: e.tensor_mul(
            sb["isbuf"][0:1, k * 16:k * 16 + 16], sb["t16a"][:],
            sb["t16b"][:]))(), r=["t16a", "t16b"], w=["isbuf"], tag="first")
        p.emit("v", (lambda k=k: lambda e: e.tensor_scalar(
            sb["t16a"][:], sb["isbuf"][0:1, k * 16:k * 16 + 16], 4.0, None,
            ALU.mult))(), r=["isbuf"], w=["t16a"])
        p.emit("v", (lambda: lambda e: e.tensor_sub(
            sb["cur"][:], sb["cur"][:], sb["t16a"][:]))(),
            r=["cur", "t16a"], w=["cur"])
    p.emit("s", lambda e: e.activation(sb["evs"][0:1, 4:8], sb["evs"][0:1, 0:4],
                                       AF.Exp), r=["evs"], w=["evs"])
    p.emit("v", lambda e: e.reduce_sum(sb["mrow"][0:1, 0:1],
                                       sb["evs"][0:1, 4:8], axis=AX.X),
           r=["evs"], w=["mrow"])
    p.emit("v", lambda e: e.tensor_scalar_add(sb["mrow"][0:1, 0:1],
                                              sb["mrow"][0:1, 0:1], 12.0),
           r=["mrow"], w=["mrow"])
    p.emit("s", lambda e: e.activation(sb["mrow"][0:1, 0:1],
                                       sb["mrow"][0:1, 0:1], AF.Ln),
           r=["mrow"], w=["mrow"])
    p.emit("s", lambda e: e.activation(sb["mrow"][0:1, 0:1],
                                       sb["mrow"][0:1, 0:1], AF.Exp,
                                       scale=-1.0), r=["mrow"], w=["mrow"])
    p.emit("v", lambda e: e.tensor_scalar(sb["grow4"][0:1, 0:4],
                                          sb["evs"][0:1, 4:8],
                                          sb["mrow"][0:1, 0:1], None, ALU.mult),
           r=["evs", "mrow"], w=["grow4"])
    p.emit("v", lambda e: e.memset(sb["Erow"][:], 0.0), w=["Erow"])
    for k in range(4):
        p.emit("v", (lambda k=k: lambda e: e.tensor_scalar(
            sb["Erow"].ap()[0:1, k::4], sb["isbuf"][0:1, k * 16:k * 16 + 16],
            sb["grow4"][0:1, k:k + 1], None, ALU.mult))(),
            r=["isbuf", "grow4"], w=["Erow"], tag="erow")

    for js in range(8):
        for half in range(2):
            p.emit("t", (lambda js=js, half=half: lambda e: e.matmul(
                ps45[:, 528 + half * 4:532 + half * 4],
                cx.const_ap["ones_row_f32"][0:1, :],
                sb["Erow"][0:1, (2 * js + half) * 4:(2 * js + half) * 4 + 4],
                start=True, stop=True))(),
                r=["Erow", "c_ones_row_f32"], w=["PS45"], tag="ebc")
        p.emit("v", (lambda: lambda e: e.tensor_copy(
            sb["ecols"][:], ps45[:, 528:536]))(), r=["PS45"], w=["ecols"])
        p.emit("v", (lambda: lambda e: e.select(
            sb["hg4"][:], cx.const_ap["lo_mask4_u8"][:], sb["ecols"][:, 0:4],
            sb["ecols"][:, 4:8]))(),
            r=["ecols", "c_lo_mask4_u8"], w=["hg4"])
        for k in range(4):
            p.emit("v", (lambda js=js, k=k: lambda e: e.tensor_scalar(
                sb["pis"][:, js * 256 + k * 64:js * 256 + k * 64 + 64],
                cx.const_ap["i64pat_f32"][:], sb["hg4"][:, k:k + 1], None,
                ALU.mult))(), r=["hg4", "c_i64pat_f32"], w=["pis"], tag="pik")

    for wnm, w_in, WE, bcol, bE in (
            ("q", wq_in, "WQE", "bqcol", "beQ"),
            ("k", wk_in, "WKE", "bkcol", "beK"),
            ("v", wv_in, "WVE", "bvcol", "beV")):
        for kt in range(8):
            p.emit("g", (lambda w_in=w_in, kt=kt: lambda e: e.dma_start(
                out=sb["wball"][:, kt * 1024:(kt + 1) * 1024],
                in_=w_in[kt]))(), r=["wball"], w=["wball"], dma=True,
                tag="lwe")
        for mt in range(8):
            for kt in range(8):
                p.emit("t", (lambda mt=mt, kt=kt: lambda e: e.matmul(
                    ps45[:, 0:256],
                    sb["wball"][:, kt * 1024 + mt * 128:
                                kt * 1024 + mt * 128 + 128],
                    sb["pis"][:, kt * 256:kt * 256 + 256],
                    start=(kt == 0), stop=(kt == 7)))(),
                    r=["wball", "pis"], w=["PS45"], tag="wemm")
            p.emit("s", (lambda WE=WE, mt=mt: lambda e: e.activation(
                sb[WE][:, mt * 256:mt * 256 + 256], ps45[:, 0:256],
                AF.Copy))(), r=["PS45"], w=[WE], tag="weev")
        for half in range(2):
            for kt in range(8):
                p.emit("t", (lambda half=half, kt=kt, bcol=bcol: lambda e:
                             e.matmul(
                    ps45[:, 256:257],
                    sb["pis"][:, kt * 256 + half * 128:
                              kt * 256 + half * 128 + 128],
                    sb[bcol][:, kt:kt + 1],
                    start=(kt == 0), stop=(kt == 7)))(),
                    r=["pis", bcol], w=["PS45"], tag="bemm")
            p.emit("v", (lambda bE=bE, half=half: lambda e: e.tensor_copy(
                sb[bE][:, half:half + 1], ps45[:, 256:257]))(),
                r=["PS45"], w=[bE])

    for WE, bE, o0, o1 in (("WQE", "beQ", "qkt0", "qkt1"),
                           ("WKE", "beK", "kkt0", "kkt1"),
                           ("WVE", "beV", "vkt0", "vkt1")):
        for half in range(2):
            onm = o0 if half == 0 else o1
            for ch in range(4):
                for mt in range(8):
                    p.emit("t", (lambda WE=WE, half=half, ch=ch, mt=mt:
                                 lambda e: e.matmul(
                        ps67[:, 0:512],
                        sb[WE][:, mt * 256 + half * 128:
                               mt * 256 + half * 128 + 128],
                        sb[f"lnb{mt}"][:, ch * 512:ch * 512 + 512],
                        start=(mt == 0), stop=(mt == 7)))(),
                        r=[WE, f"lnb{mt}"], w=["PS67"], tag="qkv")
                p.emit("s", (lambda onm=onm, bE=bE, half=half, ch=ch:
                             lambda e: e.activation(
                    sb[onm][:, ch * 512:ch * 512 + 512], ps67[:, 0:512],
                    AF.Identity, bias=sb[bE][:, half:half + 1]))(),
                    r=["PS67", bE], w=[onm], tag="qkvev")

    p.emit("v", lambda e: e.memset(sb["vtm"][:], 1.0), w=["vtm"])
    psb23 = ps23.ap().bitcast(BF16)
    for st in range(16):
        for sl in range(2):
            p.emit("t", (lambda st=st, sl=sl, psb23=psb23: lambda e:
                         e.transpose(
                psb23[:, 0:128], sb[f"vkt{sl}"][:, st * 128:st * 128 + 128],
                cx.const_ap["ident_bf"][:]))(),
                r=[f"vkt{sl}", "c_ident_bf"], w=["PS23"], tag="vtr")
            for q2 in range(2):
                k = 2 * sl + q2
                p.emit("v", (lambda st=st, k=k, q2=q2, psb23=psb23: lambda e:
                             e.tensor_copy(
                    sb["vtm"][:, st * 260 + k * 65:st * 260 + k * 65 + 64],
                    psb23[:, q2 * 64:q2 * 64 + 64]))(),
                    r=["PS23"], w=["vtm"], tag="vcp")

    PSO = [("PS45", 0), ("PS45", 512), ("PS67", 0), ("PS67", 512)]
    for k in range(4):
        pair = k // 2
        b0 = (k % 2) * 64
        qn, kn = f"qkt{pair}", f"kkt{pair}"
        on = "otA" if k < 2 else "otB"
        for qg in range(4):
            for st in range(16):
                p.emit("t", (lambda qn=qn, kn=kn, b0=b0, st=st, qg=qg:
                             lambda e: e.matmul(
                    ps01[:, 0:512],
                    sb[kn][b0:b0 + 64, st * 128:st * 128 + 128],
                    sb[qn][b0:b0 + 64, qg * 512:qg * 512 + 512],
                    start=True, stop=True))(),
                    r=[kn, qn], w=["PS01"], tag="sc")
                p.emit("s", (lambda: lambda e: e.activation(
                    sb["usb"][:], ps01[:, 0:512], AF.Exp, scale=0.125))(),
                    r=["PS01"], w=["usb"], tag="scexp")
                for qt in range(4):
                    pk, off = PSO[qt]
                    psx = ps45 if pk == "PS45" else ps67
                    p.emit("t", (lambda qt=qt, st=st, k=k, psx=psx, off=off:
                                 lambda e: e.matmul(
                        psx[:, off:off + 65],
                        sb["usb"][:, qt * 128:qt * 128 + 128],
                        sb["vtm"][:, st * 260 + k * 65:st * 260 + k * 65 + 65],
                        start=(st == 0), stop=(st == 15)))(),
                        r=["usb", "vtm"], w=[pk], tag="av")
            for qt in range(4):
                pk, off = PSO[qt]
                psx = ps45 if pk == "PS45" else ps67
                p.emit("s", (lambda psx=psx, off=off: lambda e: e.activation(
                    sb["rc"][:], psx[:, off + 64:off + 65], AF.Ln))(),
                    r=[pk], w=["rc"])
                p.emit("s", (lambda: lambda e: e.activation(
                    sb["rc"][:], sb["rc"][:], AF.Exp, scale=-1.0))(),
                    r=["rc"], w=["rc"])
                p.emit("v", (lambda psx=psx, off=off: lambda e: e.tensor_scalar(
                    sb["otmp"][:], psx[:, off:off + 64], sb["rc"][:, 0:1],
                    None, ALU.mult))(), r=[pk, "rc"], w=["otmp"], tag="osc")
                p.emit("t", (lambda psb23=psb23: lambda e: e.transpose(
                    psb23[0:64, 128:256], sb["otmp"][:],
                    cx.const_ap["ident_bf"][:]))(),
                    r=["otmp", "c_ident_bf"], w=["PS23"], tag="otr")
                p.emit("v", (lambda psb23=psb23: lambda e: e.tensor_copy(
                    sb["otm2"][:], psb23[0:64, 128:256]))(),
                    r=["PS23"], w=["otm2"])
                p.emit("g", (lambda on=on, b0=b0, qg=qg, qt=qt: lambda e:
                             e.dma_start(
                    out=sb[on][b0:b0 + 64,
                               (qg * 4 + qt) * 128:(qg * 4 + qt) * 128 + 128],
                    in_=sb["otm2"][:]))(), r=["otm2", on], w=[on], dma=True,
                    tag="omv")

    for mt in range(8):
        for ch in range(4):
            for kt in range(2):
                onm = "otA" if kt == 0 else "otB"
                p.emit("t", (lambda mt=mt, ch=ch, kt=kt, onm=onm: lambda e:
                             e.matmul(
                    ps01[:, 0:512],
                    sb["wosb"][:, kt * 1024 + mt * 128:
                               kt * 1024 + mt * 128 + 128],
                    sb[onm][:, ch * 512:ch * 512 + 512],
                    start=(kt == 0), stop=(kt == 1)))(),
                    r=["wosb", onm], w=["PS01"], tag="womm3")
            p.emit("v", (lambda mt=mt, ch=ch: lambda e: e.scalar_tensor_tensor(
                sb[f"res{mt}"][:, ch * 512:ch * 512 + 512], ps01[:, 0:512],
                sb["bocol"][:, mt:mt + 1], sb[f"res{mt}"][:, ch * 512:
                                                          ch * 512 + 512],
                ALU.add, ALU.add))(), r=["PS01", "bocol", f"res{mt}"],
                w=[f"res{mt}"], tag="woadd3")
    for i in range(8):
        p.emit("q", (lambda i=i: lambda e: e.dma_start(
            out=xout[i * 128:(i + 1) * 128, :], in_=sb[f"res{i}"][:]))(),
            r=[f"res{i}"], w=["xout_d"], dma=True, tag="xst3")
    p.barrier(engs=("q",), on=("q", "g"))
    p.barrier(engs=("g",), on=("g",))
    return cx


def c3_host_inputs(inputs):
    f32 = lambda k: np.asarray(inputs[k], np.float32)
    pk8 = lambda w: np.ascontiguousarray(
        np.asarray(w, np.float32).T.reshape(8, 128, 1024))
    col8 = lambda b: np.ascontiguousarray(
        np.asarray(b, np.float32).reshape(8, 128).T)
    wrp = np.ascontiguousarray(
        f32("L1_ffn_Wr").reshape(8, 128, 16).transpose(1, 0, 2)
        .reshape(128, 128))
    return dict(
        wqp=pk8(inputs["L1_ffn_Wq"]), wkp=pk8(inputs["L1_ffn_Wk"]),
        wvp=pk8(inputs["L1_ffn_Wv"]), wrp=wrp,
        bqc=col8(inputs["L1_ffn_bq"]), bkc=col8(inputs["L1_ffn_bk"]),
        bvc=col8(inputs["L1_ffn_bv"]), boc=col8(inputs["L1_ffn_bo"]),
        brc=np.ascontiguousarray(f32("L1_ffn_br").reshape(16, 1)),
        wop=np.ascontiguousarray(f32("L1_ffn_Wo").reshape(2, 128, 1024)))


def run_launch_c3(inputs, xl2s):
    consts = host_constants()
    cx = build_launch_c3(consts)
    cx.p.seal(cx.es)
    cx.es.close()
    base = c3_host_inputs(inputs)
    in_maps = [dict(base, xin=xl2s[c]) for c in range(NCORES)]
    return run_bass_kernel_spmd(cx.nc, in_maps, core_ids=list(range(NCORES)))



def build_launch_c123(consts):
    cx = Ctx("C123")
    nc, p = cx.nc, cx.p
    for nm in ("PS01", "PS23", "PS45", "PS67"):
        cx.psum(nm, (128, 1024))
    cx.load_consts(consts, ["ones_col_f32", "ones_row_f32", "ident_f32",
                            "ident_bf", "iota_col_f32", "iota16_f32",
                            "lt_incl_f32", "trimask_u8", "zero_f32_128",
                            "lo_mask4_u8", "i64pat_f32"])
    for i in range(8):
        cx.sbuf(f"res{i}", (128, S), F32)
    for i in range(8):
        cx.sbuf(f"lnb{i}", (128, S), BF16)
    shared = dict(cx.sb)
    ext1 = dict(
        xin=nc.dram_tensor("xin", [D, S], F32, kind="ExternalInput").ap(),
        lgown=nc.dram_tensor("lgown", [S, E], F32, kind="ExternalInput").ap(),
        yin=nc.dram_tensor("yin", [E * BLK, D], BF16,
                           kind="ExternalInput").ap(),
        swg=nc.dram_tensor("swg", [11, 128, 1024], F32,
                           kind="ExternalInput").ap(),
        swu=nc.dram_tensor("swu", [11, 128, 1024], F32,
                           kind="ExternalInput").ap(),
        swd=nc.dram_tensor("swd", [F, D], F32, kind="ExternalInput").ap())
    cx.push_scope("p1")
    cx.sb = dict(shared)
    build_launch_c(consts, cx=cx, ext=ext1)
    p.barrier()
    cx.pop_scope()
    cx.push_scope("p2")
    cx.sb = dict(shared)
    build_launch_c2(consts, cx=cx, ext={})
    p.barrier()
    cx.pop_scope()
    cx.push_scope("p3")
    cx.sb = dict(shared)
    build_launch_c3(consts, cx=cx, ext={})
    cx.pop_scope()
    return cx


def run_launch_c123(inputs, xouts, lgouts, yblks):
    consts = host_constants()
    cx = build_launch_c123(consts)
    cx.p.seal(cx.es)
    cx.es.close()
    pks = lambda w: np.ascontiguousarray(
        np.asarray(w, np.float32).reshape(8, 128, 11, 128)
        .transpose(2, 1, 0, 3).reshape(11, 128, 1024))
    swg = pks(inputs["L0_ffn_sw_gate"])
    swu = pks(inputs["L0_ffn_sw_up"])
    swd = np.ascontiguousarray(np.asarray(inputs["L0_ffn_sw_down"], np.float32))
    b2 = c2_host_inputs(inputs)
    b3 = c3_host_inputs(inputs)
    in_maps = []
    for c in range(NCORES):
        yin = np.concatenate(
            [yblks[e // 2][(e % 2) * NCORES * BLK + c * BLK:
                           (e % 2) * NCORES * BLK + (c + 1) * BLK]
             for e in range(E)], 0)
        in_maps.append(dict(b2, **b3, xin=xouts[c], lgown=lgouts[c],
                            yin=np.ascontiguousarray(yin),
                            swg=swg, swu=swu, swd=swd))
    return run_bass_kernel_spmd(cx.nc, in_maps, core_ids=list(range(NCORES)))



def _bfview(arr):
    return np.frombuffer(np.ascontiguousarray(arr).tobytes(),
                         dtype=ml_dtypes.bfloat16).reshape(arr.shape)


def kernel(**inputs):
    resA = run_launch_a(inputs)
    lgouts = [resA.results[c]["lgout"] for c in range(NCORES)]
    ablk = [_bfview(resA.results[c]["blkout"]) for c in range(NCORES)]
    xouts = [resA.results[c]["xout"] for c in range(NCORES)]
    lgfull = np.concatenate(lgouts, 0)
    resB = run_launch_b(inputs, lgfull, ablk)
    yblks = [_bfview(resB.results[c]["yblk"]) for c in range(NCORES)]
    resC = run_launch_c123(inputs, xouts, lgouts, yblks)
    return np.stack([np.ascontiguousarray(resC.results[c]["xout"].T)
                     for c in range(NCORES)], 0).astype(np.float32)
